# version 01
# baseline (speedup 1.0000x reference)
import dataclasses
import numpy as np
import concourse.bass as bass
import concourse.mybir as mybir
from concourse.bass_utils import run_bass_kernel_spmd
from contextlib import ExitStack

F32 = mybir.dt.float32
BF16 = mybir.dt.bfloat16
I32 = mybir.dt.int32
AF = mybir.ActivationFunctionType
ALU = mybir.AluOpType
ET = mybir.EngineType
AX = mybir.AxisListType

S, D, NH, DH, INTER = 8192, 2048, 16, 128, 8192
NC_ = 8
TOK = S // NC_
HALF = S // 2
DC = D // 128
IC = INTER // 128
EPS = 1e-5
RSQ_DH = float(1.0 / np.sqrt(128.0))
NBK = HALF // 512
NTH = HALF // 128
NTO = TOK // 128
NT_ALL = NTH + NTO

STAGE = 5

OFF_A = 0
OFF_B = 64 * 1024
OFF_C = 96 * 1024
OFF_E = 128 * 1024
OFF_D = 160 * 1024
OFF_F = 192 * 1024
SBUF_LIMIT = 207 * 1024

FL_WKV, FL_WQ, FL_KV, FL_WO, FL_WUP, FL_WGATE, FL_WDOWN = range(7)

VECS = [("bq", 16), ("bk", 16), ("bv", 16), ("bo", 16),
        ("su_ug", 16), ("wsc_dp", 16), ("su_dp", 64), ("wsc_ug", 128)]
NV = len(VECS)
CAST_ORDER = ["wk", "wv", "wq", "wo", "wup", "wgate", "wdown"]
CAST_IDX = {nm: i for i, nm in enumerate(CAST_ORDER)}

PE2_P2 = 25
MM_K, MM_V, MM_Q = 128, 256, 272
EV_K, EV_Q = 128, 144
def PE2_S(p, C):
    return PE2_P2 + 65 * p + C + 1
def PE2_DB(p):
    return PE2_P2 + 65 * p + 65
PE2_P3 = PE2_P2 + 8 * 65
def MM_PV(p, C):
    return MM_Q + 64 * p + C + 1
MM_P3 = MM_Q + 512
def S_EXP(p, C):
    return 64 * p + C + 1
def S_ACC(p, C):
    return 64 * p + C + 1
def S_DVE2_SW(p):
    return 2 + p + 1
MM_WO = MM_P3 + 8
PE2_P4 = PE2_P3 + 8
def MM5(hf):
    return MM_WO + hf * 336
def PE25(hf):
    return 554 + hf * 4
def EV5(hf):
    return 145 + hf * 216
def EV25(hf):
    return EV2_V + hf * 68
EV2_V = 128
ST_P4, LNR_P4, LN_P4 = 40, 41, 40
def ST5(hf):
    return 48 + hf * 8
def LNR5(hf):
    return 49 + hf * 8
def LN5(hf):
    return 48 + hf * 4
def DS5(hf):
    return 1 + hf * 4
def DVE2_H16(hf):
    return 19 + hf * 6
def DVE2_H64(hf):
    return 20 + hf * 6
def DVE2_L4(hf, tt):
    return 20 + hf * 6 + tt + 1


def butterfly_plan(n):
    slots = list(range(n))
    spare = n
    ops = []
    h = 1
    while h < n:
        for j in range(0, n, 2 * h):
            for i in range(h):
                a, b = j + i, j + i + h
                ops.append((slots[a], slots[b], spare))
                slots[a], spare = spare, slots[a]
        h *= 2
    return ops, slots


def build_program():
    nc = bass.Bass("TRN2", target_bir_lowering=False, debug=False,
                   detect_race_conditions=False)

    x_own = nc.dram_tensor("x_own", [TOK, D], F32, kind="ExternalInput").ap()
    x_half = nc.dram_tensor("x_half", [HALF, D], F32, kind="ExternalInput").ap()
    w_in, wbf = {}, {}
    for nm, shp in [("wq", [D, D]), ("wk", [D, D]), ("wv", [D, D]), ("wo", [D, D]),
                    ("wup", [INTER, D]), ("wgate", [INTER, D]), ("wdown", [D, INTER])]:
        w_in[nm] = nc.dram_tensor(nm, shp, F32, kind="ExternalInput").ap()
        wbf[nm] = nc.dram_tensor("bf_" + nm, shp, BF16, kind="Internal",
                                 addr_space="Shared").ap()
    vec_in = {}
    for nm, rows in VECS:
        vec_in[nm] = nc.dram_tensor(nm, [1, rows * 128], F32, kind="ExternalInput").ap()
    nonce = nc.dram_tensor("nonce", [1, 16], I32, kind="ExternalInput").ap()
    out = nc.dram_tensor("out", [TOK, D], F32, kind="ExternalOutput").ap()
    dbg = nc.dram_tensor("dbg", [TOK, D], F32, kind="ExternalOutput").ap() if STAGE < 5 else None
    dbgk = (nc.dram_tensor("dbgk", [D, TOK], F32, kind="ExternalOutput").ap()
            if STAGE in (2, 3) else None)

    nxh_d = nc.dram_tensor("nxh_d", [HALF, D], BF16, kind="Internal").ap()
    nxo_d = nc.dram_tensor("nxo_d", [TOK, D], BF16, kind="Internal").ap()
    x2_d = nc.dram_tensor("x2_d", [TOK, D], F32, kind="Internal").ap()
    kloc_d = nc.dram_tensor("kloc_d", [DC, 128, HALF], BF16, kind="Internal").ap()
    vloc_d = nc.dram_tensor("vloc_d", [HALF, D], BF16, kind="Internal").ap()
    ksh_d = nc.dram_tensor("ksh_d", [DC, 128, S], BF16, kind="Internal",
                           addr_space="Shared").ap()
    vsh_d = nc.dram_tensor("vsh_d", [S, D], BF16, kind="Internal",
                           addr_space="Shared").ap()
    kstat_d = nc.dram_tensor("kstat_d", [2, 1, 16], F32, kind="Internal",
                             addr_space="Shared").ap()
    flags_d = nc.dram_tensor("flags_d", [2, 7, 16], I32, kind="Internal",
                             addr_space="Shared").ap()

    with ExitStack() as ctx:
        e = ctx.enter_context
        base = (nc.sbuf_base + 31) & ~31
        usable = nc.sbuf_top - base
        assert usable >= SBUF_LIMIT, f"usable sbuf {usable} < {SBUF_LIMIT}"

        def sb(name, shape, dtype, off):
            end = off + int(np.prod(shape[1:])) * mybir.dt.size(dtype)
            assert end <= usable, (name, end, usable)
            return nc.alloc_sbuf_tensor_at(name, shape, dtype, offset=base + off).ap()

        _fo = [OFF_F]
        def sbf(name, shape, dtype):
            t = sb(name, shape, dtype, _fo[0])
            _fo[0] += ((int(np.prod(shape[1:])) * mybir.dt.size(dtype) + 31) // 32) * 32
            return t

        identb = sbf("identb", [128, 128], BF16)
        identf = sbf("identf", [128, 128], F32)
        h128 = sbf("h128", [128, 128], BF16)
        ones128 = sbf("ones128", [128, 128], BF16)
        onesrowf = sbf("onesrowf", [1, 128], F32)
        ones128f = sbf("ones128f", [128, 128], F32)
        iota_i = sbf("iota_i", [128, 128], I32)
        iota_j = sbf("iota_j", [128, 128], I32)
        hwork = sbf("hwork", [128, 128], I32)
        hworkf = sbf("hworkf", [128, 128], F32)
        bqc = sbf("bqc", [128, 16], F32)
        bkc = sbf("bkc", [128, 16], F32)
        boc = sbf("boc", [128, 16], F32)
        su1c = sbf("su1c", [128, 16], F32)
        sdpc = sbf("sdpc", [128, 16], F32)
        sdp_t = sbf("sdp_t", [128, 64], F32)
        wsc_t = sbf("wsc_t", [128, 128], F32)
        supc = sbf("supc", [128, 64], F32)
        sgtc = sbf("sgtc", [128, 64], F32)
        bvbc = sbf("bvbc", [128, 2048], BF16)
        vrow = sbf("vrow", [128, 128], F32)
        kmaxsq = sbf("kmaxsq", [1, 16], F32)
        qmaxsq = sbf("qmaxsq", [1, 16], F32)
        cwork = sbf("cwork", [1, 16], F32)
        cpair = sbf("cpair", [1, 8], F32)
        negCb = sbf("negCb", [128, 8], F32)
        nonce_sb = sbf("nonce_sb", [1, 16], I32)
        flag_sb = sbf("flag_sb", [1, 16], I32)
        kst_sb = sbf("kst_sb", [1, 16], F32)
        bnst = sbf("bnst", [128, 24], F32)
        ln_mv = sbf("ln_mv", [128, 2], F32)
        ln_r = sbf("ln_r", [128, 1], F32)
        ln_mr = sbf("ln_mr", [128, 1], F32)
        cst = sbf("cst", [128, 2], F32)
        E_ksq = [sbf(f"f_ksq_{b}", [128, 512], BF16) for b in range(2)]
        E_qsq = sbf("f_qsq", [128, 1024], BF16)
        assert _fo[0] <= SBUF_LIMIT, _fo[0]

        A_w1 = [sb(f"a_w1_{k}", [128, 2048], BF16, OFF_A + k * 4096) for k in range(16)]
        A_kt = [sb(f"a_kt_{j}", [128, 8192], BF16, OFF_A + j * 16384) for j in range(4)]
        B_nt = [[sb(f"b_nt{b}_{k}", [128, 512], BF16, OFF_B + b * 16384 + k * 1024)
                 for k in range(16)] for b in range(2)]
        B_ntq = [sb(f"b_ntq_{k}", [128, 1024], BF16, OFF_B + k * 2048) for k in range(16)]
        B_res = [sb(f"b_res_{t}", [128, 2048], F32, OFF_B + t * 8192) for t in range(4)]
        P4_li = [sb(f"p4_li_{b}", [128, 2048], BF16, OFF_B + b * 4096) for b in range(2)]
        C_xt = [sb(f"c_xt_{k}", [128, 512], BF16, OFF_C + k * 1024) for k in range(17)]
        C_sg = [sb(f"c_sg_{b}", [128, 512], BF16, OFF_C + 17408 + b * 1024) for b in range(2)]
        C_n3f = sb("c_n3f", [128, 2048], F32, OFF_C + 19456)
        sdpc_bb = sb("sdpc_bb", [128, 2048], BF16, OFF_C + 27648)
        bnst5 = sb("bnst5", [128, 96], F32, OFF_C + 31744)
        bo_row = sb("bo_row", [1, 2048], F32, OFF_E)
        sdpc_row = sb("sdpc_row", [1, 2048], F32, OFF_E + 8192)
        A_wd = [sb(f"a_wd_{b}", [128, 1024], BF16, OFF_A + 32768 + b * 2048) for b in range(2)]
        Z = []
        for zc in range(64):
            zoff = (OFF_E + zc * 1024) if zc < 32 else (OFF_D + (zc - 32) * 1024)
            Z.append(sb(f"z_{zc}", [128, 512], BF16, zoff))
        P5_out = [sb(f"p5_out_{b}", [128, 2048], F32, OFF_E + b * 8192) for b in range(2)]
        A_wb = [[sb(f"a_wb{b}_{k}", [128, 1024], BF16, OFF_A + b * 32768 + k * 2048)
                 for k in range(16)] for b in range(2)]
        C_ko = [sb(f"c_ko_{k}", [128, 512], BF16, OFF_C + k * 1024) for k in range(16)]
        C_vo = [sb(f"c_vo_{t}", [128, 2048], BF16, OFF_C + 16384 + t * 4096) for t in range(4)]
        C_ao = [sb(f"c_ao_{k}", [128, 1024], BF16, OFF_C + k * 2048) for k in range(16)]
        E_xh = [sb(f"e_xh_{b}", [128, 2048], F32, OFF_E + b * 8192) for b in range(2)]
        brow = sb("brow", [1, 2048], F32, OFF_E)
        E_nx = [sb(f"e_nx_{b}", [128, 2048], BF16, OFF_E + 16384 + b * 4096) for b in range(2)]
        E_dbg = sb("e_dbg", [128, 2048], F32, OFF_E + 24576)
        E_vp = [sb(f"e_vp_{b}", [128, 2048], BF16, OFF_E + b * 4096) for b in range(2)]
        E_pp = [sb(f"e_pp_{b}", [128, 2048], BF16, OFF_E + 8192 + b * 4096) for b in range(2)]
        E_acc = sb("e_acc", [128, 2048], F32, OFF_E + 16384)
        E_dinv = sb("e_dinv", [128, 2048], F32, OFF_E + 24576)
        E_li = [sb(f"e_li_{b}", [128, 2048], BF16, OFF_E + 24576 + b * 4096) for b in range(2)]
        D_qt = [sb(f"d_qt_{k}", [128, 1024], BF16, OFF_D + k * 2048) for k in range(16)]
        D_xo = [sb(f"d_xo_{b}", [128, 2048], F32, OFF_D + b * 8192) for b in range(2)]
        D_x2 = [sb(f"d_x2_{b}", [128, 2048], F32, OFF_D + 16384 + b * 8192) for b in range(2)]

        psf_h = e(nc.psum_tensor("psf", [128, 4096], F32))
        psf = psf_h.ap()
        psb_h = nc.virtual_psum_tensor("psb", [128, 8192], BF16)
        nc.back_virtual_tensor_with(psb_h, psf_h)
        psb = psb_h.ap()

        def bkf(k, n=1):
            return psf[:, k * 512:(k + n) * 512]

        def bkb(k, n=1):
            return psb[:, k * 1024:(k + n) * 1024]

        def sem(name):
            return e(nc.semaphore(name))
        s_gen = sem("s_gen"); s_vecp = sem("s_vecp"); s_vecd = sem("s_vecd")
        s_flag = sem("s_flag"); s_poll = sem("s_poll"); s_st = sem("s_st")
        s_lnr = sem("s_lnr"); s_ln = sem("s_ln"); s_mm = sem("s_mm")
        s_ev = sem("s_ev"); s_ev2 = sem("s_ev2"); s_sq = sem("s_sq")
        s_kst = sem("s_kst"); s_pe2 = sem("s_pe2"); s_exp = sem("s_exp")
        s_acc = sem("s_acc"); s_dve2 = sem("s_dve2"); s_ds = sem("s_ds")
        s_dbg = sem("s_dbg")
        s_chp = sem("s_chp"); s_chv = sem("s_chv"); s_cha = sem("s_cha")
        s_acc2 = sem("s_acc2"); s_ds2 = sem("s_ds2")

        def chain(eng, semh):
            last = [None]
            n = [0]
            def mark(inst):
                n[0] += 1
                inst.then_inc(semh, 1)
                last[0] = (semh, n[0])
                return inst
            def markpub(inst, pub, val):
                last[0] = (pub, val)
                return inst
            def wait():
                if last[0] is not None:
                    eng.wait_ge(last[0][0], last[0][1])
            return mark, markpub, wait

        class DS:
            def __init__(self, name, nsem):
                self.sems = [e(nc.semaphore(f"{name}{i}")) for i in range(nsem)]
                self.n = nsem
                self.j = 0
            def throttle(self, eng):
                j = self.j
                if j >= self.n:
                    eng.wait_ge(self.sems[j % self.n], 16 * (j // self.n))
            def issue(self, eng, inst):
                i = self.j % self.n
                inst.then_inc(self.sems[i], 16)
                self.j += 1
                return inst
            def wait(self, eng, j):
                eng.wait_ge(self.sems[j % self.n], 16 * (j // self.n + 1))
            def wait_all(self, eng, count):
                for i in range(min(self.n, count)):
                    cnt_i = (count - 1 - i) // self.n + 1
                    eng.wait_ge(self.sems[i], 16 * cnt_i)
            def wait_range(self, eng, j0, n):
                seen = {}
                for j in range(j0, j0 + n):
                    seen[j % self.n] = j // self.n + 1
                for i, cnt in seen.items():
                    eng.wait_ge(self.sems[i], 16 * cnt)

        ds_vl = DS("dvl", 2)
        ds_xh = DS("dxh", 2)
        ds_nxw = DS("dnw", 2)
        ds_cw = DS("dcw", 7)
        ds_bf = DS("dbf", 1)
        ds_wt = DS("dwt", 4)
        ds_nt = DS("dnt", 8)
        ds_ko = DS("dko", 4)
        ds_vv = DS("dvv", 4)
        ds_kss = DS("dks", 1)
        ds_ksf = DS("dkf", 1)
        ds_kt = DS("dkt", 2)
        ds_cp = DS("dcp", 2)
        ds_vp = DS("dvp", 2)
        ds_dbo = DS("ddb", 1)
        ds_dbi = DS("ddi", 2)
        ds_out = DS("dou", 2)
        ds_x2 = DS("dx2", 2)
        ds_wb = DS("dwb", 4)

        block = e(nc.Block())

        def shsl(t, sideval, sub):
            per = int(np.prod(t.shape[1:]))
            return dataclasses.replace(sub, offset=sideval * per + sub.offset)

        def dynoff(ap, extra):
            return dataclasses.replace(ap, offset=ap.offset + extra)

        @block.sync
        def _(eng):
            my = nc.partition_id([ET.SP])
            side = my % 2
            koff = side * HALF
            voff = side * (HALF * D)

            ds_vl.issue(eng, eng.dma_start(out=nonce_sb[:], in_=nonce[:]))
            for i, (nm, rows) in enumerate(VECS):
                if i > 0:
                    eng.wait_ge(s_vecp, i)
                ds_vl.throttle(eng)
                ds_vl.issue(eng, eng.dma_start(
                    out=vrow[0:rows, :],
                    in_=vec_in[nm].rearrange("o (r c) -> (o r) c", c=128)))

            def _nx_write(j):
                eng.wait_ge(s_ln, j + 1)
                dst = (nxh_d[j * 128:(j + 1) * 128, :] if j < NTH
                       else nxo_d[(j - NTH) * 128:(j - NTH + 1) * 128, :])
                ds_nxw.throttle(eng)
                ds_nxw.issue(eng, eng.dma_start(out=dst, in_=E_nx[j % 2]))

            def _dbg_write(j):
                if STAGE != 1 or j < NTH:
                    return
                jo = j - NTH
                eng.wait_ge(s_dbg, jo + 1)
                ds_dbo.throttle(eng)
                ds_dbo.issue(eng, eng.dma_start(
                    out=dbg[jo * 128:(jo + 1) * 128, :], in_=E_dbg[:]))

            for i in range(NT_ALL):
                if i >= 2:
                    eng.wait_ge(s_ln, i - 1)
                srct = (x_half[i * 128:(i + 1) * 128, :] if i < NTH
                        else x_own[(i - NTH) * 128:(i - NTH + 1) * 128, :])
                ds_xh.throttle(eng)
                ds_xh.issue(eng, eng.dma_start(out=E_xh[i % 2], in_=srct))
                if i >= 2:
                    _nx_write(i - 2)
                    _dbg_write(i - 2)
            _nx_write(NT_ALL - 2)
            _dbg_write(NT_ALL - 2)
            _nx_write(NT_ALL - 1)
            _dbg_write(NT_ALL - 1)
            eng.wait_ge(s_ln, NT_ALL)
            ds_vl.throttle(eng)
            ds_vl.issue(eng, eng.dma_start(out=brow[:], in_=vec_in["bv"][:, :]))

            if STAGE == 1:
                ds_dbo.wait_all(eng, NTO)
                return

            eng.wait_ge(s_flag, 1)
            ds_cw.wait(eng, CAST_IDX["wk"])
            for k in range(16):
                ds_wt.throttle(eng)
                ds_wt.issue(eng, eng.dma_start(
                    out=A_w1[k], in_=wbf["wk"][:, k * 128:(k + 1) * 128], transpose=True))
            for blk in range(NBK):
                bb = blk % 2
                if blk >= 2:
                    eng.wait_ge(s_mm, (blk - 2) * 16 + 16)
                ds_nxw.wait_all(eng, min(4 * (blk + 1), NTH))
                for k in range(16):
                    ds_nt.throttle(eng)
                    ds_nt.issue(eng, eng.dma_start(
                        out=B_nt[bb][k],
                        in_=nxh_d[blk * 512:(blk + 1) * 512, k * 128:(k + 1) * 128],
                        transpose=True))
                for dout in range(16):
                    g = blk * 16 + dout
                    eng.wait_ge(s_sq, g + 1)
                    ds_ko.throttle(eng)
                    ds_ko.issue(eng, eng.dma_start(
                        out=kloc_d[dout][:, blk * 512:(blk + 1) * 512],
                        in_=C_ko[dout]))
            eng.wait_ge(s_mm, MM_K)
            ds_cw.wait(eng, CAST_IDX["wv"])
            for k in range(16):
                ds_wt.throttle(eng)
                ds_wt.issue(eng, eng.dma_start(
                    out=A_w1[k], in_=wbf["wv"][:, k * 128:(k + 1) * 128], transpose=True))
            for blk in range(NBK):
                bb = blk % 2
                if blk >= 2:
                    eng.wait_ge(s_mm, MM_K + (blk - 2) * 16 + 16)
                for k in range(16):
                    ds_nt.throttle(eng)
                    ds_nt.issue(eng, eng.dma_start(
                        out=B_nt[bb][k],
                        in_=nxh_d[blk * 512:(blk + 1) * 512, k * 128:(k + 1) * 128],
                        transpose=True))
                for ts in range(4):
                    eng.wait_ge(s_ev2, blk * 16 + ts * 4 + 4)
                    ds_vv.throttle(eng)
                    ds_vv.issue(eng, eng.dma_start(
                        out=vsh_d[blk * 512 + ts * 128:blk * 512 + (ts + 1) * 128, :]
                        if False else
                        vloc_d[blk * 512 + ts * 128:blk * 512 + (ts + 1) * 128, :],
                        in_=C_vo[ts]))
            eng.wait_ge(s_kst, 8)
            ds_kss.issue(eng, eng.dma_start(
                out=shsl(kstat_d, side, kstat_d[0][:, :]), in_=kmaxsq[:]))
            ds_ko.wait_all(eng, NBK * 16)
            ds_vv.wait_all(eng, NBK * 4)
            for p in range(4):
                ds_cp.throttle(eng)
                ds_cp.issue(eng, eng.dma_start(
                    out=dynoff(ksh_d[4 * p:4 * p + 4][:, :, 0:HALF], koff),
                    in_=kloc_d[4 * p:4 * p + 4]))
            for p in range(2):
                ds_cp.throttle(eng)
                ds_cp.issue(eng, eng.dma_start(
                    out=dynoff(vsh_d[p * 2048:(p + 1) * 2048, :], voff),
                    in_=vloc_d[p * 2048:(p + 1) * 2048, :]))
            eng.wait_ge(s_flag, 2)
            ds_cw.wait(eng, CAST_IDX["wq"])
            eng.wait_ge(s_mm, MM_V)
            for k in range(16):
                ds_wt.throttle(eng)
                ds_wt.issue(eng, eng.dma_start(
                    out=A_w1[k], in_=wbf["wq"][:, k * 128:(k + 1) * 128], transpose=True))
            ds_nxw.wait_all(eng, NT_ALL)
            for k in range(16):
                ds_nt.throttle(eng)
                ds_nt.issue(eng, eng.dma_start(
                    out=B_ntq[k], in_=nxo_d[:, k * 128:(k + 1) * 128], transpose=True))

            if STAGE == 2:
                for k in range(16):
                    eng.wait_ge(s_dbg, k + 1)
                    ds_dbo.throttle(eng)
                    ds_dbo.issue(eng, eng.dma_start(
                        out=dbgk[k * 128:(k + 1) * 128, :], in_=E_dbg[:, 0:1024]))
                eng.wait_ge(s_flag, 3)
                ds_dbi.issue(eng, eng.dma_start(out=E_pp[0], in_=ksh_d[0][:, 0:2048]))
                ds_dbi.issue(eng, eng.dma_start(out=E_pp[1], in_=vsh_d[0:128, :]))
                for j, r0 in enumerate([0, 128, 256]):
                    eng.wait_ge(s_dbg, 17 + j)
                    ds_dbo.throttle(eng)
                    ds_dbo.issue(eng, eng.dma_start(out=dbg[r0:r0 + 128, :], in_=E_dbg[:]))
                ds_dbo.wait_all(eng, 19)
                return

            eng.wait_ge(s_flag, 3)
            for p in range(8):
                if p < 2:
                    eng.wait_ge(s_mm, MM_Q)
                else:
                    eng.wait_ge(s_pe2, PE2_DB(p - 1))
                for e2 in range(2):
                    ds_kt.throttle(eng)
                    ds_kt.issue(eng, eng.dma_start(
                        out=A_kt[(p % 2) * 2 + e2], in_=ksh_d[2 * p + e2]))
                for g in range(8):
                    vg = 8 * p + g
                    if vg >= 2:
                        q2, g2 = divmod(vg - 2, 8)
                        eng.wait_ge(s_mm, MM_PV(q2, 8 * g2 + 7))
                    ds_vp.throttle(eng)
                    ds_vp.issue(eng, eng.dma_start(
                        out=E_vp[vg % 2].rearrange("p (c d) -> p c d", c=8),
                        in_=vsh_d[g * 1024:(g + 1) * 1024,
                                  2 * p * 128:(2 * p + 2) * 128].rearrange(
                                      "(c p) d -> p c d", p=128)))

            if STAGE == 3:
                for k in range(16):
                    eng.wait_ge(s_dbg, k + 1)
                    ds_dbo.throttle(eng)
                    ds_dbo.issue(eng, eng.dma_start(
                        out=dbgk[k * 128:(k + 1) * 128, :], in_=B_res[0][:, 0:1024]))
                ds_dbo.wait_all(eng, 16)
                return

            eng.wait_ge(s_flag, 4)
            ds_cw.wait(eng, CAST_IDX["wo"])
            eng.wait_ge(s_pe2, PE2_P3)
            ds_dbi.throttle(eng)
            ds_dbi.issue(eng, eng.dma_start(out=bo_row[:], in_=vec_in["bo"]))
            ds_dbi.throttle(eng)
            ds_dbi.issue(eng, eng.dma_start(out=sdpc_row[:], in_=vec_in["wsc_dp"]))
            for k in range(16):
                ds_wt.throttle(eng)
                ds_wt.issue(eng, eng.dma_start(
                    out=A_w1[k], in_=wbf["wo"][:, k * 128:(k + 1) * 128], transpose=True))

            def _x2w(j):
                eng.wait_ge(s_ln, LN_P4 + j + 1)
                ds_nxw.throttle(eng)
                ds_nxw.issue(eng, eng.dma_start(
                    out=x2_d[j * 128:(j + 1) * 128, :], in_=D_x2[j % 2]))

            for i in range(8):
                if i >= 2:
                    eng.wait_ge(s_ln, LN_P4 + i - 1)
                ds_x2.throttle(eng)
                ds_x2.issue(eng, eng.dma_start(
                    out=D_xo[i % 2], in_=x_own[i * 128:(i + 1) * 128, :]))
                if i >= 2:
                    _x2w(i - 2)
            _x2w(6)
            _x2w(7)

            if STAGE == 4:
                ds_nxw.wait_all(eng, NT_ALL + 8)
                ds_dbo.issue(eng, eng.dma_start(out=dbg[:, :], in_=x2_d[:, :]))
                ds_dbo.wait_all(eng, 1)
                return

            for hf in range(2):
                ds_nxw.wait_all(eng, NT_ALL + 8)
                if hf == 1:
                    eng.wait_ge(s_ln, LN5(0) + 4)
                for tt in range(4):
                    ds_x2.throttle(eng)
                    ds_x2.issue(eng, eng.dma_start(
                        out=B_res[tt],
                        in_=x2_d[hf * 512 + tt * 128:hf * 512 + (tt + 1) * 128, :]))
                for wi, wnm in enumerate(["wup", "wgate"]):
                    eng.wait_ge(s_flag, 5 + wi)
                    ds_cw.wait(eng, CAST_IDX[wnm])
                    for B in range(8):
                        jbase = hf * 384 + wi * 128 + B * 16
                        if B >= 2:
                            eng.wait_ge(s_mm, MM5(hf) + 16 + wi * 64 + (B - 2) * 8 + 8)
                        elif hf == 1 or wi == 1:
                            prev = MM5(hf) + 16 + wi * 64 if wi == 1 else MM5(hf - 1) + 336
                            eng.wait_ge(s_mm, prev)
                        else:
                            eng.wait_ge(s_mm, MM_WO)
                        for k in range(16):
                            ds_wb.throttle(eng)
                            ds_wb.issue(eng, eng.dma_start(
                                out=A_wb[B % 2][k],
                                in_=wbf[wnm][B * 1024:(B + 1) * 1024,
                                             k * 128:(k + 1) * 128], transpose=True))
                eng.wait_ge(s_flag, 7)
                ds_cw.wait(eng, CAST_IDX["wdown"])
                for dh2 in range(2):
                    for k2 in range(64):
                        kk = dh2 * 64 + k2
                        if kk >= 2:
                            eng.wait_ge(s_mm, MM5(hf) + 208 + kk - 1)
                        else:
                            eng.wait_ge(s_mm, MM5(hf) + 144)
                        ds_wb.throttle(eng)
                        ds_wb.issue(eng, eng.dma_start(
                            out=A_wd[k2 % 2],
                            in_=wbf["wdown"][dh2 * 1024:(dh2 + 1) * 1024,
                                             k2 * 128:(k2 + 1) * 128], transpose=True))
                for tt in range(4):
                    eng.wait_ge(s_ln, LN5(hf) + tt + 1)
                    ds_out.throttle(eng)
                    ds_out.issue(eng, eng.dma_start(
                        out=out[hf * 512 + tt * 128:hf * 512 + (tt + 1) * 128, :],
                        in_=P5_out[tt % 2]))
            ds_out.wait_all(eng, 8)

        @block.gpsimd
        def _(eng):
            my = nc.partition_id([ET.Pool])
            side = my % 2
            bud = 1 - side
            pmark, pmarkpub, pwait = chain(eng, s_chp)

            for op in [lambda: eng.memset(identf[:], 0.0),
                       lambda: eng.affine_select(out=identf[:], in_=identf[:],
                                                 compare_op=ALU.not_equal, fill=1.0,
                                                 base=0, pattern=[[-1, 128]],
                                                 channel_multiplier=1),
                       lambda: eng.iota(iota_i[:], pattern=[[0, 128]], base=0,
                                        channel_multiplier=1),
                       lambda: eng.iota(iota_j[:], pattern=[[1, 128]], base=0,
                                        channel_multiplier=0),
                       lambda: eng.memset(ones128[:], 1.0),
                       lambda: eng.memset(ones128f[:], 1.0),
                       lambda: eng.memset(onesrowf[:], 1.0),
                       lambda: eng.memset(cst[:, 0:1], EPS),
                       lambda: eng.memset(kmaxsq[:], 0.0)]:
                pmark(op())
                pwait()
            eng.memset(cst[:, 1:2], 0.0).then_inc(s_gen, 1)

            if STAGE < 2:
                return

            ds_vl.wait(eng, 0)

            def dynoff_p(ap, extra):
                return dataclasses.replace(ap, offset=ap.offset + extra)

            for nm in CAST_ORDER:
                rows = w_in[nm].shape[0]
                hr = rows // 2
                off = side * (hr * D if nm != "wdown" else hr * INTER)
                ds_cw.issue(eng, eng.dma_start(
                    out=dynoff_p(wbf[nm][0:hr, :], off),
                    in_=dynoff_p(w_in[nm][0:hr, :], off)))
            FLAG_OF = {"wv": FL_WKV, "wq": FL_WQ, "wdown": FL_WO}
            for nm in CAST_ORDER:
                if nm not in FLAG_OF:
                    continue
                ds_cw.wait(eng, CAST_IDX[nm])
                if nm == "wv":
                    ds_cw.wait(eng, CAST_IDX["wk"])
                if nm == "wdown":
                    for other in ("wo", "wup", "wgate"):
                        ds_cw.wait(eng, CAST_IDX[other])
                ds_bf.throttle(eng)
                ds_bf.issue(eng, eng.dma_start(
                    out=shsl(flags_d, side, flags_d[0][FLAG_OF[nm]:FLAG_OF[nm] + 1, :]),
                    in_=nonce_sb[:]))

            with eng.register("want") as want, eng.register("go") as go, \
                 eng.register("cur") as cur, eng.register("itr") as itr:
                eng.reg_load(want, nonce_sb[0:1, 0:1])
                eng.reg_mov(itr, 0)

                def poll(fl):
                    eng.reg_mov(go, 1)
                    with eng.While(go):
                        eng.reg_alu(itr, itr, 16, ALU.add)
                        eng.dma_start(out=flag_sb[:],
                                      in_=shsl(flags_d, bud, flags_d[0][fl:fl + 1, :])
                                      ).then_inc(s_poll, 16)
                        eng.wait_ge(s_poll, itr)
                        eng.reg_load(cur, flag_sb[0:1, 0:1])
                        eng.reg_alu(go, cur, want, ALU.not_equal)
                    eng.reg_alu(go, go, 0, ALU.add).then_inc(s_flag, 1)

                poll(FL_WKV)
                poll(FL_WQ)
                ds_cp.wait_all(eng, 6)
                ds_kss.wait(eng, 0)
                ds_bf.throttle(eng)
                ds_bf.issue(eng, eng.dma_start(
                    out=shsl(flags_d, side, flags_d[0][FL_KV:FL_KV + 1, :]),
                    in_=nonce_sb[:]))
                poll(FL_KV)
                ds_ksf.issue(eng, eng.dma_start(
                    out=kst_sb[:], in_=shsl(kstat_d, bud, kstat_d[0][:, :])))
                if STAGE >= 3:
                    for p3 in range(8):
                        for C3 in range(64):
                            eng.wait_ge(s_exp, S_EXP(p3, C3))
                            if C3 >= 1:
                                eng.wait_ge(s_acc2, S_ACC(p3, C3 - 1))
                            pwait()
                            if C3 == 0:
                                pmarkpub(eng.tensor_copy(
                                    E_acc[:, 1024:2048], E_pp[0][:, 1024:2048]
                                    ).then_inc(s_acc2, 1), s_acc2, S_ACC(p3, C3))
                            else:
                                pmarkpub(eng.tensor_tensor(
                                    E_acc[:, 1024:2048], E_acc[:, 1024:2048],
                                    E_pp[C3 % 2][:, 1024:2048],
                                    ALU.add).then_inc(s_acc2, 1), s_acc2, S_ACC(p3, C3))
                        eng.wait_ge(s_pe2, PE2_DB(p3))
                if STAGE >= 4:
                    poll(FL_WO)
                if STAGE >= 5:
                    for _x5 in range(3):
                        eng.reg_alu(go, go, 0, ALU.add).then_inc(s_flag, 1)

        @block.tensor
        def _(eng):
            eng.wait_ge(s_gen, 1)
            for i, (nm, rows) in enumerate(VECS):
                ds_vl.wait(eng, i + 1)
                if i > 0:
                    eng.wait_ge(s_vecd, i)
                eng.transpose(psf[0:128, 6 * 512:6 * 512 + rows], vrow[0:rows, :],
                              identf[0:rows, 0:rows]).then_inc(s_vecp, 1)
            ds_vl.wait(eng, NV + 1)
            eng.wait_ge(s_vecd, NV)
            for c in range(16):
                inst = eng.matmul(psf[:, c * 128:(c + 1) * 128],
                                  onesrowf[:, :], brow[0:1, c * 128:(c + 1) * 128],
                                  start=(c % 4 == 0), stop=(c % 4 == 3),
                                  skip_group_check=True)
            inst.then_inc(s_vecp, 1)

            if STAGE < 2:
                return

            ds_wt.wait_all(eng, 16)
            eng.wait_ge(s_vecd, NV + 2)
            for blk in range(NBK):
                bb = blk % 2
                ds_nt.wait_all(eng, 16 * (blk + 1))
                if blk >= 1:
                    eng.wait_ge(s_kst, blk)
                for dout in range(16):
                    g = blk * 16 + dout
                    if g >= 4:
                        eng.wait_ge(s_ev, g - 3)
                    for k in range(16):
                        inst = eng.matmul(bkf(dout % 4),
                                          A_w1[k][:, dout * 128:(dout + 1) * 128],
                                          B_nt[bb][k], start=(k == 0), stop=(k == 15))
                    inst.then_inc(s_mm, 1)
                    eng.wait_ge(s_sq, g + 1)
                    inst = eng.matmul(psf[0:1, 4 * 512:5 * 512], ones128[:, 0:1],
                                      E_ksq[dout % 2],
                                      start=(dout == 0), stop=(dout == 15))
                inst.then_inc(s_pe2, 1)
            ds_wt.wait_all(eng, 32)
            for blk in range(NBK):
                bb = blk % 2
                ds_nt.wait_all(eng, 128 + 16 * (blk + 1))
                for ts in range(4):
                    for ns in range(4):
                        g = blk * 16 + ts * 4 + ns
                        if g >= 4:
                            eng.wait_ge(s_ev2, g - 3)
                        for k in range(16):
                            inst = eng.matmul(bkf(g % 4),
                                              B_nt[bb][k][:, ts * 128:(ts + 1) * 128],
                                              A_w1[k][:, ns * 512:(ns + 1) * 512],
                                              start=(k == 0), stop=(k == 15))
                        inst.then_inc(s_mm, 1)
            ds_wt.wait_all(eng, 48)
            ds_nt.wait_all(eng, 256 + 16)
            for dout in range(16):
                if dout >= 2:
                    eng.wait_ge(s_ev, EV_K + dout - 1)
                b0 = (dout % 2) * 2
                for k in range(16):
                    eng.matmul(bkf(b0), A_w1[k][:, dout * 128:(dout + 1) * 128],
                               B_ntq[k][:, 0:512], start=(k == 0), stop=(k == 15))
                for k in range(16):
                    inst = eng.matmul(bkf(b0 + 1), A_w1[k][:, dout * 128:(dout + 1) * 128],
                                      B_ntq[k][:, 512:1024], start=(k == 0), stop=(k == 15))
                inst.then_inc(s_mm, 1)
            for h in range(16):
                eng.wait_ge(s_sq, EV_K + h + 1)
                if h >= 1:
                    eng.wait_ge(s_kst, 8 + h)
                eng.matmul(psf[0:1, 4 * 512:5 * 512], ones128[:, 0:1], E_qsq[:, 0:512],
                           start=True, stop=True)
                eng.matmul(psf[0:1, 5 * 512:6 * 512], ones128[:, 0:1], E_qsq[:, 512:1024],
                           start=True, stop=True).then_inc(s_pe2, 1)
            eng.wait_ge(s_dve2, 1)
            eng.matmul(psf[:, 6 * 512:6 * 512 + 8], onesrowf[:, :], cpair[:, :],
                       start=True, stop=True).then_inc(s_pe2, 1)

            if STAGE < 3:
                return

            def pv_group(p, Cp):
                vg = 8 * p + Cp // 8
                if Cp % 8 == 0:
                    ds_vp.wait(eng, vg)
                st, sp = (Cp == 0), (Cp == 63)
                vpb = E_vp[vg % 2]
                sl = (Cp % 8) * 256
                ppb = E_pp[Cp % 2]
                eng.matmul(bkf(4), vpb[:, sl:sl + 128], ppb[:, 0:512], start=st, stop=sp)
                eng.matmul(bkf(5), vpb[:, sl:sl + 128], ppb[:, 512:1024],
                           start=st, stop=sp)
                eng.matmul(bkf(6), vpb[:, sl + 128:sl + 256], ppb[:, 1024:1536],
                           start=st, stop=sp)
                eng.matmul(bkf(7), vpb[:, sl + 128:sl + 256], ppb[:, 1536:2048],
                           start=st, stop=sp).then_inc(s_mm, 1)

            for p in range(8):
                kta = A_kt[(p % 2) * 2]
                ktb = A_kt[(p % 2) * 2 + 1]
                for C in range(64):
                    eng.wait_ge(s_exp, S_EXP(p, C) - 1)
                    if C == 0:
                        eng.wait_ge(s_dve2, S_DVE2_SW(p) - 1)
                        ds_kt.wait(eng, 2 * p)
                        ds_kt.wait(eng, 2 * p + 1)
                    eng.matmul(bkf(0), kta[:, C * 128:(C + 1) * 128], D_qt[2 * p][:, 0:512],
                               start=True, stop=True)
                    eng.matmul(bkf(1), kta[:, C * 128:(C + 1) * 128],
                               D_qt[2 * p][:, 512:1024], start=True, stop=True)
                    eng.matmul(bkf(2), ktb[:, C * 128:(C + 1) * 128],
                               D_qt[2 * p + 1][:, 0:512], start=True, stop=True)
                    eng.matmul(bkf(3), ktb[:, C * 128:(C + 1) * 128],
                               D_qt[2 * p + 1][:, 512:1024],
                               start=True, stop=True).then_inc(s_pe2, 1)
                    if C >= 1:
                        pv_group(p, C - 1)
                eng.wait_ge(s_exp, S_EXP(p, 63))
                pv_group(p, 63)
                eng.wait_ge(s_acc, S_ACC(p, 63))
                eng.wait_ge(s_acc2, S_ACC(p, 63))
                eng.matmul(bkf(0), ones128f[:, :], E_acc[:, 0:512], start=True, stop=True)
                eng.matmul(bkf(1), ones128f[:, :], E_acc[:, 512:1024], start=True, stop=True)
                eng.matmul(bkf(2), ones128f[:, :], E_acc[:, 1024:1536], start=True, stop=True)
                eng.matmul(bkf(3), ones128f[:, :], E_acc[:, 1536:2048],
                           start=True, stop=True).then_inc(s_pe2, 1)

            if STAGE < 4:
                return

            ds_wt.wait_all(eng, 64)
            eng.wait_ge(s_dve2, 10)
            ds_dbi.wait(eng, 0)
            for i in range(8):
                if i >= 2:
                    eng.wait_ge(s_dve2, 10 + i - 1)
                b0 = (i % 2) * 4
                for dseg in range(4):
                    for k in range(16):
                        inst = eng.matmul(bkf(b0 + dseg),
                                          C_ao[k][:, i * 128:(i + 1) * 128],
                                          A_w1[k][:, dseg * 512:(dseg + 1) * 512],
                                          start=(k == 0), stop=False,
                                          skip_group_check=True)
                    if dseg == 3:
                        inst.then_inc(s_mm, 1)
                    inst = eng.matmul(bkf(b0 + dseg), onesrowf[:, :],
                                      bo_row[0:1, dseg * 512:(dseg + 1) * 512],
                                      start=False, stop=True, skip_group_check=True)
                    if dseg == 3:
                        inst.then_inc(s_pe2, 1)

            if STAGE < 5:
                return

            eng.wait_ge(s_dve2, 18)
            ds_dbi.wait(eng, 1)
            for cc in range(16):
                inst = eng.matmul(psf[:, cc * 128:(cc + 1) * 128], onesrowf[:, :],
                                  sdpc_row[0:1, cc * 128:(cc + 1) * 128],
                                  start=(cc % 4 == 0), stop=(cc % 4 == 3),
                                  skip_group_check=True)
            inst.then_inc(s_pe2, 1)

            for hf in range(2):
                for tt in range(4):
                    eng.wait_ge(s_ds, DS5(hf) + tt + 1)
                    if tt >= 1:
                        eng.wait_ge(s_ev, EV5(hf) + tt)
                    elif hf == 1:
                        eng.wait_ge(s_dve2, DVE2_L4(0, 3))
                    dst = bkf(4, 4)
                    for cc in range(16):
                        inst = eng.matmul(dst[:, cc * 128:(cc + 1) * 128],
                                          C_n3f[:, cc * 128:(cc + 1) * 128],
                                          identf[:, :], is_transpose=True,
                                          start=(cc % 4 == 0), stop=(cc % 4 == 3),
                                          skip_group_check=True)
                    inst.then_inc(s_pe2, 1)
                eng.wait_ge(s_dve2, DVE2_H16(hf))
                _, sl_map = butterfly_plan(16)
                for cc in range(16):
                    if cc >= 4:
                        eng.wait_ge(s_ev, EV5(hf) + 4 + cc - 3)
                    eng.matmul(bkf(cc % 4), h128[:, :], C_xt[sl_map[cc]],
                               start=True, stop=True).then_inc(s_mm, 1)
                for wi in range(2):
                    for B in range(8):
                        ds_wb.wait_range(eng, hf * 384 + wi * 128 + B * 16, 16)
                        for c8 in range(8):
                            g5 = B * 8 + c8
                            if wi == 0 and B == 0 and c8 < 4:
                                eng.wait_ge(s_ev, EV5(hf) + 20)
                            else:
                                gg = wi * 64 + g5
                                if gg >= 4:
                                    eng.wait_ge(s_ev, EV5(hf) + 20 + gg - 3)
                            for k in range(16):
                                inst = eng.matmul(bkf(c8 % 4),
                                                  A_wb[B % 2][k][:, c8 * 128:(c8 + 1) * 128],
                                                  C_xt[sl_map[k]],
                                                  start=(k == 0), stop=(k == 15))
                            inst.then_inc(s_mm, 1)
                eng.wait_ge(s_dve2, DVE2_H64(hf))
                _, slz = butterfly_plan(64)

                def zph(sl5):
                    return Z[sl5] if sl5 < 64 else C_xt[16]

                for cc in range(64):
                    if cc >= 4:
                        eng.wait_ge(s_ev, EV5(hf) + 148 + cc - 3)
                    eng.matmul(bkf(cc % 4), h128[:, :], zph(slz[cc]),
                               start=True, stop=True).then_inc(s_mm, 1)
                for dh2 in range(2):
                    for k2 in range(64):
                        kk = dh2 * 64 + k2
                        ds_wb.wait(eng, hf * 384 + 256 + kk)
                        if k2 == 0:
                            eng.wait_ge(s_ev, EV5(hf) + 148 + 64)
                            if dh2 == 1:
                                eng.wait_ge(s_ev, EV5(hf) + 212 + 4)
                        for tt in range(4):
                            for ds2 in range(2):
                                inst = eng.matmul(
                                    bkf(tt * 2 + ds2),
                                    zph(slz[k2])[:, tt * 128:(tt + 1) * 128],
                                    A_wd[k2 % 2][:, ds2 * 512:(ds2 + 1) * 512],
                                    start=(k2 == 0), stop=(k2 == 63),
                                    skip_group_check=True)
                        inst.then_inc(s_mm, 1)

        @block.scalar
        def _(eng):
            amark, amarkpub, await_ = chain(eng, s_cha)
            for i in range(NT_ALL):
                eng.wait_ge(s_st, i + 1)
                amark(eng.activation(ln_mv[:, 1:2], ln_mv[:, 1:2], AF.Ln,
                                     bias=cst[:, 0:1], scale=1.0))
                await_()
                eng.activation(ln_r[:], ln_mv[:, 1:2], AF.Exp, bias=cst[:, 1:2],
                               scale=-0.5).then_inc(s_lnr, 1)

            if STAGE < 2:
                return

            for blk in range(NBK):
                for dout in range(16):
                    g = blk * 16 + dout
                    eng.wait_ge(s_mm, g + 1)
                    if blk >= 1:
                        ds_ko.wait(eng, (blk - 1) * 16 + dout)
                        eng.wait_ge(s_sq, g - 16 + 1)
                    eng.activation(C_ko[dout], bkf(dout % 4), AF.Identity,
                                   bias=bkc[:, dout:dout + 1], scale=1.0).then_inc(s_ev, 1)
            for dout in range(16):
                eng.wait_ge(s_mm, MM_V + dout + 1)
                amark(eng.activation(D_qt[dout][:, 0:512], bkf((dout % 2) * 2), AF.Identity,
                                     bias=bqc[:, dout:dout + 1], scale=RSQ_DH))
                await_()
                eng.activation(D_qt[dout][:, 512:1024], bkf((dout % 2) * 2 + 1), AF.Identity,
                               bias=bqc[:, dout:dout + 1], scale=RSQ_DH).then_inc(s_ev, 1)
            eng.wait_ge(s_kst, 24)
            eng.wait_ge(s_ds, 1)
            amark(eng.activation(cwork[:], cwork[:], AF.Ln, bias=cst[0:1, 1:2], scale=1.0))
            await_()
            eng.activation(cwork[:], cwork[:], AF.Exp, bias=cst[0:1, 1:2],
                           scale=0.5).then_inc(s_lnr, 1)

            if STAGE < 3:
                return

            eng.wait_ge(s_dve2, 2)
            for p in range(8):
                for C in range(64):
                    eng.wait_ge(s_pe2, PE2_S(p, C))
                    if C >= 2:
                        eng.wait_ge(s_mm, MM_PV(p, C - 2))
                        eng.wait_ge(s_acc, S_ACC(p, C - 2))
                        eng.wait_ge(s_acc2, S_ACC(p, C - 2))
                    eng.activation(E_pp[C % 2], psf[:, 0:2048], AF.Exp,
                                   bias=negCb[:, p:p + 1],
                                   scale=1.0).then_inc(s_exp, 1)

            if STAGE < 4:
                return

            for i in range(8):
                eng.wait_ge(s_st, ST_P4 + i + 1)
                amark(eng.activation(ln_mv[:, 1:2], ln_mv[:, 1:2], AF.Ln,
                                     bias=cst[:, 0:1], scale=1.0))
                await_()
                eng.activation(ln_r[:], ln_mv[:, 1:2], AF.Exp, bias=cst[:, 1:2],
                               scale=-0.5).then_inc(s_lnr, 1)

            if STAGE < 5:
                return

            eng.wait_ge(s_pe2, 554)
            eng.activation(sdpc_bb[:], bkf(0, 4), AF.Identity, bias=cst[:, 1:2],
                           scale=64.0).then_inc(s_ev, 1)
            for hf in range(2):
                _, sl_map = butterfly_plan(16)
                _, slz = butterfly_plan(64)
                for tt in range(4):
                    eng.wait_ge(s_st, ST5(hf) + tt + 1)
                    amark(eng.activation(ln_mv[:, 1:2], ln_mv[:, 1:2], AF.Ln,
                                         bias=cst[:, 0:1], scale=1.0))
                    await_()
                    eng.activation(ln_r[:], ln_mv[:, 1:2], AF.Exp, bias=cst[:, 1:2],
                                   scale=-0.5).then_inc(s_lnr, 1)
                    eng.wait_ge(s_pe2, PE25(hf) + tt + 1)
                    for cc in range(16):
                        src2 = bkf(4, 4)[:, cc * 128:(cc + 1) * 128]
                        inst = eng.activation(C_xt[cc][:, tt * 128:(tt + 1) * 128], src2,
                                              AF.Identity, bias=cst[:, 1:2],
                                              scale=su1c[:, cc:cc + 1])
                        if cc == 15:
                            inst.then_inc(s_ev, 1)
                        else:
                            amark(inst)
                            await_()
                for cc in range(16):
                    eng.wait_ge(s_mm, MM5(hf) + cc + 1)
                    eng.activation(C_xt[sl_map[cc]], bkf(cc % 4), AF.Identity,
                                   bias=cst[:, 1:2],
                                   scale=1.0).then_inc(s_ev, 1)
                if hf == 1:
                    ds_out.wait_range(eng, 0, 4)
                for wi in range(2):
                    for B in range(8):
                        for c8 in range(8):
                            g5 = B * 8 + c8
                            eng.wait_ge(s_mm, MM5(hf) + 16 + wi * 64 + g5 + 1)
                            cgl = B * 8 + c8
                            if wi == 0:
                                eng.activation(Z[cgl], bkf(c8 % 4), AF.Identity,
                                               bias=cst[:, 1:2],
                                               scale=supc[:, cgl:cgl + 1]
                                               ).then_inc(s_ev, 1)
                            else:
                                if g5 >= 2:
                                    eng.wait_ge(s_ev2, EV25(hf) + g5 - 1)
                                eng.activation(C_sg[c8 % 2], bkf(c8 % 4), AF.Silu,
                                               bias=cst[:, 1:2],
                                               scale=sgtc[:, cgl:cgl + 1]
                                               ).then_inc(s_ev, 1)
                for cc in range(64):
                    eng.wait_ge(s_mm, MM5(hf) + 144 + cc + 1)
                    zt = Z[slz[cc]] if slz[cc] < 64 else C_xt[16]
                    eng.activation(zt, bkf(cc % 4), AF.Identity, bias=cst[:, 1:2],
                                   scale=1.0).then_inc(s_ev, 1)
                for tt in range(4):
                    eng.wait_ge(s_ev2, EV25(hf) + 64 + tt + 1)
                    eng.activation(C_ao[tt], bkf(tt * 2, 2), AF.Identity,
                                   bias=cst[:, 1:2],
                                   scale=1.0).then_inc(s_ev, 1)
                for tt in range(4):
                    eng.wait_ge(s_st, ST5(hf) + 4 + tt + 1)
                    amark(eng.activation(ln_mv[:, 1:2], ln_mv[:, 1:2], AF.Ln,
                                         bias=cst[:, 0:1], scale=1.0))
                    await_()
                    eng.activation(ln_r[:], ln_mv[:, 1:2], AF.Exp, bias=cst[:, 1:2],
                                   scale=-0.5).then_inc(s_lnr, 1)

        @block.vector
        def _(eng):
            vmark, vmarkpub, vwait = chain(eng, s_chv)
            eng.wait_ge(s_gen, 1)
            vmark(eng.tensor_copy(identb[:], identf[:]))
            vwait()
            vmark(eng.tensor_tensor(hwork[:], iota_i[:], iota_j[:], ALU.bitwise_and))
            for sh in (4, 2, 1):
                vwait()
                vmark(eng.tensor_scalar(iota_j[:], hwork[:], sh, None,
                                        ALU.logical_shift_right))
                vwait()
                vmark(eng.tensor_tensor(hwork[:], hwork[:], iota_j[:], ALU.bitwise_xor))
            vwait()
            vmark(eng.tensor_scalar(hwork[:], hwork[:], 1, None, ALU.bitwise_and))
            vwait()
            vmark(eng.tensor_copy(hworkf[:], hwork[:]))
            vwait()
            vmark(eng.tensor_scalar(h128[:], hworkf[:], -2.0, 1.0, ALU.mult, ALU.add))

            for i, (nm, rows) in enumerate(VECS):
                eng.wait_ge(s_vecp, i + 1)
                srcp = psf[0:128, 6 * 512:6 * 512 + rows]
                if nm == "bq":
                    inst = eng.tensor_scalar(bqc[:], srcp, RSQ_DH, None, ALU.mult)
                elif nm == "bk":
                    inst = eng.tensor_copy(bkc[:], srcp)
                elif nm == "bv":
                    inst = eng.tensor_copy(hworkf[:, 0:16], srcp)
                elif nm == "bo":
                    inst = eng.tensor_copy(boc[:], srcp)
                elif nm == "su_ug":
                    inst = eng.tensor_scalar(su1c[:], srcp,
                                             float(1.0 / (64.0 * np.sqrt(2048.0))),
                                             None, ALU.mult)
                elif nm == "wsc_dp":
                    inst = eng.tensor_scalar(sdpc[:], srcp, 64.0, None, ALU.mult)
                elif nm == "su_dp":
                    inst = eng.tensor_copy(sdp_t[:], srcp)
                elif nm == "wsc_ug":
                    inst = eng.tensor_copy(wsc_t[:], srcp)
                inst.then_inc(s_vecd, 1)
            eng.wait_ge(s_vecd, NV)
            vmark(eng.tensor_tensor(supc[:], wsc_t[:, 0:64], sdp_t[:], ALU.mult))
            vwait()
            vmark(eng.tensor_scalar(supc[:], supc[:], float(1.0 / np.sqrt(8192.0)),
                                    None, ALU.mult))
            vwait()
            eng.tensor_scalar(sgtc[:], wsc_t[:, 64:128], 64.0, None,
                              ALU.mult).then_inc(s_vecd, 1)

            for i in range(NT_ALL):
                ds_xh.wait(eng, i)
                for sl in range(4):
                    vmark(eng.bn_stats(bnst[:, sl * 6:(sl + 1) * 6],
                                       E_xh[i % 2][:, sl * 512:(sl + 1) * 512]))
                vwait()
                eng.bn_aggr(ln_mv[:], bnst[:]).then_inc(s_st, 1)
                eng.wait_ge(s_lnr, i + 1)
                vmark(eng.tensor_scalar(ln_mr[:], ln_mv[:, 0:1], ln_r[:], -1.0,
                                        ALU.mult, ALU.mult))
                if i >= 2:
                    ds_nxw.wait(eng, i - 2)
                vwait()
                vmarkpub(eng.tensor_scalar(E_nx[i % 2], E_xh[i % 2], ln_r[:], ln_mr[:],
                                           ALU.mult, ALU.add).then_inc(s_ln, 1),
                         s_ln, i + 1)
                if STAGE == 1 and i >= NTH:
                    if i > NTH:
                        ds_dbo.wait(eng, i - NTH - 1)
                    vwait()
                    vmarkpub(eng.tensor_copy(E_dbg[:], E_nx[i % 2]).then_inc(s_dbg, 1),
                             s_dbg, i - NTH + 1)

            if STAGE < 2:
                return

            eng.wait_ge(s_vecp, NV + 1)
            eng.tensor_copy(bvbc[:], psf[:, 0:2048]).then_inc(s_vecd, 1)

            for blk in range(NBK):
                for dout in range(16):
                    g = blk * 16 + dout
                    eng.wait_ge(s_ev, g + 1)
                    if g >= 2:
                        eng.wait_ge(s_mm, g)
                    vwait()
                    vmarkpub(eng.tensor_tensor(E_ksq[g % 2], C_ko[dout], C_ko[dout],
                                               ALU.mult).then_inc(s_sq, 1), s_sq, g + 1)
                eng.wait_ge(s_pe2, blk + 1)
                vwait()
                vmark(eng.reduce_max(cwork[0:1, 0:1], psf[0:1, 4 * 512:5 * 512], axis=AX.X))
                vwait()
                if blk == 0:
                    vmarkpub(eng.tensor_copy(kmaxsq[0:1, 0:1],
                                             cwork[0:1, 0:1]).then_inc(s_kst, 1),
                             s_kst, blk + 1)
                else:
                    vmarkpub(eng.tensor_tensor(kmaxsq[0:1, 0:1], kmaxsq[0:1, 0:1],
                                               cwork[0:1, 0:1], ALU.max).then_inc(s_kst, 1),
                             s_kst, blk + 1)
                vwait()

            for blk in range(NBK):
                for ts in range(4):
                    for ns in range(4):
                        g = blk * 16 + ts * 4 + ns
                        eng.wait_ge(s_mm, MM_K + g + 1)
                        if blk >= 1:
                            ds_vv.wait(eng, (blk - 1) * 4 + ts)
                        eng.tensor_tensor(C_vo[ts][:, ns * 512:(ns + 1) * 512],
                                          bkf(g % 4), bvbc[:, ns * 512:(ns + 1) * 512],
                                          ALU.add).then_inc(s_ev2, 1)

            for h in range(16):
                eng.wait_ge(s_ev, EV_K + h + 1)
                if h >= 1:
                    eng.wait_ge(s_pe2, 8 + h)
                eng.tensor_tensor(E_qsq, D_qt[h], D_qt[h], ALU.mult).then_inc(s_sq, 1)
                eng.wait_ge(s_pe2, 8 + h + 1)
                vwait()
                vmark(eng.reduce_max(cwork[0:1, 0:1], psf[0:1, 4 * 512:6 * 512], axis=AX.X))
                vwait()
                vmarkpub(eng.tensor_copy(qmaxsq[0:1, h:h + 1],
                                         cwork[0:1, 0:1]).then_inc(s_kst, 1),
                         s_kst, 8 + h + 1)
                vwait()

            eng.wait_ge(s_flag, 3)
            ds_ksf.wait(eng, 0)
            vmark(eng.tensor_tensor(kst_sb[0:1, 0:1], kst_sb[0:1, 0:1],
                                    kmaxsq[0:1, 0:1], ALU.max))
            vwait()
            eng.tensor_scalar(cwork[:], qmaxsq[:], kst_sb[0:1, 0:1], None,
                              ALU.mult).then_inc(s_ds, 1)
            eng.wait_ge(s_lnr, NT_ALL + 1)
            vmark(eng.tensor_tensor(cpair[:], cwork[0:1, 0:16:2], cwork[0:1, 1:16:2],
                                    ALU.max))
            vwait()
            eng.tensor_scalar(cpair[:], cpair[:], -1.0, None,
                              ALU.mult).then_inc(s_dve2, 1)
            eng.wait_ge(s_pe2, PE2_P2)
            eng.tensor_copy(negCb[:], psf[:, 6 * 512:6 * 512 + 8]).then_inc(s_dve2, 1)

            if STAGE >= 3:
                for p in range(8):
                    for C in range(64):
                        eng.wait_ge(s_exp, S_EXP(p, C))
                        if C >= 1:
                            eng.wait_ge(s_acc, S_ACC(p, C - 1))
                        vwait()
                        if C == 0:
                            vmarkpub(eng.tensor_copy(E_acc[:, 0:1024],
                                                     E_pp[0][:, 0:1024]
                                                     ).then_inc(s_acc, 1),
                                     s_acc, S_ACC(p, C))
                        else:
                            vmarkpub(eng.tensor_tensor(E_acc[:, 0:1024], E_acc[:, 0:1024],
                                                       E_pp[C % 2][:, 0:1024],
                                                       ALU.add).then_inc(s_acc, 1),
                                     s_acc, S_ACC(p, C))
                    eng.wait_ge(s_pe2, PE2_DB(p))
                    vwait()
                    vmark(eng.reciprocal(E_dinv[:, 0:1024], bkf(0, 2)))
                    vwait()
                    vmark(eng.reciprocal(E_dinv[:, 1024:2048], bkf(2, 2)))
                    vwait()
                    vmark(eng.tensor_tensor(C_ao[2 * p], bkf(4, 2),
                                            E_dinv[:, 0:1024], ALU.mult))
                    vwait()
                    vmarkpub(eng.tensor_tensor(C_ao[2 * p + 1], bkf(6, 2),
                                               E_dinv[:, 1024:2048],
                                               ALU.mult).then_inc(s_dve2, 1),
                             s_dve2, S_DVE2_SW(p))

            if STAGE == 3:
                for k in range(16):
                    if k > 0:
                        ds_dbo.wait(eng, k - 1)
                    vwait()
                    vmarkpub(eng.tensor_copy(B_res[0][:, 0:1024],
                                             C_ao[k]).then_inc(s_dbg, 1), s_dbg, k + 1)
                return

            if STAGE == 2:
                for k in range(16):
                    if k > 0:
                        ds_dbo.wait(eng, k - 1)
                    vwait()
                    vmarkpub(eng.tensor_copy(E_dbg[:, 0:1024],
                                             D_qt[k]).then_inc(s_dbg, 1), s_dbg, k + 1)
                ds_dbo.wait(eng, 15)
                ds_dbi.wait(eng, 0)
                vwait()
                vmarkpub(eng.tensor_copy(E_dbg[:], E_pp[0]).then_inc(s_dbg, 1),
                         s_dbg, 17)
                ds_dbo.wait(eng, 16)
                ds_dbi.wait(eng, 1)
                vwait()
                vmarkpub(eng.tensor_copy(E_dbg[:], E_pp[1]).then_inc(s_dbg, 1),
                         s_dbg, 18)
                ds_dbo.wait(eng, 17)
                vwait()
                vmark(eng.memset(E_dbg[:], 0.0))
                vwait()
                vmark(eng.tensor_copy(E_dbg[:, 0:8], negCb[:]))
                vwait()
                vmark(eng.tensor_copy(E_dbg[0:1, 8:24], kmaxsq[:]))
                vwait()
                vmark(eng.tensor_copy(E_dbg[0:1, 24:40], qmaxsq[:]))
                vwait()
                eng.tensor_copy(E_dbg[0:1, 40:56], kst_sb[:]).then_inc(s_dbg, 1)
                return

            if STAGE < 4:
                return

            for i in range(8):
                eng.wait_ge(s_pe2, PE2_P3 + i + 1)
                src = bkf((i % 2) * 4, 4)
                for sl in range(4):
                    vmark(eng.bn_stats(bnst[:, sl * 6:(sl + 1) * 6],
                                       src[:, sl * 512:(sl + 1) * 512]))
                vwait()
                eng.bn_aggr(ln_mv[:], bnst[:]).then_inc(s_st, 1)
                eng.wait_ge(s_lnr, LNR_P4 + i + 1)
                vmark(eng.tensor_scalar(ln_mr[:], ln_mv[:, 0:1], ln_r[:], -1.0,
                                        ALU.mult, ALU.mult))
                ds_x2.wait(eng, i)
                if i >= 2:
                    ds_nxw.wait(eng, NT_ALL + i - 2)
                vwait()
                vmarkpub(eng.tensor_scalar(D_x2[i % 2], src, ln_r[:], ln_mr[:],
                                           ALU.mult, ALU.add).then_inc(s_dve2, 1),
                         s_dve2, 10 + i + 1)
                vwait()
                vmarkpub(eng.tensor_tensor(D_x2[i % 2], D_x2[i % 2], D_xo[i % 2],
                                           ALU.add).then_inc(s_ln, 1),
                         s_ln, LN_P4 + i + 1)

            if STAGE < 5:
                return

            for hf in range(2):
                ops16, sl_map = butterfly_plan(16)
                ops64, slz = butterfly_plan(64)

                def zph(sl5):
                    return Z[sl5] if sl5 < 64 else C_xt[16]

                for tt in range(4):
                    ds_x2.wait(eng, 8 + hf * 4 + tt)
                    for sl in range(4):
                        vmark(eng.bn_stats(bnst[:, sl * 6:(sl + 1) * 6],
                                           B_res[tt][:, sl * 512:(sl + 1) * 512]))
                    vwait()
                    eng.bn_aggr(ln_mv[:], bnst[:]).then_inc(s_st, 1)
                    eng.wait_ge(s_lnr, LNR5(hf) + tt + 1)
                    vmark(eng.tensor_scalar(ln_mr[:], ln_mv[:, 0:1], ln_r[:], -1.0,
                                            ALU.mult, ALU.mult))
                    if tt >= 1:
                        eng.wait_ge(s_pe2, PE25(hf) + tt)
                    elif hf == 1:
                        eng.wait_ge(s_pe2, PE25(0) + 4)
                    vwait()
                    vmarkpub(eng.tensor_scalar(C_n3f[:], B_res[tt], ln_r[:],
                                               ln_mr[:], ALU.mult,
                                               ALU.add).then_inc(s_ds, 1),
                             s_ds, DS5(hf) + tt + 1)
                eng.wait_ge(s_ev, EV5(hf) + 4)
                for j, (ba, bb, sp) in enumerate(ops16):
                    vwait()
                    vmark(eng.tensor_tensor(C_xt[sp], C_xt[ba], C_xt[bb], ALU.add))
                    vwait()
                    inst = eng.tensor_tensor(C_xt[bb], C_xt[ba], C_xt[bb],
                                             ALU.subtract)
                    if j == len(ops16) - 1:
                        vmarkpub(inst.then_inc(s_dve2, 1), s_dve2, DVE2_H16(hf))
                    else:
                        vmark(inst)
                for g5 in range(64):
                    eng.wait_ge(s_ev, EV5(hf) + 84 + g5 + 1)
                    vwait()
                    vmarkpub(eng.tensor_tensor(Z[g5], Z[g5], C_sg[g5 % 2],
                                               ALU.mult).then_inc(s_ev2, 1),
                             s_ev2, EV25(hf) + g5 + 1)
                for j, (ba, bb, sp) in enumerate(ops64):
                    vwait()
                    vmark(eng.tensor_tensor(zph(sp), zph(ba), zph(bb), ALU.add))
                    vwait()
                    inst = eng.tensor_tensor(zph(bb), zph(ba), zph(bb), ALU.subtract)
                    if j == len(ops64) - 1:
                        vmarkpub(inst.then_inc(s_dve2, 1), s_dve2, DVE2_H64(hf))
                    else:
                        vmark(inst)
                eng.wait_ge(s_ev, EV5(0))
                for dh2 in range(2):
                    eng.wait_ge(s_mm, MM5(hf) + 208 + dh2 * 64 + 64)
                    for tt in range(4):
                        vwait()
                        vmark(eng.tensor_tensor(bkf(tt * 2, 2), bkf(tt * 2, 2),
                                                sdpc_bb[:, dh2 * 1024:(dh2 + 1) * 1024],
                                                ALU.mult))
                        vwait()
                        vmark(eng.bn_stats(bnst5[:, (tt * 4 + dh2 * 2) * 6:
                                                 (tt * 4 + dh2 * 2 + 1) * 6],
                                           bkf(tt * 2)))
                        vwait()
                        inst = eng.bn_stats(bnst5[:, (tt * 4 + dh2 * 2 + 1) * 6:
                                                  (tt * 4 + dh2 * 2 + 2) * 6],
                                            bkf(tt * 2 + 1))
                        if dh2 == 0:
                            vmarkpub(inst.then_inc(s_ev2, 1), s_ev2,
                                     EV25(hf) + 64 + tt + 1)
                            continue
                        vmark(inst)
                        vwait()
                        eng.bn_aggr(ln_mv[:], bnst5[:, tt * 24:(tt + 1) * 24]
                                    ).then_inc(s_st, 1)
                        eng.wait_ge(s_lnr, LNR5(hf) + 4 + tt + 1)
                        vmark(eng.tensor_scalar(ln_mr[:], ln_mv[:, 0:1], ln_r[:],
                                                -1.0, ALU.mult, ALU.mult))
                        if hf == 1 or tt >= 2:
                            ds_out.wait(eng, hf * 4 + tt - 2)
                        vwait()
                        vmark(eng.tensor_scalar(P5_out[tt % 2][:, 0:1024], C_ao[tt],
                                                ln_r[:], ln_mr[:], ALU.mult, ALU.add))
                        vwait()
                        vmarkpub(eng.tensor_scalar(P5_out[tt % 2][:, 1024:2048],
                                                   bkf(tt * 2, 2), ln_r[:], ln_mr[:],
                                                   ALU.mult,
                                                   ALU.add).then_inc(s_dve2, 1),
                                 s_dve2, DVE2_L4(hf, tt))
                        vwait()
                        vmarkpub(eng.tensor_tensor(P5_out[tt % 2], P5_out[tt % 2],
                                                   B_res[tt], ALU.add
                                                   ).then_inc(s_ln, 1),
                                 s_ln, LN5(hf) + tt + 1)

    return nc


_CACHE = {}


def kernel(L0_attn_X, L0_attn_Wq, L0_attn_bq, L0_attn_Wk, L0_attn_bk, L0_attn_Wv,
           L0_attn_bv, L0_attn_Wo, L0_attn_bo, L0_ffn_SU_ug, L0_ffn_SU_dp,
           L0_ffn_Wscale_ug, L0_ffn_Wscale_dp, L0_ffn_W_up, L0_ffn_W_gate,
           L0_ffn_W_down):
    X = np.ascontiguousarray(np.asarray(L0_attn_X).reshape(S, D))
    if "nc" not in _CACHE:
        _CACHE["nc"] = build_program()
    nc = _CACHE["nc"]
    nonce_val = int(np.random.SeedSequence().entropy % (2**31 - 2)) + 1
    common = {
        "wq": np.ascontiguousarray(L0_attn_Wq), "wk": np.ascontiguousarray(L0_attn_Wk),
        "wv": np.ascontiguousarray(L0_attn_Wv), "wo": np.ascontiguousarray(L0_attn_Wo),
        "wup": np.ascontiguousarray(L0_ffn_W_up),
        "wgate": np.ascontiguousarray(L0_ffn_W_gate),
        "wdown": np.ascontiguousarray(L0_ffn_W_down),
        "bq": np.ascontiguousarray(np.asarray(L0_attn_bq).reshape(1, D)),
        "bk": np.ascontiguousarray(np.asarray(L0_attn_bk).reshape(1, D)),
        "bv": np.ascontiguousarray(np.asarray(L0_attn_bv).reshape(1, D)),
        "bo": np.ascontiguousarray(np.asarray(L0_attn_bo).reshape(1, D)),
        "su_ug": np.ascontiguousarray(np.asarray(L0_ffn_SU_ug).reshape(1, D)),
        "su_dp": np.ascontiguousarray(np.asarray(L0_ffn_SU_dp).reshape(1, INTER)),
        "wsc_ug": np.ascontiguousarray(np.asarray(L0_ffn_Wscale_ug).reshape(1, 2 * INTER)),
        "wsc_dp": np.ascontiguousarray(np.asarray(L0_ffn_Wscale_dp).reshape(1, D)),
        "nonce": np.full((1, 16), nonce_val, np.int32),
    }
    in_maps = []
    for c in range(NC_):
        side = c % 2
        m = dict(common)
        m["x_own"] = np.ascontiguousarray(X[c * TOK:(c + 1) * TOK, :])
        m["x_half"] = np.ascontiguousarray(X[side * HALF:(side + 1) * HALF, :])
        in_maps.append(m)
    res = run_bass_kernel_spmd(nc, in_maps, core_ids=list(range(NC_)))
    outp = np.concatenate([res.results[c]["out"] for c in range(NC_)], axis=0)
    return outp.reshape(1, S, D)


# version 02
# speedup vs baseline: 1.1116x; 1.1116x over previous
import dataclasses
import numpy as np
import concourse.bass as bass
import concourse.mybir as mybir
from concourse.bass_utils import run_bass_kernel_spmd
from contextlib import ExitStack

F32 = mybir.dt.float32
BF16 = mybir.dt.bfloat16
I32 = mybir.dt.int32
AF = mybir.ActivationFunctionType
ALU = mybir.AluOpType
ET = mybir.EngineType
AX = mybir.AxisListType

S, D, NH, DH, INTER = 8192, 2048, 16, 128, 8192
NC_ = 8
TOK = S // NC_
HALF = S // 2
DC = D // 128
IC = INTER // 128
EPS = 1e-5
RSQ_DH = float(1.0 / np.sqrt(128.0))
NBK = HALF // 512
NTH = HALF // 128
NTO = TOK // 128
NT_ALL = NTH + NTO

STAGE = 5

OFF_A = 0
OFF_B = 64 * 1024
OFF_C = 96 * 1024
OFF_E = 128 * 1024
OFF_D = 160 * 1024
OFF_F = 192 * 1024
SBUF_LIMIT = 207 * 1024

FL_WKV, FL_WQ, FL_KV, FL_WO, FL_WUP, FL_WGATE, FL_WDOWN = range(7)

VECS = [("bq", 16), ("bk", 16), ("bv", 16), ("bo", 16),
        ("su_ug", 16), ("wsc_dp", 16), ("su_dp", 64), ("wsc_ug", 128)]
NV = len(VECS)
CAST_ORDER = ["wk", "wv", "wq", "wo", "wup", "wgate", "wdown"]
CAST_T0 = {nm: i for i, nm in enumerate(CAST_ORDER)}
CAST_NT = {nm: 1 for nm in CAST_ORDER}
def cast_wait(ds, eng, nm):
    ds.wait_range(eng, CAST_T0[nm], CAST_NT[nm])

PE2_P2 = 25
MM_K, MM_V, MM_Q = 128, 256, 272
EV_K, EV_Q = 128, 144
def PE2_S(p, C):
    return PE2_P2 + 65 * p + C + 1
def PE2_DB(p):
    return PE2_P2 + 65 * p + 65
PE2_P3 = PE2_P2 + 8 * 65
def MM_PV(p, C):
    return MM_Q + 64 * p + C + 1
MM_P3 = MM_Q + 512
def S_EXP(p, C):
    return 64 * p + C + 1
def S_ACC(p, C):
    return 64 * p + C + 1
def S_DVE2_SW(p):
    return 2 + p + 1
MM_WO = MM_P3 + 8
PE2_P4 = PE2_P3 + 8
def MM5(hf):
    return MM_WO + hf * 336
def PE25(hf):
    return 554 + hf * 4
def EV5(hf):
    return 145 + hf * 216
def EV25(hf):
    return EV2_V + hf * 68
EV2_V = 128
ST_P4, LNR_P4, LN_P4 = 40, 41, 40
def ST5(hf):
    return 48 + hf * 8
def LNR5(hf):
    return 49 + hf * 8
def LN5(hf):
    return 48 + hf * 4
def DS5(hf):
    return 1 + hf * 4
def DVE2_H16(hf):
    return 19 + hf * 6
def DVE2_H64(hf):
    return 20 + hf * 6
def DVE2_L4(hf, tt):
    return 20 + hf * 6 + tt + 1


def butterfly_plan(n):
    slots = list(range(n))
    spare = n
    ops = []
    h = 1
    while h < n:
        for j in range(0, n, 2 * h):
            for i in range(h):
                a, b = j + i, j + i + h
                ops.append((slots[a], slots[b], spare))
                slots[a], spare = spare, slots[a]
        h *= 2
    return ops, slots


def build_program():
    nc = bass.Bass("TRN2", target_bir_lowering=False, debug=False,
                   detect_race_conditions=False)

    x_own = nc.dram_tensor("x_own", [TOK, D], F32, kind="ExternalInput").ap()
    x_half = nc.dram_tensor("x_half", [HALF, D], F32, kind="ExternalInput").ap()
    w_in, wbf = {}, {}
    for nm, shp in [("wq", [D, D]), ("wk", [D, D]), ("wv", [D, D]), ("wo", [D, D]),
                    ("wup", [INTER, D]), ("wgate", [INTER, D]), ("wdown", [D, INTER])]:
        w_in[nm] = nc.dram_tensor(nm, shp, F32, kind="ExternalInput").ap()
        wbf[nm] = nc.dram_tensor("bf_" + nm, shp, BF16, kind="Internal",
                                 addr_space="Shared").ap()
    vec_in = {}
    for nm, rows in VECS:
        vec_in[nm] = nc.dram_tensor(nm, [1, rows * 128], F32, kind="ExternalInput").ap()
    nonce = nc.dram_tensor("nonce", [1, 16], I32, kind="ExternalInput").ap()
    out = nc.dram_tensor("out", [TOK, D], F32, kind="ExternalOutput").ap()
    dbg = nc.dram_tensor("dbg", [TOK, D], F32, kind="ExternalOutput").ap() if STAGE < 5 else None
    dbgk = (nc.dram_tensor("dbgk", [D, TOK], F32, kind="ExternalOutput").ap()
            if STAGE in (2, 3) else None)

    nxh_d = nc.dram_tensor("nxh_d", [HALF, D], BF16, kind="Internal").ap()
    nxo_d = nc.dram_tensor("nxo_d", [TOK, D], BF16, kind="Internal").ap()
    x2_d = nc.dram_tensor("x2_d", [TOK, D], F32, kind="Internal").ap()
    kloc_d = nc.dram_tensor("kloc_d", [DC, 128, HALF], BF16, kind="Internal").ap()
    vloc_d = nc.dram_tensor("vloc_d", [HALF, D], BF16, kind="Internal").ap()
    ksh_d = nc.dram_tensor("ksh_d", [DC, 128, S], BF16, kind="Internal",
                           addr_space="Shared").ap()
    vsh_d = nc.dram_tensor("vsh_d", [S, D], BF16, kind="Internal",
                           addr_space="Shared").ap()
    kstat_d = nc.dram_tensor("kstat_d", [2, 1, 16], F32, kind="Internal",
                             addr_space="Shared").ap()
    flags_d = nc.dram_tensor("flags_d", [2, 7, 16], I32, kind="Internal",
                             addr_space="Shared").ap()

    with ExitStack() as ctx:
        e = ctx.enter_context
        base = (nc.sbuf_base + 31) & ~31
        usable = nc.sbuf_top - base
        assert usable >= SBUF_LIMIT, f"usable sbuf {usable} < {SBUF_LIMIT}"

        def sb(name, shape, dtype, off):
            end = off + int(np.prod(shape[1:])) * mybir.dt.size(dtype)
            assert end <= usable, (name, end, usable)
            return nc.alloc_sbuf_tensor_at(name, shape, dtype, offset=base + off).ap()

        _fo = [OFF_F]
        def sbf(name, shape, dtype):
            t = sb(name, shape, dtype, _fo[0])
            _fo[0] += ((int(np.prod(shape[1:])) * mybir.dt.size(dtype) + 31) // 32) * 32
            return t

        identb = sbf("identb", [128, 128], BF16)
        identf = sbf("identf", [128, 128], F32)
        h128 = sbf("h128", [128, 128], BF16)
        ones128 = sbf("ones128", [128, 128], BF16)
        onesrowf = sbf("onesrowf", [1, 128], F32)
        ones128f = sbf("ones128f", [128, 128], F32)
        iota_i = sbf("iota_i", [128, 128], I32)
        iota_j = sbf("iota_j", [128, 128], I32)
        hwork = sbf("hwork", [128, 128], I32)
        hworkf = sbf("hworkf", [128, 128], F32)
        bqc = sbf("bqc", [128, 16], F32)
        bkc = sbf("bkc", [128, 16], F32)
        boc = sbf("boc", [128, 16], F32)
        su1c = sbf("su1c", [128, 16], F32)
        sdpc = sbf("sdpc", [128, 16], F32)
        sdp_t = sbf("sdp_t", [128, 64], F32)
        wsc_t = sbf("wsc_t", [128, 128], F32)
        supc = sbf("supc", [128, 64], F32)
        sgtc = sbf("sgtc", [128, 64], F32)
        bvbc = sbf("bvbc", [128, 2048], BF16)
        vrow = sbf("vrow", [128, 128], F32)
        kmaxsq = sbf("kmaxsq", [1, 16], F32)
        qmaxsq = sbf("qmaxsq", [1, 16], F32)
        cwork = sbf("cwork", [1, 16], F32)
        cpair = sbf("cpair", [1, 8], F32)
        negCb = sbf("negCb", [128, 8], F32)
        nonce_sb = sbf("nonce_sb", [1, 16], I32)
        flag_sb = sbf("flag_sb", [1, 16], I32)
        kst_sb = sbf("kst_sb", [1, 16], F32)
        bnst = sbf("bnst", [128, 24], F32)
        ln_mv = sbf("ln_mv", [128, 2], F32)
        ln_r = sbf("ln_r", [128, 1], F32)
        ln_mr = sbf("ln_mr", [128, 1], F32)
        cst = sbf("cst", [128, 2], F32)
        E_ksq = [sbf(f"f_ksq_{b}", [128, 512], BF16) for b in range(2)]
        E_qsq = sbf("f_qsq", [128, 1024], BF16)
        assert _fo[0] <= SBUF_LIMIT, _fo[0]

        A_w1 = [sb(f"a_w1_{k}", [128, 2048], BF16, OFF_A + k * 4096) for k in range(16)]
        A_kt = [sb(f"a_kt_{j}", [128, 8192], BF16, OFF_A + j * 16384) for j in range(4)]
        B_nt = [[sb(f"b_nt{b}_{k}", [128, 512], BF16, OFF_B + b * 16384 + k * 1024)
                 for k in range(16)] for b in range(2)]
        B_ntq = [sb(f"b_ntq_{k}", [128, 1024], BF16, OFF_B + k * 2048) for k in range(16)]
        B_res = [sb(f"b_res_{t}", [128, 2048], F32, OFF_B + t * 8192) for t in range(4)]
        P4_li = [sb(f"p4_li_{b}", [128, 2048], BF16, OFF_B + b * 4096) for b in range(2)]
        C_xt = [sb(f"c_xt_{k}", [128, 512], BF16, OFF_C + k * 1024) for k in range(17)]
        C_sg = [sb(f"c_sg_{b}", [128, 512], BF16, OFF_C + 17408 + b * 1024) for b in range(2)]
        C_n3f = sb("c_n3f", [128, 2048], F32, OFF_C + 19456)
        sdpc_bb = sb("sdpc_bb", [128, 2048], BF16, OFF_C + 27648)
        bnst5 = sb("bnst5", [128, 96], F32, OFF_C + 31744)
        bo_row = sb("bo_row", [1, 2048], F32, OFF_E)
        sdpc_row = sb("sdpc_row", [1, 2048], F32, OFF_E + 8192)
        A_wd = [sb(f"a_wd_{b}", [128, 1024], BF16, OFF_A + 32768 + b * 2048) for b in range(2)]
        Z = []
        for zc in range(64):
            zoff = (OFF_E + zc * 1024) if zc < 32 else (OFF_D + (zc - 32) * 1024)
            Z.append(sb(f"z_{zc}", [128, 512], BF16, zoff))
        P5_out = [sb(f"p5_out_{b}", [128, 2048], F32, OFF_E + b * 8192) for b in range(2)]
        A_wb = [[sb(f"a_wb{b}_{k}", [128, 1024], BF16, OFF_A + b * 32768 + k * 2048)
                 for k in range(16)] for b in range(2)]
        C_ko = [sb(f"c_ko_{k}", [128, 512], BF16, OFF_C + k * 1024) for k in range(16)]
        C_vo = [sb(f"c_vo_{t}", [128, 2048], BF16, OFF_C + 16384 + t * 4096) for t in range(4)]
        C_ao = [sb(f"c_ao_{k}", [128, 1024], BF16, OFF_C + k * 2048) for k in range(16)]
        E_xh = [sb(f"e_xh_{b}", [128, 2048], F32, OFF_E + b * 8192) for b in range(2)]
        brow = sb("brow", [1, 2048], F32, OFF_E)
        E_nx = [sb(f"e_nx_{b}", [128, 2048], BF16, OFF_E + 16384 + b * 4096) for b in range(2)]
        E_dbg = sb("e_dbg", [128, 2048], F32, OFF_E + 24576)
        E_vp = [sb(f"e_vp_{b}", [128, 2048], BF16, OFF_E + b * 4096) for b in range(2)]
        E_pp = [sb(f"e_pp_{b}", [128, 2048], BF16, OFF_E + 8192 + b * 4096) for b in range(2)]
        E_acc = sb("e_acc", [128, 2048], F32, OFF_E + 16384)
        E_dinv = sb("e_dinv", [128, 2048], F32, OFF_E + 24576)
        E_li = [sb(f"e_li_{b}", [128, 2048], BF16, OFF_E + 24576 + b * 4096) for b in range(2)]
        D_qt = [sb(f"d_qt_{k}", [128, 1024], BF16, OFF_D + k * 2048) for k in range(16)]
        D_xo = [sb(f"d_xo_{b}", [128, 2048], F32, OFF_D + b * 8192) for b in range(2)]
        D_x2 = [sb(f"d_x2_{b}", [128, 2048], F32, OFF_D + 16384 + b * 8192) for b in range(2)]

        psf_h = e(nc.psum_tensor("psf", [128, 4096], F32))
        psf = psf_h.ap()
        psb_h = nc.virtual_psum_tensor("psb", [128, 8192], BF16)
        nc.back_virtual_tensor_with(psb_h, psf_h)
        psb = psb_h.ap()

        def bkf(k, n=1):
            return psf[:, k * 512:(k + n) * 512]

        def bkb(k, n=1):
            return psb[:, k * 1024:(k + n) * 1024]

        def sem(name):
            return e(nc.semaphore(name))
        s_gen = sem("s_gen"); s_vecp = sem("s_vecp"); s_vecd = sem("s_vecd")
        s_flag = sem("s_flag"); s_poll = sem("s_poll"); s_st = sem("s_st")
        s_lnr = sem("s_lnr"); s_ln = sem("s_ln"); s_mm = sem("s_mm")
        s_ev = sem("s_ev"); s_ev2 = sem("s_ev2"); s_sq = sem("s_sq")
        s_kst = sem("s_kst"); s_pe2 = sem("s_pe2"); s_exp = sem("s_exp")
        s_acc = sem("s_acc"); s_dve2 = sem("s_dve2"); s_ds = sem("s_ds")
        s_dbg = sem("s_dbg")
        s_chp = sem("s_chp"); s_chv = sem("s_chv"); s_cha = sem("s_cha")
        s_acc2 = sem("s_acc2"); s_ds2 = sem("s_ds2")

        def chain(eng, semh):
            last = [None]
            n = [0]
            def mark(inst):
                n[0] += 1
                inst.then_inc(semh, 1)
                last[0] = (semh, n[0])
                return inst
            def markpub(inst, pub, val):
                last[0] = (pub, val)
                return inst
            def wait():
                if last[0] is not None:
                    eng.wait_ge(last[0][0], last[0][1])
            return mark, markpub, wait

        class DS:
            def __init__(self, name, nsem):
                self.sems = [e(nc.semaphore(f"{name}{i}")) for i in range(nsem)]
                self.n = nsem
                self.j = 0
            def throttle(self, eng):
                j = self.j
                if j >= self.n:
                    eng.wait_ge(self.sems[j % self.n], 16 * (j // self.n))
            def issue(self, eng, inst):
                i = self.j % self.n
                inst.then_inc(self.sems[i], 16)
                self.j += 1
                return inst
            def wait(self, eng, j):
                eng.wait_ge(self.sems[j % self.n], 16 * (j // self.n + 1))
            def wait_all(self, eng, count):
                for i in range(min(self.n, count)):
                    cnt_i = (count - 1 - i) // self.n + 1
                    eng.wait_ge(self.sems[i], 16 * cnt_i)
            def wait_range(self, eng, j0, n):
                seen = {}
                for j in range(j0, j0 + n):
                    seen[j % self.n] = j // self.n + 1
                for i, cnt in seen.items():
                    eng.wait_ge(self.sems[i], 16 * cnt)

        ds_vl = DS("dvl", 2)
        ds_xh = DS("dxh", 2)
        ds_nxw = DS("dnw", 2)
        ds_cw = DS("dcw", 7)
        ds_bf = DS("dbf", 1)
        ds_wt = DS("dwt", 4)
        ds_nt = DS("dnt", 8)
        ds_ko = DS("dko", 4)
        ds_vv = DS("dvv", 4)
        ds_kss = DS("dks", 1)
        ds_ksf = DS("dkf", 1)
        ds_kt = DS("dkt", 2)
        ds_cp = DS("dcp", 2)
        ds_vp = DS("dvp", 2)
        ds_dbo = DS("ddb", 1)
        ds_dbi = DS("ddi", 2)
        ds_out = DS("dou", 2)
        ds_x2 = DS("dx2", 2)
        ds_wb = DS("dwb", 4)

        block = e(nc.Block())

        def shsl(t, sideval, sub):
            per = int(np.prod(t.shape[1:]))
            return dataclasses.replace(sub, offset=sideval * per + sub.offset)

        def dynoff(ap, extra):
            return dataclasses.replace(ap, offset=ap.offset + extra)

        @block.sync
        def _(eng):
            my = nc.partition_id([ET.SP])
            side = my % 2
            koff = side * HALF
            voff = side * (HALF * D)

            ds_vl.issue(eng, eng.dma_start(out=nonce_sb[:], in_=nonce[:]))
            for i, (nm, rows) in enumerate(VECS):
                if i > 0:
                    eng.wait_ge(s_vecp, i)
                ds_vl.throttle(eng)
                ds_vl.issue(eng, eng.dma_start(
                    out=vrow[0:rows, :],
                    in_=vec_in[nm].rearrange("o (r c) -> (o r) c", c=128)))

            def _nx_write(j):
                eng.wait_ge(s_ln, j + 1)
                dst = (nxh_d[j * 128:(j + 1) * 128, :] if j < NTH
                       else nxo_d[(j - NTH) * 128:(j - NTH + 1) * 128, :])
                ds_nxw.throttle(eng)
                ds_nxw.issue(eng, eng.dma_start(out=dst, in_=E_nx[j % 2]))

            def _dbg_write(j):
                if STAGE != 1 or j < NTH:
                    return
                jo = j - NTH
                eng.wait_ge(s_dbg, jo + 1)
                ds_dbo.throttle(eng)
                ds_dbo.issue(eng, eng.dma_start(
                    out=dbg[jo * 128:(jo + 1) * 128, :], in_=E_dbg[:]))

            for i in range(NT_ALL):
                if i >= 2:
                    eng.wait_ge(s_ln, i - 1)
                srct = (x_half[i * 128:(i + 1) * 128, :] if i < NTH
                        else x_own[(i - NTH) * 128:(i - NTH + 1) * 128, :])
                ds_xh.throttle(eng)
                ds_xh.issue(eng, eng.dma_start(out=E_xh[i % 2], in_=srct))
                if i >= 2:
                    _nx_write(i - 2)
                    _dbg_write(i - 2)
            _nx_write(NT_ALL - 2)
            _dbg_write(NT_ALL - 2)
            _nx_write(NT_ALL - 1)
            _dbg_write(NT_ALL - 1)
            eng.wait_ge(s_ln, NT_ALL)
            ds_vl.throttle(eng)
            ds_vl.issue(eng, eng.dma_start(out=brow[:], in_=vec_in["bv"][:, :]))

            if STAGE == 1:
                ds_dbo.wait_all(eng, NTO)
                return

            eng.wait_ge(s_flag, 1)
            cast_wait(ds_cw, eng, "wk")
            for k in range(16):
                ds_wt.throttle(eng)
                ds_wt.issue(eng, eng.dma_start(
                    out=A_w1[k], in_=wbf["wk"][:, k * 128:(k + 1) * 128], transpose=True))
            for blk in range(NBK):
                bb = blk % 2
                if blk >= 2:
                    eng.wait_ge(s_mm, (blk - 2) * 16 + 16)
                ds_nxw.wait_all(eng, min(4 * (blk + 1), NTH))
                for k in range(16):
                    ds_nt.throttle(eng)
                    ds_nt.issue(eng, eng.dma_start(
                        out=B_nt[bb][k],
                        in_=nxh_d[blk * 512:(blk + 1) * 512, k * 128:(k + 1) * 128],
                        transpose=True))
                for dout in range(16):
                    g = blk * 16 + dout
                    eng.wait_ge(s_sq, g + 1)
                    ds_ko.throttle(eng)
                    ds_ko.issue(eng, eng.dma_start(
                        out=kloc_d[dout][:, blk * 512:(blk + 1) * 512],
                        in_=C_ko[dout]))
            eng.wait_ge(s_mm, MM_K)
            cast_wait(ds_cw, eng, "wv")
            for k in range(16):
                ds_wt.throttle(eng)
                ds_wt.issue(eng, eng.dma_start(
                    out=A_w1[k], in_=wbf["wv"][:, k * 128:(k + 1) * 128], transpose=True))
            for blk in range(NBK):
                bb = blk % 2
                if blk >= 2:
                    eng.wait_ge(s_mm, MM_K + (blk - 2) * 16 + 16)
                for k in range(16):
                    ds_nt.throttle(eng)
                    ds_nt.issue(eng, eng.dma_start(
                        out=B_nt[bb][k],
                        in_=nxh_d[blk * 512:(blk + 1) * 512, k * 128:(k + 1) * 128],
                        transpose=True))
                for ts in range(4):
                    eng.wait_ge(s_ev2, blk * 16 + ts * 4 + 4)
                    ds_vv.throttle(eng)
                    ds_vv.issue(eng, eng.dma_start(
                        out=vsh_d[blk * 512 + ts * 128:blk * 512 + (ts + 1) * 128, :]
                        if False else
                        vloc_d[blk * 512 + ts * 128:blk * 512 + (ts + 1) * 128, :],
                        in_=C_vo[ts]))
            eng.wait_ge(s_kst, 8)
            ds_kss.issue(eng, eng.dma_start(
                out=shsl(kstat_d, side, kstat_d[0][:, :]), in_=kmaxsq[:]))
            ds_ko.wait_all(eng, NBK * 16)
            ds_vv.wait_all(eng, NBK * 4)
            for p in range(4):
                ds_cp.throttle(eng)
                ds_cp.issue(eng, eng.dma_start(
                    out=dynoff(ksh_d[4 * p:4 * p + 4][:, :, 0:HALF], koff),
                    in_=kloc_d[4 * p:4 * p + 4]))
            for p in range(2):
                ds_cp.throttle(eng)
                ds_cp.issue(eng, eng.dma_start(
                    out=dynoff(vsh_d[p * 2048:(p + 1) * 2048, :], voff),
                    in_=vloc_d[p * 2048:(p + 1) * 2048, :]))
            eng.wait_ge(s_flag, 2)
            cast_wait(ds_cw, eng, "wq")
            eng.wait_ge(s_mm, MM_V)
            for k in range(16):
                ds_wt.throttle(eng)
                ds_wt.issue(eng, eng.dma_start(
                    out=A_w1[k], in_=wbf["wq"][:, k * 128:(k + 1) * 128], transpose=True))
            ds_nxw.wait_all(eng, NT_ALL)
            for k in range(16):
                ds_nt.throttle(eng)
                ds_nt.issue(eng, eng.dma_start(
                    out=B_ntq[k], in_=nxo_d[:, k * 128:(k + 1) * 128], transpose=True))

            if STAGE == 2:
                for k in range(16):
                    eng.wait_ge(s_dbg, k + 1)
                    ds_dbo.throttle(eng)
                    ds_dbo.issue(eng, eng.dma_start(
                        out=dbgk[k * 128:(k + 1) * 128, :], in_=E_dbg[:, 0:1024]))
                eng.wait_ge(s_flag, 3)
                ds_dbi.issue(eng, eng.dma_start(out=E_pp[0], in_=ksh_d[0][:, 0:2048]))
                ds_dbi.issue(eng, eng.dma_start(out=E_pp[1], in_=vsh_d[0:128, :]))
                for j, r0 in enumerate([0, 128, 256]):
                    eng.wait_ge(s_dbg, 17 + j)
                    ds_dbo.throttle(eng)
                    ds_dbo.issue(eng, eng.dma_start(out=dbg[r0:r0 + 128, :], in_=E_dbg[:]))
                ds_dbo.wait_all(eng, 19)
                return

            eng.wait_ge(s_flag, 3)
            for p in range(8):
                if p < 2:
                    eng.wait_ge(s_mm, MM_Q)
                else:
                    eng.wait_ge(s_pe2, PE2_DB(p - 1))
                for e2 in range(2):
                    ds_kt.throttle(eng)
                    ds_kt.issue(eng, eng.dma_start(
                        out=A_kt[(p % 2) * 2 + e2], in_=ksh_d[2 * p + e2]))
                for g in range(8):
                    vg = 8 * p + g
                    if vg >= 2:
                        q2, g2 = divmod(vg - 2, 8)
                        eng.wait_ge(s_mm, MM_PV(q2, 8 * g2 + 7))
                    ds_vp.throttle(eng)
                    ds_vp.issue(eng, eng.dma_start(
                        out=E_vp[vg % 2].rearrange("p (c d) -> p c d", c=8),
                        in_=vsh_d[g * 1024:(g + 1) * 1024,
                                  2 * p * 128:(2 * p + 2) * 128].rearrange(
                                      "(c p) d -> p c d", p=128)))

            if STAGE == 3:
                for k in range(16):
                    eng.wait_ge(s_dbg, k + 1)
                    ds_dbo.throttle(eng)
                    ds_dbo.issue(eng, eng.dma_start(
                        out=dbgk[k * 128:(k + 1) * 128, :], in_=B_res[0][:, 0:1024]))
                ds_dbo.wait_all(eng, 16)
                return

            eng.wait_ge(s_flag, 4)
            cast_wait(ds_cw, eng, "wo")
            eng.wait_ge(s_pe2, PE2_P3)
            ds_dbi.throttle(eng)
            ds_dbi.issue(eng, eng.dma_start(out=bo_row[:], in_=vec_in["bo"]))
            ds_dbi.throttle(eng)
            ds_dbi.issue(eng, eng.dma_start(out=sdpc_row[:], in_=vec_in["wsc_dp"]))
            for k in range(16):
                ds_wt.throttle(eng)
                ds_wt.issue(eng, eng.dma_start(
                    out=A_w1[k], in_=wbf["wo"][:, k * 128:(k + 1) * 128], transpose=True))

            def _x2w(j):
                eng.wait_ge(s_ln, LN_P4 + j + 1)
                ds_nxw.throttle(eng)
                ds_nxw.issue(eng, eng.dma_start(
                    out=x2_d[j * 128:(j + 1) * 128, :], in_=D_x2[j % 2]))

            for i in range(8):
                if i >= 2:
                    eng.wait_ge(s_ln, LN_P4 + i - 1)
                ds_x2.throttle(eng)
                ds_x2.issue(eng, eng.dma_start(
                    out=D_xo[i % 2], in_=x_own[i * 128:(i + 1) * 128, :]))
                if i >= 2:
                    _x2w(i - 2)
            _x2w(6)
            _x2w(7)

            if STAGE == 4:
                ds_nxw.wait_all(eng, NT_ALL + 8)
                ds_dbo.issue(eng, eng.dma_start(out=dbg[:, :], in_=x2_d[:, :]))
                ds_dbo.wait_all(eng, 1)
                return

            for hf in range(2):
                ds_nxw.wait_all(eng, NT_ALL + 8)
                if hf == 1:
                    eng.wait_ge(s_ln, LN5(0) + 4)
                for tt in range(4):
                    ds_x2.throttle(eng)
                    ds_x2.issue(eng, eng.dma_start(
                        out=B_res[tt],
                        in_=x2_d[hf * 512 + tt * 128:hf * 512 + (tt + 1) * 128, :]))
                for wi, wnm in enumerate(["wup", "wgate"]):
                    eng.wait_ge(s_flag, 5 + wi)
                    cast_wait(ds_cw, eng, wnm)
                    for B in range(8):
                        jbase = hf * 384 + wi * 128 + B * 16
                        if B >= 2:
                            eng.wait_ge(s_mm, MM5(hf) + 16 + wi * 64 + (B - 2) * 8 + 8)
                        elif hf == 1 or wi == 1:
                            prev = MM5(hf) + 16 + wi * 64 if wi == 1 else MM5(hf - 1) + 336
                            eng.wait_ge(s_mm, prev)
                        else:
                            eng.wait_ge(s_mm, MM_WO)
                        for k in range(16):
                            ds_wb.throttle(eng)
                            ds_wb.issue(eng, eng.dma_start(
                                out=A_wb[B % 2][k],
                                in_=wbf[wnm][B * 1024:(B + 1) * 1024,
                                             k * 128:(k + 1) * 128], transpose=True))
                eng.wait_ge(s_flag, 7)
                cast_wait(ds_cw, eng, "wdown")
                for dh2 in range(2):
                    for k2 in range(64):
                        kk = dh2 * 64 + k2
                        if kk >= 2:
                            eng.wait_ge(s_mm, MM5(hf) + 208 + kk - 1)
                        else:
                            eng.wait_ge(s_mm, MM5(hf) + 144)
                        ds_wb.throttle(eng)
                        ds_wb.issue(eng, eng.dma_start(
                            out=A_wd[k2 % 2],
                            in_=wbf["wdown"][dh2 * 1024:(dh2 + 1) * 1024,
                                             k2 * 128:(k2 + 1) * 128], transpose=True))
                for tt in range(4):
                    eng.wait_ge(s_ln, LN5(hf) + tt + 1)
                    ds_out.throttle(eng)
                    ds_out.issue(eng, eng.dma_start(
                        out=out[hf * 512 + tt * 128:hf * 512 + (tt + 1) * 128, :],
                        in_=P5_out[tt % 2]))
            ds_out.wait_all(eng, 8)

        @block.gpsimd
        def _(eng):
            my = nc.partition_id([ET.Pool])
            side = my % 2
            bud = 1 - side
            pmark, pmarkpub, pwait = chain(eng, s_chp)

            for op in [lambda: eng.memset(identf[:], 0.0),
                       lambda: eng.affine_select(out=identf[:], in_=identf[:],
                                                 compare_op=ALU.not_equal, fill=1.0,
                                                 base=0, pattern=[[-1, 128]],
                                                 channel_multiplier=1),
                       lambda: eng.iota(iota_i[:], pattern=[[0, 128]], base=0,
                                        channel_multiplier=1),
                       lambda: eng.iota(iota_j[:], pattern=[[1, 128]], base=0,
                                        channel_multiplier=0),
                       lambda: eng.memset(ones128[:], 1.0),
                       lambda: eng.memset(ones128f[:], 1.0),
                       lambda: eng.memset(onesrowf[:], 1.0),
                       lambda: eng.memset(cst[:, 0:1], EPS),
                       lambda: eng.memset(kmaxsq[:], 0.0)]:
                pmark(op())
                pwait()
            eng.memset(cst[:, 1:2], 0.0).then_inc(s_gen, 1)

            if STAGE < 2:
                return

            ds_vl.wait(eng, 0)

            def dynoff_p(ap, extra):
                return dataclasses.replace(ap, offset=ap.offset + extra)

            for nm in CAST_ORDER:
                rows, cols = w_in[nm].shape
                l2 = rows * cols // 2
                half_off = side * l2
                wf_o = wbf[nm].rearrange("(o a) b -> o (a b)", o=1)
                wf_i = w_in[nm].rearrange("(o a) b -> o (a b)", o=1)
                ds_cw.throttle(eng)
                ds_cw.issue(eng, eng.dma_start(
                    out=dynoff_p(wf_o[0:1, 0:l2], half_off),
                    in_=dynoff_p(wf_i[0:1, 0:l2], half_off)))
            FLAG_OF = {"wv": FL_WKV, "wq": FL_WQ, "wdown": FL_WO}
            for nm in CAST_ORDER:
                if nm not in FLAG_OF:
                    continue
                cast_wait(ds_cw, eng, nm)
                if nm == "wv":
                    cast_wait(ds_cw, eng, "wk")
                if nm == "wdown":
                    for other in ("wo", "wup", "wgate"):
                        cast_wait(ds_cw, eng, other)
                ds_bf.throttle(eng)
                ds_bf.issue(eng, eng.dma_start(
                    out=shsl(flags_d, side, flags_d[0][FLAG_OF[nm]:FLAG_OF[nm] + 1, :]),
                    in_=nonce_sb[:]))

            with eng.register("want") as want, eng.register("go") as go, \
                 eng.register("cur") as cur, eng.register("itr") as itr:
                eng.reg_load(want, nonce_sb[0:1, 0:1])
                eng.reg_mov(itr, 0)

                def poll(fl):
                    eng.reg_mov(go, 1)
                    with eng.While(go):
                        eng.reg_alu(itr, itr, 16, ALU.add)
                        eng.dma_start(out=flag_sb[:],
                                      in_=shsl(flags_d, bud, flags_d[0][fl:fl + 1, :])
                                      ).then_inc(s_poll, 16)
                        eng.wait_ge(s_poll, itr)
                        eng.reg_load(cur, flag_sb[0:1, 0:1])
                        eng.reg_alu(go, cur, want, ALU.not_equal)
                    eng.reg_alu(go, go, 0, ALU.add).then_inc(s_flag, 1)

                poll(FL_WKV)
                poll(FL_WQ)
                ds_cp.wait_all(eng, 6)
                ds_kss.wait(eng, 0)
                ds_bf.throttle(eng)
                ds_bf.issue(eng, eng.dma_start(
                    out=shsl(flags_d, side, flags_d[0][FL_KV:FL_KV + 1, :]),
                    in_=nonce_sb[:]))
                poll(FL_KV)
                ds_ksf.issue(eng, eng.dma_start(
                    out=kst_sb[:], in_=shsl(kstat_d, bud, kstat_d[0][:, :])))
                if STAGE >= 3:
                    for p3 in range(8):
                        for C3 in range(64):
                            eng.wait_ge(s_exp, S_EXP(p3, C3))
                            if C3 >= 1:
                                eng.wait_ge(s_acc2, S_ACC(p3, C3 - 1))
                            pwait()
                            if C3 == 0:
                                pmarkpub(eng.tensor_copy(
                                    E_acc[:, 1024:2048], E_pp[0][:, 1024:2048]
                                    ).then_inc(s_acc2, 1), s_acc2, S_ACC(p3, C3))
                            else:
                                pmarkpub(eng.tensor_tensor(
                                    E_acc[:, 1024:2048], E_acc[:, 1024:2048],
                                    E_pp[C3 % 2][:, 1024:2048],
                                    ALU.add).then_inc(s_acc2, 1), s_acc2, S_ACC(p3, C3))
                        eng.wait_ge(s_pe2, PE2_DB(p3))
                if STAGE >= 4:
                    poll(FL_WO)
                if STAGE >= 5:
                    for _x5 in range(3):
                        eng.reg_alu(go, go, 0, ALU.add).then_inc(s_flag, 1)

        @block.tensor
        def _(eng):
            eng.wait_ge(s_gen, 1)
            for i, (nm, rows) in enumerate(VECS):
                ds_vl.wait(eng, i + 1)
                if i > 0:
                    eng.wait_ge(s_vecd, i)
                eng.transpose(psf[0:128, 6 * 512:6 * 512 + rows], vrow[0:rows, :],
                              identf[0:rows, 0:rows]).then_inc(s_vecp, 1)
            ds_vl.wait(eng, NV + 1)
            eng.wait_ge(s_vecd, NV)
            for c in range(16):
                inst = eng.matmul(psf[:, c * 128:(c + 1) * 128],
                                  onesrowf[:, :], brow[0:1, c * 128:(c + 1) * 128],
                                  start=(c % 4 == 0), stop=(c % 4 == 3),
                                  skip_group_check=True)
            inst.then_inc(s_vecp, 1)

            if STAGE < 2:
                return

            ds_wt.wait_all(eng, 16)
            eng.wait_ge(s_vecd, NV + 2)
            for blk in range(NBK):
                bb = blk % 2
                ds_nt.wait_all(eng, 16 * (blk + 1))
                if blk >= 1:
                    eng.wait_ge(s_kst, blk)
                for dout in range(16):
                    g = blk * 16 + dout
                    if g >= 4:
                        eng.wait_ge(s_ev, g - 3)
                    for k in range(16):
                        inst = eng.matmul(bkf(dout % 4),
                                          A_w1[k][:, dout * 128:(dout + 1) * 128],
                                          B_nt[bb][k], start=(k == 0), stop=(k == 15))
                    inst.then_inc(s_mm, 1)
                    eng.wait_ge(s_sq, g + 1)
                    inst = eng.matmul(psf[0:1, 4 * 512:5 * 512], ones128[:, 0:1],
                                      E_ksq[dout % 2],
                                      start=(dout == 0), stop=(dout == 15))
                inst.then_inc(s_pe2, 1)
            ds_wt.wait_all(eng, 32)
            for blk in range(NBK):
                bb = blk % 2
                ds_nt.wait_all(eng, 128 + 16 * (blk + 1))
                for ts in range(4):
                    for ns in range(4):
                        g = blk * 16 + ts * 4 + ns
                        if g >= 4:
                            eng.wait_ge(s_ev2, g - 3)
                        for k in range(16):
                            inst = eng.matmul(bkf(g % 4),
                                              B_nt[bb][k][:, ts * 128:(ts + 1) * 128],
                                              A_w1[k][:, ns * 512:(ns + 1) * 512],
                                              start=(k == 0), stop=(k == 15))
                        inst.then_inc(s_mm, 1)
            ds_wt.wait_all(eng, 48)
            ds_nt.wait_all(eng, 256 + 16)
            for dout in range(16):
                if dout >= 2:
                    eng.wait_ge(s_ev, EV_K + dout - 1)
                b0 = (dout % 2) * 2
                for k in range(16):
                    eng.matmul(bkf(b0), A_w1[k][:, dout * 128:(dout + 1) * 128],
                               B_ntq[k][:, 0:512], start=(k == 0), stop=(k == 15))
                for k in range(16):
                    inst = eng.matmul(bkf(b0 + 1), A_w1[k][:, dout * 128:(dout + 1) * 128],
                                      B_ntq[k][:, 512:1024], start=(k == 0), stop=(k == 15))
                inst.then_inc(s_mm, 1)
            for h in range(16):
                eng.wait_ge(s_sq, EV_K + h + 1)
                if h >= 1:
                    eng.wait_ge(s_kst, 8 + h)
                eng.matmul(psf[0:1, 4 * 512:5 * 512], ones128[:, 0:1], E_qsq[:, 0:512],
                           start=True, stop=True)
                eng.matmul(psf[0:1, 5 * 512:6 * 512], ones128[:, 0:1], E_qsq[:, 512:1024],
                           start=True, stop=True).then_inc(s_pe2, 1)
            eng.wait_ge(s_dve2, 1)
            eng.matmul(psf[:, 6 * 512:6 * 512 + 8], onesrowf[:, :], cpair[:, :],
                       start=True, stop=True).then_inc(s_pe2, 1)

            if STAGE < 3:
                return

            def pv_group(p, Cp):
                vg = 8 * p + Cp // 8
                if Cp % 8 == 0:
                    ds_vp.wait(eng, vg)
                st, sp = (Cp == 0), (Cp == 63)
                vpb = E_vp[vg % 2]
                sl = (Cp % 8) * 256
                ppb = E_pp[Cp % 2]
                eng.matmul(bkf(4), vpb[:, sl:sl + 128], ppb[:, 0:512], start=st, stop=sp)
                eng.matmul(bkf(5), vpb[:, sl:sl + 128], ppb[:, 512:1024],
                           start=st, stop=sp)
                eng.matmul(bkf(6), vpb[:, sl + 128:sl + 256], ppb[:, 1024:1536],
                           start=st, stop=sp)
                eng.matmul(bkf(7), vpb[:, sl + 128:sl + 256], ppb[:, 1536:2048],
                           start=st, stop=sp).then_inc(s_mm, 1)

            for p in range(8):
                kta = A_kt[(p % 2) * 2]
                ktb = A_kt[(p % 2) * 2 + 1]
                for C in range(64):
                    eng.wait_ge(s_exp, S_EXP(p, C) - 1)
                    if C == 0:
                        eng.wait_ge(s_dve2, S_DVE2_SW(p) - 1)
                        ds_kt.wait(eng, 2 * p)
                        ds_kt.wait(eng, 2 * p + 1)
                    eng.matmul(bkf(0), kta[:, C * 128:(C + 1) * 128], D_qt[2 * p][:, 0:512],
                               start=True, stop=True)
                    eng.matmul(bkf(1), kta[:, C * 128:(C + 1) * 128],
                               D_qt[2 * p][:, 512:1024], start=True, stop=True)
                    eng.matmul(bkf(2), ktb[:, C * 128:(C + 1) * 128],
                               D_qt[2 * p + 1][:, 0:512], start=True, stop=True)
                    eng.matmul(bkf(3), ktb[:, C * 128:(C + 1) * 128],
                               D_qt[2 * p + 1][:, 512:1024],
                               start=True, stop=True).then_inc(s_pe2, 1)
                    if C >= 1:
                        pv_group(p, C - 1)
                eng.wait_ge(s_exp, S_EXP(p, 63))
                pv_group(p, 63)
                eng.wait_ge(s_acc, S_ACC(p, 63))
                eng.wait_ge(s_acc2, S_ACC(p, 63))
                eng.matmul(bkf(0), ones128f[:, :], E_acc[:, 0:512], start=True, stop=True)
                eng.matmul(bkf(1), ones128f[:, :], E_acc[:, 512:1024], start=True, stop=True)
                eng.matmul(bkf(2), ones128f[:, :], E_acc[:, 1024:1536], start=True, stop=True)
                eng.matmul(bkf(3), ones128f[:, :], E_acc[:, 1536:2048],
                           start=True, stop=True).then_inc(s_pe2, 1)

            if STAGE < 4:
                return

            ds_wt.wait_all(eng, 64)
            eng.wait_ge(s_dve2, 10)
            ds_dbi.wait(eng, 0)
            for i in range(8):
                if i >= 2:
                    eng.wait_ge(s_dve2, 10 + i - 1)
                b0 = (i % 2) * 4
                for dseg in range(4):
                    for k in range(16):
                        inst = eng.matmul(bkf(b0 + dseg),
                                          C_ao[k][:, i * 128:(i + 1) * 128],
                                          A_w1[k][:, dseg * 512:(dseg + 1) * 512],
                                          start=(k == 0), stop=False,
                                          skip_group_check=True)
                    if dseg == 3:
                        inst.then_inc(s_mm, 1)
                    inst = eng.matmul(bkf(b0 + dseg), onesrowf[:, :],
                                      bo_row[0:1, dseg * 512:(dseg + 1) * 512],
                                      start=False, stop=True, skip_group_check=True)
                    if dseg == 3:
                        inst.then_inc(s_pe2, 1)

            if STAGE < 5:
                return

            eng.wait_ge(s_dve2, 18)
            ds_dbi.wait(eng, 1)
            for cc in range(16):
                inst = eng.matmul(psf[:, cc * 128:(cc + 1) * 128], onesrowf[:, :],
                                  sdpc_row[0:1, cc * 128:(cc + 1) * 128],
                                  start=(cc % 4 == 0), stop=(cc % 4 == 3),
                                  skip_group_check=True)
            inst.then_inc(s_pe2, 1)

            for hf in range(2):
                for tt in range(4):
                    eng.wait_ge(s_ds, DS5(hf) + tt + 1)
                    if tt >= 1:
                        eng.wait_ge(s_ev, EV5(hf) + tt)
                    elif hf == 1:
                        eng.wait_ge(s_dve2, DVE2_L4(0, 3))
                    dst = bkf(4, 4)
                    for cc in range(16):
                        inst = eng.matmul(dst[:, cc * 128:(cc + 1) * 128],
                                          C_n3f[:, cc * 128:(cc + 1) * 128],
                                          identf[:, :], is_transpose=True,
                                          start=(cc % 4 == 0), stop=(cc % 4 == 3),
                                          skip_group_check=True)
                    inst.then_inc(s_pe2, 1)
                eng.wait_ge(s_dve2, DVE2_H16(hf))
                _, sl_map = butterfly_plan(16)
                for cc in range(16):
                    if cc >= 4:
                        eng.wait_ge(s_ev, EV5(hf) + 4 + cc - 3)
                    eng.matmul(bkf(cc % 4), h128[:, :], C_xt[sl_map[cc]],
                               start=True, stop=True).then_inc(s_mm, 1)
                for wi in range(2):
                    for B in range(8):
                        ds_wb.wait_range(eng, hf * 384 + wi * 128 + B * 16, 16)
                        for c8 in range(8):
                            g5 = B * 8 + c8
                            if wi == 0 and B == 0 and c8 < 4:
                                eng.wait_ge(s_ev, EV5(hf) + 20)
                            else:
                                gg = wi * 64 + g5
                                if gg >= 4:
                                    eng.wait_ge(s_ev, EV5(hf) + 20 + gg - 3)
                            for k in range(16):
                                inst = eng.matmul(bkf(c8 % 4),
                                                  A_wb[B % 2][k][:, c8 * 128:(c8 + 1) * 128],
                                                  C_xt[sl_map[k]],
                                                  start=(k == 0), stop=(k == 15))
                            inst.then_inc(s_mm, 1)
                eng.wait_ge(s_dve2, DVE2_H64(hf))
                _, slz = butterfly_plan(64)

                def zph(sl5):
                    return Z[sl5] if sl5 < 64 else C_xt[16]

                for cc in range(64):
                    if cc >= 4:
                        eng.wait_ge(s_ev, EV5(hf) + 148 + cc - 3)
                    eng.matmul(bkf(cc % 4), h128[:, :], zph(slz[cc]),
                               start=True, stop=True).then_inc(s_mm, 1)
                for dh2 in range(2):
                    for k2 in range(64):
                        kk = dh2 * 64 + k2
                        ds_wb.wait(eng, hf * 384 + 256 + kk)
                        if k2 == 0:
                            eng.wait_ge(s_ev, EV5(hf) + 148 + 64)
                            if dh2 == 1:
                                eng.wait_ge(s_ev, EV5(hf) + 212 + 4)
                        for tt in range(4):
                            for ds2 in range(2):
                                inst = eng.matmul(
                                    bkf(tt * 2 + ds2),
                                    zph(slz[k2])[:, tt * 128:(tt + 1) * 128],
                                    A_wd[k2 % 2][:, ds2 * 512:(ds2 + 1) * 512],
                                    start=(k2 == 0), stop=(k2 == 63),
                                    skip_group_check=True)
                        inst.then_inc(s_mm, 1)

        @block.scalar
        def _(eng):
            amark, amarkpub, await_ = chain(eng, s_cha)
            for i in range(NT_ALL):
                eng.wait_ge(s_st, i + 1)
                amark(eng.activation(ln_mv[:, 1:2], ln_mv[:, 1:2], AF.Ln,
                                     bias=cst[:, 0:1], scale=1.0))
                await_()
                eng.activation(ln_r[:], ln_mv[:, 1:2], AF.Exp, bias=cst[:, 1:2],
                               scale=-0.5).then_inc(s_lnr, 1)

            if STAGE < 2:
                return

            for blk in range(NBK):
                for dout in range(16):
                    g = blk * 16 + dout
                    eng.wait_ge(s_mm, g + 1)
                    if blk >= 1:
                        ds_ko.wait(eng, (blk - 1) * 16 + dout)
                        eng.wait_ge(s_sq, g - 16 + 1)
                    eng.activation(C_ko[dout], bkf(dout % 4), AF.Identity,
                                   bias=bkc[:, dout:dout + 1], scale=1.0).then_inc(s_ev, 1)
            for dout in range(16):
                eng.wait_ge(s_mm, MM_V + dout + 1)
                amark(eng.activation(D_qt[dout][:, 0:512], bkf((dout % 2) * 2), AF.Identity,
                                     bias=bqc[:, dout:dout + 1], scale=RSQ_DH))
                await_()
                eng.activation(D_qt[dout][:, 512:1024], bkf((dout % 2) * 2 + 1), AF.Identity,
                               bias=bqc[:, dout:dout + 1], scale=RSQ_DH).then_inc(s_ev, 1)
            eng.wait_ge(s_kst, 24)
            eng.wait_ge(s_ds, 1)
            amark(eng.activation(cwork[:], cwork[:], AF.Ln, bias=cst[0:1, 1:2], scale=1.0))
            await_()
            eng.activation(cwork[:], cwork[:], AF.Exp, bias=cst[0:1, 1:2],
                           scale=0.5).then_inc(s_lnr, 1)

            if STAGE < 3:
                return

            eng.wait_ge(s_dve2, 2)
            for p in range(8):
                for C in range(64):
                    eng.wait_ge(s_pe2, PE2_S(p, C))
                    if C >= 2:
                        eng.wait_ge(s_mm, MM_PV(p, C - 2))
                        eng.wait_ge(s_acc, S_ACC(p, C - 2))
                        eng.wait_ge(s_acc2, S_ACC(p, C - 2))
                    eng.activation(E_pp[C % 2], psf[:, 0:2048], AF.Exp,
                                   bias=negCb[:, p:p + 1],
                                   scale=1.0).then_inc(s_exp, 1)

            if STAGE < 4:
                return

            for i in range(8):
                eng.wait_ge(s_st, ST_P4 + i + 1)
                amark(eng.activation(ln_mv[:, 1:2], ln_mv[:, 1:2], AF.Ln,
                                     bias=cst[:, 0:1], scale=1.0))
                await_()
                eng.activation(ln_r[:], ln_mv[:, 1:2], AF.Exp, bias=cst[:, 1:2],
                               scale=-0.5).then_inc(s_lnr, 1)

            if STAGE < 5:
                return

            eng.wait_ge(s_pe2, 554)
            eng.activation(sdpc_bb[:], bkf(0, 4), AF.Identity, bias=cst[:, 1:2],
                           scale=64.0).then_inc(s_ev, 1)
            for hf in range(2):
                _, sl_map = butterfly_plan(16)
                _, slz = butterfly_plan(64)
                for tt in range(4):
                    eng.wait_ge(s_st, ST5(hf) + tt + 1)
                    amark(eng.activation(ln_mv[:, 1:2], ln_mv[:, 1:2], AF.Ln,
                                         bias=cst[:, 0:1], scale=1.0))
                    await_()
                    eng.activation(ln_r[:], ln_mv[:, 1:2], AF.Exp, bias=cst[:, 1:2],
                                   scale=-0.5).then_inc(s_lnr, 1)
                    eng.wait_ge(s_pe2, PE25(hf) + tt + 1)
                    for cc in range(16):
                        src2 = bkf(4, 4)[:, cc * 128:(cc + 1) * 128]
                        inst = eng.activation(C_xt[cc][:, tt * 128:(tt + 1) * 128], src2,
                                              AF.Identity, bias=cst[:, 1:2],
                                              scale=su1c[:, cc:cc + 1])
                        if cc == 15:
                            inst.then_inc(s_ev, 1)
                        else:
                            amark(inst)
                            await_()
                for cc in range(16):
                    eng.wait_ge(s_mm, MM5(hf) + cc + 1)
                    eng.activation(C_xt[sl_map[cc]], bkf(cc % 4), AF.Identity,
                                   bias=cst[:, 1:2],
                                   scale=1.0).then_inc(s_ev, 1)
                if hf == 1:
                    ds_out.wait_range(eng, 0, 4)
                for wi in range(2):
                    for B in range(8):
                        for c8 in range(8):
                            g5 = B * 8 + c8
                            eng.wait_ge(s_mm, MM5(hf) + 16 + wi * 64 + g5 + 1)
                            cgl = B * 8 + c8
                            if wi == 0:
                                eng.activation(Z[cgl], bkf(c8 % 4), AF.Identity,
                                               bias=cst[:, 1:2],
                                               scale=supc[:, cgl:cgl + 1]
                                               ).then_inc(s_ev, 1)
                            else:
                                if g5 >= 2:
                                    eng.wait_ge(s_ev2, EV25(hf) + g5 - 1)
                                eng.activation(C_sg[c8 % 2], bkf(c8 % 4), AF.Silu,
                                               bias=cst[:, 1:2],
                                               scale=sgtc[:, cgl:cgl + 1]
                                               ).then_inc(s_ev, 1)
                for cc in range(64):
                    eng.wait_ge(s_mm, MM5(hf) + 144 + cc + 1)
                    zt = Z[slz[cc]] if slz[cc] < 64 else C_xt[16]
                    eng.activation(zt, bkf(cc % 4), AF.Identity, bias=cst[:, 1:2],
                                   scale=1.0).then_inc(s_ev, 1)
                for tt in range(4):
                    eng.wait_ge(s_ev2, EV25(hf) + 64 + tt + 1)
                    eng.activation(C_ao[tt], bkf(tt * 2, 2), AF.Identity,
                                   bias=cst[:, 1:2],
                                   scale=1.0).then_inc(s_ev, 1)
                for tt in range(4):
                    eng.wait_ge(s_st, ST5(hf) + 4 + tt + 1)
                    amark(eng.activation(ln_mv[:, 1:2], ln_mv[:, 1:2], AF.Ln,
                                         bias=cst[:, 0:1], scale=1.0))
                    await_()
                    eng.activation(ln_r[:], ln_mv[:, 1:2], AF.Exp, bias=cst[:, 1:2],
                                   scale=-0.5).then_inc(s_lnr, 1)

        @block.vector
        def _(eng):
            vmark, vmarkpub, vwait = chain(eng, s_chv)
            eng.wait_ge(s_gen, 1)
            vmark(eng.tensor_copy(identb[:], identf[:]))
            vwait()
            vmark(eng.tensor_tensor(hwork[:], iota_i[:], iota_j[:], ALU.bitwise_and))
            for sh in (4, 2, 1):
                vwait()
                vmark(eng.tensor_scalar(iota_j[:], hwork[:], sh, None,
                                        ALU.logical_shift_right))
                vwait()
                vmark(eng.tensor_tensor(hwork[:], hwork[:], iota_j[:], ALU.bitwise_xor))
            vwait()
            vmark(eng.tensor_scalar(hwork[:], hwork[:], 1, None, ALU.bitwise_and))
            vwait()
            vmark(eng.tensor_copy(hworkf[:], hwork[:]))
            vwait()
            vmark(eng.tensor_scalar(h128[:], hworkf[:], -2.0, 1.0, ALU.mult, ALU.add))

            for i, (nm, rows) in enumerate(VECS):
                eng.wait_ge(s_vecp, i + 1)
                srcp = psf[0:128, 6 * 512:6 * 512 + rows]
                if nm == "bq":
                    inst = eng.tensor_scalar(bqc[:], srcp, RSQ_DH, None, ALU.mult)
                elif nm == "bk":
                    inst = eng.tensor_copy(bkc[:], srcp)
                elif nm == "bv":
                    inst = eng.tensor_copy(hworkf[:, 0:16], srcp)
                elif nm == "bo":
                    inst = eng.tensor_copy(boc[:], srcp)
                elif nm == "su_ug":
                    inst = eng.tensor_scalar(su1c[:], srcp,
                                             float(1.0 / (64.0 * np.sqrt(2048.0))),
                                             None, ALU.mult)
                elif nm == "wsc_dp":
                    inst = eng.tensor_scalar(sdpc[:], srcp, 64.0, None, ALU.mult)
                elif nm == "su_dp":
                    inst = eng.tensor_copy(sdp_t[:], srcp)
                elif nm == "wsc_ug":
                    inst = eng.tensor_copy(wsc_t[:], srcp)
                inst.then_inc(s_vecd, 1)
            eng.wait_ge(s_vecd, NV)
            vmark(eng.tensor_tensor(supc[:], wsc_t[:, 0:64], sdp_t[:], ALU.mult))
            vwait()
            vmark(eng.tensor_scalar(supc[:], supc[:], float(1.0 / np.sqrt(8192.0)),
                                    None, ALU.mult))
            vwait()
            eng.tensor_scalar(sgtc[:], wsc_t[:, 64:128], 64.0, None,
                              ALU.mult).then_inc(s_vecd, 1)

            for i in range(NT_ALL):
                ds_xh.wait(eng, i)
                for sl in range(4):
                    vmark(eng.bn_stats(bnst[:, sl * 6:(sl + 1) * 6],
                                       E_xh[i % 2][:, sl * 512:(sl + 1) * 512]))
                vwait()
                eng.bn_aggr(ln_mv[:], bnst[:]).then_inc(s_st, 1)
                eng.wait_ge(s_lnr, i + 1)
                vmark(eng.tensor_scalar(ln_mr[:], ln_mv[:, 0:1], ln_r[:], -1.0,
                                        ALU.mult, ALU.mult))
                if i >= 2:
                    ds_nxw.wait(eng, i - 2)
                vwait()
                vmarkpub(eng.tensor_scalar(E_nx[i % 2], E_xh[i % 2], ln_r[:], ln_mr[:],
                                           ALU.mult, ALU.add).then_inc(s_ln, 1),
                         s_ln, i + 1)
                if STAGE == 1 and i >= NTH:
                    if i > NTH:
                        ds_dbo.wait(eng, i - NTH - 1)
                    vwait()
                    vmarkpub(eng.tensor_copy(E_dbg[:], E_nx[i % 2]).then_inc(s_dbg, 1),
                             s_dbg, i - NTH + 1)

            if STAGE < 2:
                return

            eng.wait_ge(s_vecp, NV + 1)
            eng.tensor_copy(bvbc[:], psf[:, 0:2048]).then_inc(s_vecd, 1)

            for blk in range(NBK):
                for dout in range(16):
                    g = blk * 16 + dout
                    eng.wait_ge(s_ev, g + 1)
                    if g >= 2:
                        eng.wait_ge(s_mm, g)
                    vwait()
                    vmarkpub(eng.tensor_tensor(E_ksq[g % 2], C_ko[dout], C_ko[dout],
                                               ALU.mult).then_inc(s_sq, 1), s_sq, g + 1)
                eng.wait_ge(s_pe2, blk + 1)
                vwait()
                vmark(eng.reduce_max(cwork[0:1, 0:1], psf[0:1, 4 * 512:5 * 512], axis=AX.X))
                vwait()
                if blk == 0:
                    vmarkpub(eng.tensor_copy(kmaxsq[0:1, 0:1],
                                             cwork[0:1, 0:1]).then_inc(s_kst, 1),
                             s_kst, blk + 1)
                else:
                    vmarkpub(eng.tensor_tensor(kmaxsq[0:1, 0:1], kmaxsq[0:1, 0:1],
                                               cwork[0:1, 0:1], ALU.max).then_inc(s_kst, 1),
                             s_kst, blk + 1)
                vwait()

            for blk in range(NBK):
                for ts in range(4):
                    for ns in range(4):
                        g = blk * 16 + ts * 4 + ns
                        eng.wait_ge(s_mm, MM_K + g + 1)
                        if blk >= 1:
                            ds_vv.wait(eng, (blk - 1) * 4 + ts)
                        eng.tensor_tensor(C_vo[ts][:, ns * 512:(ns + 1) * 512],
                                          bkf(g % 4), bvbc[:, ns * 512:(ns + 1) * 512],
                                          ALU.add).then_inc(s_ev2, 1)

            for h in range(16):
                eng.wait_ge(s_ev, EV_K + h + 1)
                if h >= 1:
                    eng.wait_ge(s_pe2, 8 + h)
                eng.tensor_tensor(E_qsq, D_qt[h], D_qt[h], ALU.mult).then_inc(s_sq, 1)
                eng.wait_ge(s_pe2, 8 + h + 1)
                vwait()
                vmark(eng.reduce_max(cwork[0:1, 0:1], psf[0:1, 4 * 512:6 * 512], axis=AX.X))
                vwait()
                vmarkpub(eng.tensor_copy(qmaxsq[0:1, h:h + 1],
                                         cwork[0:1, 0:1]).then_inc(s_kst, 1),
                         s_kst, 8 + h + 1)
                vwait()

            eng.wait_ge(s_flag, 3)
            ds_ksf.wait(eng, 0)
            vmark(eng.tensor_tensor(kst_sb[0:1, 0:1], kst_sb[0:1, 0:1],
                                    kmaxsq[0:1, 0:1], ALU.max))
            vwait()
            eng.tensor_scalar(cwork[:], qmaxsq[:], kst_sb[0:1, 0:1], None,
                              ALU.mult).then_inc(s_ds, 1)
            eng.wait_ge(s_lnr, NT_ALL + 1)
            vmark(eng.tensor_tensor(cpair[:], cwork[0:1, 0:16:2], cwork[0:1, 1:16:2],
                                    ALU.max))
            vwait()
            eng.tensor_scalar(cpair[:], cpair[:], -1.0, None,
                              ALU.mult).then_inc(s_dve2, 1)
            eng.wait_ge(s_pe2, PE2_P2)
            eng.tensor_copy(negCb[:], psf[:, 6 * 512:6 * 512 + 8]).then_inc(s_dve2, 1)

            if STAGE >= 3:
                for p in range(8):
                    for C in range(64):
                        eng.wait_ge(s_exp, S_EXP(p, C))
                        if C >= 1:
                            eng.wait_ge(s_acc, S_ACC(p, C - 1))
                        vwait()
                        if C == 0:
                            vmarkpub(eng.tensor_copy(E_acc[:, 0:1024],
                                                     E_pp[0][:, 0:1024]
                                                     ).then_inc(s_acc, 1),
                                     s_acc, S_ACC(p, C))
                        else:
                            vmarkpub(eng.tensor_tensor(E_acc[:, 0:1024], E_acc[:, 0:1024],
                                                       E_pp[C % 2][:, 0:1024],
                                                       ALU.add).then_inc(s_acc, 1),
                                     s_acc, S_ACC(p, C))
                    eng.wait_ge(s_pe2, PE2_DB(p))
                    vwait()
                    vmark(eng.reciprocal(E_dinv[:, 0:1024], bkf(0, 2)))
                    vwait()
                    vmark(eng.reciprocal(E_dinv[:, 1024:2048], bkf(2, 2)))
                    vwait()
                    vmark(eng.tensor_tensor(C_ao[2 * p], bkf(4, 2),
                                            E_dinv[:, 0:1024], ALU.mult))
                    vwait()
                    vmarkpub(eng.tensor_tensor(C_ao[2 * p + 1], bkf(6, 2),
                                               E_dinv[:, 1024:2048],
                                               ALU.mult).then_inc(s_dve2, 1),
                             s_dve2, S_DVE2_SW(p))

            if STAGE == 3:
                for k in range(16):
                    if k > 0:
                        ds_dbo.wait(eng, k - 1)
                    vwait()
                    vmarkpub(eng.tensor_copy(B_res[0][:, 0:1024],
                                             C_ao[k]).then_inc(s_dbg, 1), s_dbg, k + 1)
                return

            if STAGE == 2:
                for k in range(16):
                    if k > 0:
                        ds_dbo.wait(eng, k - 1)
                    vwait()
                    vmarkpub(eng.tensor_copy(E_dbg[:, 0:1024],
                                             D_qt[k]).then_inc(s_dbg, 1), s_dbg, k + 1)
                ds_dbo.wait(eng, 15)
                ds_dbi.wait(eng, 0)
                vwait()
                vmarkpub(eng.tensor_copy(E_dbg[:], E_pp[0]).then_inc(s_dbg, 1),
                         s_dbg, 17)
                ds_dbo.wait(eng, 16)
                ds_dbi.wait(eng, 1)
                vwait()
                vmarkpub(eng.tensor_copy(E_dbg[:], E_pp[1]).then_inc(s_dbg, 1),
                         s_dbg, 18)
                ds_dbo.wait(eng, 17)
                vwait()
                vmark(eng.memset(E_dbg[:], 0.0))
                vwait()
                vmark(eng.tensor_copy(E_dbg[:, 0:8], negCb[:]))
                vwait()
                vmark(eng.tensor_copy(E_dbg[0:1, 8:24], kmaxsq[:]))
                vwait()
                vmark(eng.tensor_copy(E_dbg[0:1, 24:40], qmaxsq[:]))
                vwait()
                eng.tensor_copy(E_dbg[0:1, 40:56], kst_sb[:]).then_inc(s_dbg, 1)
                return

            if STAGE < 4:
                return

            for i in range(8):
                eng.wait_ge(s_pe2, PE2_P3 + i + 1)
                src = bkf((i % 2) * 4, 4)
                for sl in range(4):
                    vmark(eng.bn_stats(bnst[:, sl * 6:(sl + 1) * 6],
                                       src[:, sl * 512:(sl + 1) * 512]))
                vwait()
                eng.bn_aggr(ln_mv[:], bnst[:]).then_inc(s_st, 1)
                eng.wait_ge(s_lnr, LNR_P4 + i + 1)
                vmark(eng.tensor_scalar(ln_mr[:], ln_mv[:, 0:1], ln_r[:], -1.0,
                                        ALU.mult, ALU.mult))
                ds_x2.wait(eng, i)
                if i >= 2:
                    ds_nxw.wait(eng, NT_ALL + i - 2)
                vwait()
                vmarkpub(eng.tensor_scalar(D_x2[i % 2], src, ln_r[:], ln_mr[:],
                                           ALU.mult, ALU.add).then_inc(s_dve2, 1),
                         s_dve2, 10 + i + 1)
                vwait()
                vmarkpub(eng.tensor_tensor(D_x2[i % 2], D_x2[i % 2], D_xo[i % 2],
                                           ALU.add).then_inc(s_ln, 1),
                         s_ln, LN_P4 + i + 1)

            if STAGE < 5:
                return

            for hf in range(2):
                ops16, sl_map = butterfly_plan(16)
                ops64, slz = butterfly_plan(64)

                def zph(sl5):
                    return Z[sl5] if sl5 < 64 else C_xt[16]

                for tt in range(4):
                    ds_x2.wait(eng, 8 + hf * 4 + tt)
                    for sl in range(4):
                        vmark(eng.bn_stats(bnst[:, sl * 6:(sl + 1) * 6],
                                           B_res[tt][:, sl * 512:(sl + 1) * 512]))
                    vwait()
                    eng.bn_aggr(ln_mv[:], bnst[:]).then_inc(s_st, 1)
                    eng.wait_ge(s_lnr, LNR5(hf) + tt + 1)
                    vmark(eng.tensor_scalar(ln_mr[:], ln_mv[:, 0:1], ln_r[:], -1.0,
                                            ALU.mult, ALU.mult))
                    if tt >= 1:
                        eng.wait_ge(s_pe2, PE25(hf) + tt)
                    elif hf == 1:
                        eng.wait_ge(s_pe2, PE25(0) + 4)
                    vwait()
                    vmarkpub(eng.tensor_scalar(C_n3f[:], B_res[tt], ln_r[:],
                                               ln_mr[:], ALU.mult,
                                               ALU.add).then_inc(s_ds, 1),
                             s_ds, DS5(hf) + tt + 1)
                eng.wait_ge(s_ev, EV5(hf) + 4)
                for j, (ba, bb, sp) in enumerate(ops16):
                    vwait()
                    vmark(eng.tensor_tensor(C_xt[sp], C_xt[ba], C_xt[bb], ALU.add))
                    vwait()
                    inst = eng.tensor_tensor(C_xt[bb], C_xt[ba], C_xt[bb],
                                             ALU.subtract)
                    if j == len(ops16) - 1:
                        vmarkpub(inst.then_inc(s_dve2, 1), s_dve2, DVE2_H16(hf))
                    else:
                        vmark(inst)
                for g5 in range(64):
                    eng.wait_ge(s_ev, EV5(hf) + 84 + g5 + 1)
                    vwait()
                    vmarkpub(eng.tensor_tensor(Z[g5], Z[g5], C_sg[g5 % 2],
                                               ALU.mult).then_inc(s_ev2, 1),
                             s_ev2, EV25(hf) + g5 + 1)
                for j, (ba, bb, sp) in enumerate(ops64):
                    vwait()
                    vmark(eng.tensor_tensor(zph(sp), zph(ba), zph(bb), ALU.add))
                    vwait()
                    inst = eng.tensor_tensor(zph(bb), zph(ba), zph(bb), ALU.subtract)
                    if j == len(ops64) - 1:
                        vmarkpub(inst.then_inc(s_dve2, 1), s_dve2, DVE2_H64(hf))
                    else:
                        vmark(inst)
                eng.wait_ge(s_ev, EV5(0))
                for dh2 in range(2):
                    eng.wait_ge(s_mm, MM5(hf) + 208 + dh2 * 64 + 64)
                    for tt in range(4):
                        vwait()
                        vmark(eng.tensor_tensor(bkf(tt * 2, 2), bkf(tt * 2, 2),
                                                sdpc_bb[:, dh2 * 1024:(dh2 + 1) * 1024],
                                                ALU.mult))
                        vwait()
                        vmark(eng.bn_stats(bnst5[:, (tt * 4 + dh2 * 2) * 6:
                                                 (tt * 4 + dh2 * 2 + 1) * 6],
                                           bkf(tt * 2)))
                        vwait()
                        inst = eng.bn_stats(bnst5[:, (tt * 4 + dh2 * 2 + 1) * 6:
                                                  (tt * 4 + dh2 * 2 + 2) * 6],
                                            bkf(tt * 2 + 1))
                        if dh2 == 0:
                            vmarkpub(inst.then_inc(s_ev2, 1), s_ev2,
                                     EV25(hf) + 64 + tt + 1)
                            continue
                        vmark(inst)
                        vwait()
                        eng.bn_aggr(ln_mv[:], bnst5[:, tt * 24:(tt + 1) * 24]
                                    ).then_inc(s_st, 1)
                        eng.wait_ge(s_lnr, LNR5(hf) + 4 + tt + 1)
                        vmark(eng.tensor_scalar(ln_mr[:], ln_mv[:, 0:1], ln_r[:],
                                                -1.0, ALU.mult, ALU.mult))
                        if hf == 1 or tt >= 2:
                            ds_out.wait(eng, hf * 4 + tt - 2)
                        vwait()
                        vmark(eng.tensor_scalar(P5_out[tt % 2][:, 0:1024], C_ao[tt],
                                                ln_r[:], ln_mr[:], ALU.mult, ALU.add))
                        vwait()
                        vmarkpub(eng.tensor_scalar(P5_out[tt % 2][:, 1024:2048],
                                                   bkf(tt * 2, 2), ln_r[:], ln_mr[:],
                                                   ALU.mult,
                                                   ALU.add).then_inc(s_dve2, 1),
                                 s_dve2, DVE2_L4(hf, tt))
                        vwait()
                        vmarkpub(eng.tensor_tensor(P5_out[tt % 2], P5_out[tt % 2],
                                                   B_res[tt], ALU.add
                                                   ).then_inc(s_ln, 1),
                                 s_ln, LN5(hf) + tt + 1)

    return nc


_CACHE = {}


def kernel(L0_attn_X, L0_attn_Wq, L0_attn_bq, L0_attn_Wk, L0_attn_bk, L0_attn_Wv,
           L0_attn_bv, L0_attn_Wo, L0_attn_bo, L0_ffn_SU_ug, L0_ffn_SU_dp,
           L0_ffn_Wscale_ug, L0_ffn_Wscale_dp, L0_ffn_W_up, L0_ffn_W_gate,
           L0_ffn_W_down):
    X = np.ascontiguousarray(np.asarray(L0_attn_X).reshape(S, D))
    if "nc" not in _CACHE:
        _CACHE["nc"] = build_program()
    nc = _CACHE["nc"]
    nonce_val = int(np.random.SeedSequence().entropy % (2**31 - 2)) + 1
    common = {
        "wq": np.ascontiguousarray(L0_attn_Wq), "wk": np.ascontiguousarray(L0_attn_Wk),
        "wv": np.ascontiguousarray(L0_attn_Wv), "wo": np.ascontiguousarray(L0_attn_Wo),
        "wup": np.ascontiguousarray(L0_ffn_W_up),
        "wgate": np.ascontiguousarray(L0_ffn_W_gate),
        "wdown": np.ascontiguousarray(L0_ffn_W_down),
        "bq": np.ascontiguousarray(np.asarray(L0_attn_bq).reshape(1, D)),
        "bk": np.ascontiguousarray(np.asarray(L0_attn_bk).reshape(1, D)),
        "bv": np.ascontiguousarray(np.asarray(L0_attn_bv).reshape(1, D)),
        "bo": np.ascontiguousarray(np.asarray(L0_attn_bo).reshape(1, D)),
        "su_ug": np.ascontiguousarray(np.asarray(L0_ffn_SU_ug).reshape(1, D)),
        "su_dp": np.ascontiguousarray(np.asarray(L0_ffn_SU_dp).reshape(1, INTER)),
        "wsc_ug": np.ascontiguousarray(np.asarray(L0_ffn_Wscale_ug).reshape(1, 2 * INTER)),
        "wsc_dp": np.ascontiguousarray(np.asarray(L0_ffn_Wscale_dp).reshape(1, D)),
        "nonce": np.full((1, 16), nonce_val, np.int32),
    }
    in_maps = []
    for c in range(NC_):
        side = c % 2
        m = dict(common)
        m["x_own"] = np.ascontiguousarray(X[c * TOK:(c + 1) * TOK, :])
        m["x_half"] = np.ascontiguousarray(X[side * HALF:(side + 1) * HALF, :])
        in_maps.append(m)
    res = run_bass_kernel_spmd(nc, in_maps, core_ids=list(range(NC_)))
    outp = np.concatenate([res.results[c]["out"] for c in range(NC_)], axis=0)
    return outp.reshape(1, S, D)


# version 03 (final)
# speedup vs baseline: 1.1267x; 1.0136x over previous
import dataclasses
import numpy as np
import concourse.bass as bass
import concourse.mybir as mybir
from concourse.bass_utils import run_bass_kernel_spmd
from contextlib import ExitStack

F32 = mybir.dt.float32
BF16 = mybir.dt.bfloat16
I32 = mybir.dt.int32
AF = mybir.ActivationFunctionType
ALU = mybir.AluOpType
ET = mybir.EngineType
AX = mybir.AxisListType

S, D, NH, DH, INTER = 8192, 2048, 16, 128, 8192
NC_ = 8
TOK = S // NC_
HALF = S // 2
DC = D // 128
IC = INTER // 128
EPS = 1e-5
RSQ_DH = float(1.0 / np.sqrt(128.0))
NBK = HALF // 512
NTH = HALF // 128
NTO = TOK // 128
NT_ALL = NTH + NTO

STAGE = 5

OFF_A = 0
OFF_B = 64 * 1024
OFF_C = 96 * 1024
OFF_E = 128 * 1024
OFF_D = 160 * 1024
OFF_F = 192 * 1024
SBUF_LIMIT = 207 * 1024

FL_WK, FL_WV, FL_WQ, FL_KV, FL_WO = range(5)

VECS = [("bq", 16), ("bk", 16), ("bv", 16), ("bo", 16),
        ("su_ug", 16), ("wsc_dp", 16), ("su_dp", 64), ("wsc_ug", 128)]
NV = len(VECS)
CAST_ORDER = ["wk", "wv", "wq", "wo", "wup", "wgate", "wdown"]
CAST_T0 = {nm: i for i, nm in enumerate(CAST_ORDER)}
CAST_NT = {nm: 1 for nm in CAST_ORDER}
def cast_wait(ds, eng, nm):
    ds.wait_range(eng, CAST_T0[nm], CAST_NT[nm])

PE2_P2 = 25
MM_K, MM_V, MM_Q = 128, 256, 272
EV_K, EV_Q = 128, 144
def PE2_S(p, C):
    return PE2_P2 + 65 * p + C + 1
def PE2_DB(p):
    return PE2_P2 + 65 * p + 65
PE2_P3 = PE2_P2 + 8 * 65
def MM_PV(p, C):
    return MM_Q + 64 * p + C + 1
MM_P3 = MM_Q + 512
def S_EXP(p, C):
    return 64 * p + C + 1
def S_ACC(p, C):
    return 64 * p + C + 1
def S_DVE2_SW(p):
    return 2 + p + 1
MM_WO = MM_P3 + 8
PE2_P4 = PE2_P3 + 8
def MM5(hf):
    return MM_WO + hf * 336
def PE25(hf):
    return 554 + hf * 4
def EV5(hf):
    return 145 + hf * 216
def EV25(hf):
    return EV2_V + hf * 68
EV2_V = 128
ST_P4, LNR_P4, LN_P4 = 40, 41, 40
def ST5(hf):
    return 48 + hf * 8
def LNR5(hf):
    return 49 + hf * 8
def LN5(hf):
    return 48 + hf * 4
def DS5(hf):
    return 1 + hf * 4
def DVE2_H16(hf):
    return 19 + hf * 6
def DVE2_H64(hf):
    return 20 + hf * 6
def DVE2_L4(hf, tt):
    return 20 + hf * 6 + tt + 1


def butterfly_plan(n):
    slots = list(range(n))
    spare = n
    ops = []
    h = 1
    while h < n:
        for j in range(0, n, 2 * h):
            for i in range(h):
                a, b = j + i, j + i + h
                ops.append((slots[a], slots[b], spare))
                slots[a], spare = spare, slots[a]
        h *= 2
    return ops, slots


def build_program():
    nc = bass.Bass("TRN2", target_bir_lowering=False, debug=False,
                   detect_race_conditions=False)

    x_own = nc.dram_tensor("x_own", [TOK, D], F32, kind="ExternalInput").ap()
    x_half = nc.dram_tensor("x_half", [HALF, D], F32, kind="ExternalInput").ap()
    w_in, wbf = {}, {}
    for nm, shp in [("wq", [D, D]), ("wk", [D, D]), ("wv", [D, D]), ("wo", [D, D]),
                    ("wup", [INTER, D]), ("wgate", [INTER, D]), ("wdown", [D, INTER])]:
        w_in[nm] = nc.dram_tensor(nm, shp, F32, kind="ExternalInput").ap()
        wbf[nm] = nc.dram_tensor("bf_" + nm, shp, BF16, kind="Internal",
                                 addr_space="Shared").ap()
    vec_in = {}
    for nm, rows in VECS:
        vec_in[nm] = nc.dram_tensor(nm, [1, rows * 128], F32, kind="ExternalInput").ap()
    nonce = nc.dram_tensor("nonce", [1, 16], I32, kind="ExternalInput").ap()
    out = nc.dram_tensor("out", [TOK, D], F32, kind="ExternalOutput").ap()
    dbg = nc.dram_tensor("dbg", [TOK, D], F32, kind="ExternalOutput").ap() if STAGE < 5 else None
    dbgk = (nc.dram_tensor("dbgk", [D, TOK], F32, kind="ExternalOutput").ap()
            if STAGE in (2, 3) else None)

    nxh_d = nc.dram_tensor("nxh_d", [HALF, D], BF16, kind="Internal").ap()
    nxo_d = nc.dram_tensor("nxo_d", [TOK, D], BF16, kind="Internal").ap()
    x2_d = nc.dram_tensor("x2_d", [TOK, D], F32, kind="Internal").ap()
    kloc_d = nc.dram_tensor("kloc_d", [DC, 128, HALF], BF16, kind="Internal").ap()
    vloc_d = nc.dram_tensor("vloc_d", [HALF, D], BF16, kind="Internal").ap()
    ksh_d = nc.dram_tensor("ksh_d", [DC, 128, S], BF16, kind="Internal",
                           addr_space="Shared").ap()
    vsh_d = nc.dram_tensor("vsh_d", [S, D], BF16, kind="Internal",
                           addr_space="Shared").ap()
    kstat_d = nc.dram_tensor("kstat_d", [2, 1, 16], F32, kind="Internal",
                             addr_space="Shared").ap()
    flags_d = nc.dram_tensor("flags_d", [2, 5, 16], I32, kind="Internal",
                             addr_space="Shared").ap()

    with ExitStack() as ctx:
        e = ctx.enter_context
        base = (nc.sbuf_base + 31) & ~31
        usable = nc.sbuf_top - base
        assert usable >= SBUF_LIMIT, f"usable sbuf {usable} < {SBUF_LIMIT}"

        def sb(name, shape, dtype, off):
            end = off + int(np.prod(shape[1:])) * mybir.dt.size(dtype)
            assert end <= usable, (name, end, usable)
            return nc.alloc_sbuf_tensor_at(name, shape, dtype, offset=base + off).ap()

        _fo = [OFF_F]
        def sbf(name, shape, dtype):
            t = sb(name, shape, dtype, _fo[0])
            _fo[0] += ((int(np.prod(shape[1:])) * mybir.dt.size(dtype) + 31) // 32) * 32
            return t

        identb = sbf("identb", [128, 128], BF16)
        identf = sbf("identf", [128, 128], F32)
        h128 = sbf("h128", [128, 128], BF16)
        ones128 = sbf("ones128", [128, 128], BF16)
        onesrowf = sbf("onesrowf", [1, 128], F32)
        ones128f = sbf("ones128f", [128, 128], F32)
        iota_i = sbf("iota_i", [128, 128], I32)
        iota_j = sbf("iota_j", [128, 128], I32)
        hwork = sbf("hwork", [128, 128], I32)
        hworkf = sbf("hworkf", [128, 128], F32)
        bqc = sbf("bqc", [128, 16], F32)
        bkc = sbf("bkc", [128, 16], F32)
        boc = sbf("boc", [128, 16], F32)
        su1c = sbf("su1c", [128, 16], F32)
        sdpc = sbf("sdpc", [128, 16], F32)
        sdp_t = sbf("sdp_t", [128, 64], F32)
        wsc_t = sbf("wsc_t", [128, 128], F32)
        supc = sbf("supc", [128, 64], F32)
        sgtc = sbf("sgtc", [128, 64], F32)
        bvbc = sbf("bvbc", [128, 2048], BF16)
        vrow = sbf("vrow", [128, 128], F32)
        kmaxsq = sbf("kmaxsq", [1, 16], F32)
        qmaxsq = sbf("qmaxsq", [1, 16], F32)
        cwork = sbf("cwork", [1, 16], F32)
        cpair = sbf("cpair", [1, 8], F32)
        negCb = sbf("negCb", [128, 8], F32)
        nonce_sb = sbf("nonce_sb", [1, 16], I32)
        flag_sb = sbf("flag_sb", [1, 16], I32)
        kst_sb = sbf("kst_sb", [1, 16], F32)
        bnst = sbf("bnst", [128, 24], F32)
        ln_mv = sbf("ln_mv", [128, 2], F32)
        ln_r = sbf("ln_r", [128, 1], F32)
        ln_mr = sbf("ln_mr", [128, 1], F32)
        cst = sbf("cst", [128, 2], F32)
        E_ksq = [sbf(f"f_ksq_{b}", [128, 512], BF16) for b in range(2)]
        E_qsq = sbf("f_qsq", [128, 1024], BF16)
        assert _fo[0] <= SBUF_LIMIT, _fo[0]

        A_w1 = [sb(f"a_w1_{k}", [128, 2048], BF16, OFF_A + k * 4096) for k in range(16)]
        A_kt = [sb(f"a_kt_{j}", [128, 8192], BF16, OFF_A + j * 16384) for j in range(4)]
        B_nt = [[sb(f"b_nt{b}_{k}", [128, 512], BF16, OFF_B + b * 16384 + k * 1024)
                 for k in range(16)] for b in range(2)]
        B_ntq = [sb(f"b_ntq_{k}", [128, 1024], BF16, OFF_B + k * 2048) for k in range(16)]
        B_res = [sb(f"b_res_{t}", [128, 2048], F32, OFF_B + t * 8192) for t in range(4)]
        P4_li = [sb(f"p4_li_{b}", [128, 2048], BF16, OFF_B + b * 4096) for b in range(2)]
        C_xt = [sb(f"c_xt_{k}", [128, 512], BF16, OFF_C + k * 1024) for k in range(17)]
        C_sg = [sb(f"c_sg_{b}", [128, 512], BF16, OFF_C + 17408 + b * 1024) for b in range(2)]
        C_n3f = sb("c_n3f", [128, 2048], F32, OFF_C + 19456)
        sdpc_bb = sb("sdpc_bb", [128, 2048], BF16, OFF_C + 27648)
        bnst5 = sb("bnst5", [128, 96], F32, OFF_C + 31744)
        bo_row = sb("bo_row", [1, 2048], F32, OFF_E)
        sdpc_row = sb("sdpc_row", [1, 2048], F32, OFF_E + 8192)
        A_wd = [sb(f"a_wd_{b}", [128, 1024], BF16, OFF_A + 32768 + b * 2048) for b in range(2)]
        Z = []
        for zc in range(64):
            zoff = (OFF_E + zc * 1024) if zc < 32 else (OFF_D + (zc - 32) * 1024)
            Z.append(sb(f"z_{zc}", [128, 512], BF16, zoff))
        P5_out = [sb(f"p5_out_{b}", [128, 2048], F32, OFF_E + b * 8192) for b in range(2)]
        A_wb = [[sb(f"a_wb{b}_{k}", [128, 1024], BF16, OFF_A + b * 32768 + k * 2048)
                 for k in range(16)] for b in range(2)]
        C_ko = [sb(f"c_ko_{k}", [128, 512], BF16, OFF_C + k * 1024) for k in range(16)]
        C_vo = [sb(f"c_vo_{t}", [128, 2048], BF16, OFF_C + 16384 + t * 4096) for t in range(4)]
        C_ao = [sb(f"c_ao_{k}", [128, 1024], BF16, OFF_C + k * 2048) for k in range(16)]
        E_xh = [sb(f"e_xh_{b}", [128, 2048], F32, OFF_E + b * 8192) for b in range(2)]
        brow = sb("brow", [1, 2048], F32, OFF_D)
        E_nx = [sb(f"e_nx_{b}", [128, 2048], BF16, OFF_E + 16384 + b * 4096) for b in range(2)]
        E_dbg = sb("e_dbg", [128, 2048], F32, OFF_E + 24576)
        E_vp = [sb(f"e_vp_{b}", [128, 2048], BF16, OFF_E + b * 4096) for b in range(2)]
        E_pp = [sb(f"e_pp_{b}", [128, 2048], BF16, OFF_E + 8192 + b * 4096) for b in range(2)]
        E_acc = sb("e_acc", [128, 2048], F32, OFF_E + 16384)
        E_dinv = sb("e_dinv", [128, 2048], F32, OFF_E + 24576)
        E_li = [sb(f"e_li_{b}", [128, 2048], BF16, OFF_E + 24576 + b * 4096) for b in range(2)]
        D_qt = [sb(f"d_qt_{k}", [128, 1024], BF16, OFF_D + k * 2048) for k in range(16)]
        D_xo = [sb(f"d_xo_{b}", [128, 2048], F32, OFF_D + b * 8192) for b in range(2)]
        D_x2 = [sb(f"d_x2_{b}", [128, 2048], F32, OFF_D + 16384 + b * 8192) for b in range(2)]

        psf_h = e(nc.psum_tensor("psf", [128, 4096], F32))
        psf = psf_h.ap()
        psb_h = nc.virtual_psum_tensor("psb", [128, 8192], BF16)
        nc.back_virtual_tensor_with(psb_h, psf_h)
        psb = psb_h.ap()

        def bkf(k, n=1):
            return psf[:, k * 512:(k + n) * 512]

        def bkb(k, n=1):
            return psb[:, k * 1024:(k + n) * 1024]

        def sem(name):
            return e(nc.semaphore(name))
        s_gen = sem("s_gen"); s_vecp = sem("s_vecp"); s_vecd = sem("s_vecd")
        s_flag = sem("s_flag"); s_poll = sem("s_poll"); s_st = sem("s_st")
        s_lnr = sem("s_lnr"); s_ln = sem("s_ln"); s_mm = sem("s_mm")
        s_ev = sem("s_ev"); s_ev2 = sem("s_ev2"); s_sq = sem("s_sq")
        s_kst = sem("s_kst"); s_pe2 = sem("s_pe2"); s_exp = sem("s_exp")
        s_acc = sem("s_acc"); s_dve2 = sem("s_dve2"); s_ds = sem("s_ds")
        s_dbg = sem("s_dbg")
        s_chp = sem("s_chp"); s_chv = sem("s_chv"); s_cha = sem("s_cha")
        s_acc2 = sem("s_acc2"); s_ds2 = sem("s_ds2")

        def chain(eng, semh):
            last = [None]
            n = [0]
            def mark(inst):
                n[0] += 1
                inst.then_inc(semh, 1)
                last[0] = (semh, n[0])
                return inst
            def markpub(inst, pub, val):
                last[0] = (pub, val)
                return inst
            def wait():
                if last[0] is not None:
                    eng.wait_ge(last[0][0], last[0][1])
            return mark, markpub, wait

        class DS:
            def __init__(self, name, nsem):
                self.sems = [e(nc.semaphore(f"{name}{i}")) for i in range(nsem)]
                self.n = nsem
                self.j = 0
            def throttle(self, eng):
                j = self.j
                if j >= self.n:
                    eng.wait_ge(self.sems[j % self.n], 16 * (j // self.n))
            def issue(self, eng, inst):
                i = self.j % self.n
                inst.then_inc(self.sems[i], 16)
                self.j += 1
                return inst
            def wait(self, eng, j):
                eng.wait_ge(self.sems[j % self.n], 16 * (j // self.n + 1))
            def wait_all(self, eng, count):
                for i in range(min(self.n, count)):
                    cnt_i = (count - 1 - i) // self.n + 1
                    eng.wait_ge(self.sems[i], 16 * cnt_i)
            def wait_range(self, eng, j0, n):
                seen = {}
                for j in range(j0, j0 + n):
                    seen[j % self.n] = j // self.n + 1
                for i, cnt in seen.items():
                    eng.wait_ge(self.sems[i], 16 * cnt)

        ds_vl = DS("dvl", 2)
        ds_xh = DS("dxh", 2)
        ds_nxw = DS("dnw", 2)
        ds_cw = DS("dcw", 7)
        ds_bf = DS("dbf", 1)
        ds_wt = DS("dwt", 4)
        ds_nt = DS("dnt", 8)
        ds_ko = DS("dko", 4)
        ds_vv = DS("dvv", 4)
        ds_kss = DS("dks", 1)
        ds_ksf = DS("dkf", 1)
        ds_kt = DS("dkt", 2)
        ds_cp = DS("dcp", 2)
        ds_vp = DS("dvp", 2)
        ds_dbo = DS("ddb", 1)
        ds_dbi = DS("ddi", 2)
        ds_out = DS("dou", 2)
        ds_x2 = DS("dx2", 2)
        ds_wb = DS("dwb", 4)

        block = e(nc.Block())

        def shsl(t, sideval, sub):
            per = int(np.prod(t.shape[1:]))
            return dataclasses.replace(sub, offset=sideval * per + sub.offset)

        def dynoff(ap, extra):
            return dataclasses.replace(ap, offset=ap.offset + extra)

        @block.sync
        def _(eng):
            my = nc.partition_id([ET.SP])
            side = my % 2
            koff = side * HALF
            voff = side * (HALF * D)

            ds_vl.issue(eng, eng.dma_start(out=nonce_sb[:], in_=nonce[:]))
            for i, (nm, rows) in enumerate(VECS):
                if i > 0:
                    eng.wait_ge(s_vecp, i)
                ds_vl.throttle(eng)
                ds_vl.issue(eng, eng.dma_start(
                    out=vrow[0:rows, :],
                    in_=vec_in[nm].rearrange("o (r c) -> (o r) c", c=128)))

            def _nx_write(j):
                eng.wait_ge(s_ln, j + 1)
                dst = (nxh_d[j * 128:(j + 1) * 128, :] if j < NTH
                       else nxo_d[(j - NTH) * 128:(j - NTH + 1) * 128, :])
                ds_nxw.throttle(eng)
                ds_nxw.issue(eng, eng.dma_start(out=dst, in_=E_nx[j % 2]))

            def _dbg_write(j):
                if STAGE != 1 or j < NTH:
                    return
                jo = j - NTH
                eng.wait_ge(s_dbg, jo + 1)
                ds_dbo.throttle(eng)
                ds_dbo.issue(eng, eng.dma_start(
                    out=dbg[jo * 128:(jo + 1) * 128, :], in_=E_dbg[:]))

            ds_vl.throttle(eng)
            ds_vl.issue(eng, eng.dma_start(out=brow[:], in_=vec_in["bv"][:, :]))
            for i in range(NT_ALL):
                if i >= 2:
                    eng.wait_ge(s_ln, i - 1)
                srct = (x_half[i * 128:(i + 1) * 128, :] if i < NTH
                        else x_own[(i - NTH) * 128:(i - NTH + 1) * 128, :])
                ds_xh.throttle(eng)
                ds_xh.issue(eng, eng.dma_start(out=E_xh[i % 2], in_=srct))
                if i >= 2:
                    _nx_write(i - 2)
                    _dbg_write(i - 2)
            _nx_write(NT_ALL - 2)
            _dbg_write(NT_ALL - 2)
            _nx_write(NT_ALL - 1)
            _dbg_write(NT_ALL - 1)

            if STAGE == 1:
                ds_dbo.wait_all(eng, NTO)
                return

            eng.wait_ge(s_flag, 1)
            cast_wait(ds_cw, eng, "wk")
            for k in range(16):
                ds_wt.throttle(eng)
                ds_wt.issue(eng, eng.dma_start(
                    out=A_w1[k], in_=wbf["wk"][:, k * 128:(k + 1) * 128], transpose=True))
            for blk in range(NBK):
                bb = blk % 2
                if blk >= 2:
                    eng.wait_ge(s_mm, (blk - 2) * 16 + 16)
                ds_nxw.wait_all(eng, min(4 * (blk + 1), NTH))
                for k in range(16):
                    ds_nt.throttle(eng)
                    ds_nt.issue(eng, eng.dma_start(
                        out=B_nt[bb][k],
                        in_=nxh_d[blk * 512:(blk + 1) * 512, k * 128:(k + 1) * 128],
                        transpose=True))
                for dout in range(16):
                    g = blk * 16 + dout
                    eng.wait_ge(s_sq, g + 1)
                    ds_ko.throttle(eng)
                    ds_ko.issue(eng, eng.dma_start(
                        out=kloc_d[dout][:, blk * 512:(blk + 1) * 512],
                        in_=C_ko[dout]))
            eng.wait_ge(s_mm, MM_K)
            eng.wait_ge(s_flag, 2)
            cast_wait(ds_cw, eng, "wv")
            for k in range(16):
                ds_wt.throttle(eng)
                ds_wt.issue(eng, eng.dma_start(
                    out=A_w1[k], in_=wbf["wv"][:, k * 128:(k + 1) * 128], transpose=True))
            for blk in range(NBK):
                bb = blk % 2
                if blk >= 2:
                    eng.wait_ge(s_mm, MM_K + (blk - 2) * 16 + 16)
                for k in range(16):
                    ds_nt.throttle(eng)
                    ds_nt.issue(eng, eng.dma_start(
                        out=B_nt[bb][k],
                        in_=nxh_d[blk * 512:(blk + 1) * 512, k * 128:(k + 1) * 128],
                        transpose=True))
                for ts in range(4):
                    eng.wait_ge(s_ev2, blk * 16 + ts * 4 + 4)
                    ds_vv.throttle(eng)
                    ds_vv.issue(eng, eng.dma_start(
                        out=vsh_d[blk * 512 + ts * 128:blk * 512 + (ts + 1) * 128, :]
                        if False else
                        vloc_d[blk * 512 + ts * 128:blk * 512 + (ts + 1) * 128, :],
                        in_=C_vo[ts]))
            eng.wait_ge(s_kst, 8)
            ds_kss.issue(eng, eng.dma_start(
                out=shsl(kstat_d, side, kstat_d[0][:, :]), in_=kmaxsq[:]))
            ds_ko.wait_all(eng, NBK * 16)
            ds_vv.wait_all(eng, NBK * 4)
            for p in range(4):
                ds_cp.throttle(eng)
                ds_cp.issue(eng, eng.dma_start(
                    out=dynoff(ksh_d[4 * p:4 * p + 4][:, :, 0:HALF], koff),
                    in_=kloc_d[4 * p:4 * p + 4]))
            for p in range(2):
                ds_cp.throttle(eng)
                ds_cp.issue(eng, eng.dma_start(
                    out=dynoff(vsh_d[p * 2048:(p + 1) * 2048, :], voff),
                    in_=vloc_d[p * 2048:(p + 1) * 2048, :]))
            eng.wait_ge(s_flag, 3)
            cast_wait(ds_cw, eng, "wq")
            eng.wait_ge(s_mm, MM_V)
            for k in range(16):
                ds_wt.throttle(eng)
                ds_wt.issue(eng, eng.dma_start(
                    out=A_w1[k], in_=wbf["wq"][:, k * 128:(k + 1) * 128], transpose=True))
            ds_nxw.wait_all(eng, NT_ALL)
            for k in range(16):
                ds_nt.throttle(eng)
                ds_nt.issue(eng, eng.dma_start(
                    out=B_ntq[k], in_=nxo_d[:, k * 128:(k + 1) * 128], transpose=True))

            if STAGE == 2:
                for k in range(16):
                    eng.wait_ge(s_dbg, k + 1)
                    ds_dbo.throttle(eng)
                    ds_dbo.issue(eng, eng.dma_start(
                        out=dbgk[k * 128:(k + 1) * 128, :], in_=E_dbg[:, 0:1024]))
                eng.wait_ge(s_flag, 4)
                ds_dbi.issue(eng, eng.dma_start(out=E_pp[0], in_=ksh_d[0][:, 0:2048]))
                ds_dbi.issue(eng, eng.dma_start(out=E_pp[1], in_=vsh_d[0:128, :]))
                for j, r0 in enumerate([0, 128, 256]):
                    eng.wait_ge(s_dbg, 17 + j)
                    ds_dbo.throttle(eng)
                    ds_dbo.issue(eng, eng.dma_start(out=dbg[r0:r0 + 128, :], in_=E_dbg[:]))
                ds_dbo.wait_all(eng, 19)
                return

            eng.wait_ge(s_flag, 4)
            for p in range(8):
                if p < 2:
                    eng.wait_ge(s_mm, MM_Q)
                else:
                    eng.wait_ge(s_pe2, PE2_DB(p - 1))
                for e2 in range(2):
                    ds_kt.throttle(eng)
                    ds_kt.issue(eng, eng.dma_start(
                        out=A_kt[(p % 2) * 2 + e2], in_=ksh_d[2 * p + e2]))
                for g in range(8):
                    vg = 8 * p + g
                    if vg >= 2:
                        q2, g2 = divmod(vg - 2, 8)
                        eng.wait_ge(s_mm, MM_PV(q2, 8 * g2 + 7))
                    ds_vp.throttle(eng)
                    ds_vp.issue(eng, eng.dma_start(
                        out=E_vp[vg % 2].rearrange("p (c d) -> p c d", c=8),
                        in_=vsh_d[g * 1024:(g + 1) * 1024,
                                  2 * p * 128:(2 * p + 2) * 128].rearrange(
                                      "(c p) d -> p c d", p=128)))

            if STAGE == 3:
                for k in range(16):
                    eng.wait_ge(s_dbg, k + 1)
                    ds_dbo.throttle(eng)
                    ds_dbo.issue(eng, eng.dma_start(
                        out=dbgk[k * 128:(k + 1) * 128, :], in_=B_res[0][:, 0:1024]))
                ds_dbo.wait_all(eng, 16)
                return

            eng.wait_ge(s_flag, 5)
            cast_wait(ds_cw, eng, "wo")
            eng.wait_ge(s_pe2, PE2_P3)
            ds_dbi.throttle(eng)
            ds_dbi.issue(eng, eng.dma_start(out=bo_row[:], in_=vec_in["bo"]))
            ds_dbi.throttle(eng)
            ds_dbi.issue(eng, eng.dma_start(out=sdpc_row[:], in_=vec_in["wsc_dp"]))
            for k in range(16):
                ds_wt.throttle(eng)
                ds_wt.issue(eng, eng.dma_start(
                    out=A_w1[k], in_=wbf["wo"][:, k * 128:(k + 1) * 128], transpose=True))

            def _x2w(j):
                eng.wait_ge(s_ln, LN_P4 + j + 1)
                ds_nxw.throttle(eng)
                ds_nxw.issue(eng, eng.dma_start(
                    out=x2_d[j * 128:(j + 1) * 128, :], in_=D_x2[j % 2]))

            for i in range(8):
                if i >= 2:
                    eng.wait_ge(s_ln, LN_P4 + i - 1)
                ds_x2.throttle(eng)
                ds_x2.issue(eng, eng.dma_start(
                    out=D_xo[i % 2], in_=x_own[i * 128:(i + 1) * 128, :]))
                if i >= 2:
                    _x2w(i - 2)
            _x2w(6)
            _x2w(7)

            if STAGE == 4:
                ds_nxw.wait_all(eng, NT_ALL + 8)
                ds_dbo.issue(eng, eng.dma_start(out=dbg[:, :], in_=x2_d[:, :]))
                ds_dbo.wait_all(eng, 1)
                return

            for hf in range(2):
                ds_nxw.wait_all(eng, NT_ALL + 8)
                if hf == 1:
                    eng.wait_ge(s_ln, LN5(0) + 4)
                for tt in range(4):
                    ds_x2.throttle(eng)
                    ds_x2.issue(eng, eng.dma_start(
                        out=B_res[tt],
                        in_=x2_d[hf * 512 + tt * 128:hf * 512 + (tt + 1) * 128, :]))
                for wi, wnm in enumerate(["wup", "wgate"]):
                    eng.wait_ge(s_flag, 6 + wi)
                    cast_wait(ds_cw, eng, wnm)
                    for B in range(8):
                        jbase = hf * 384 + wi * 128 + B * 16
                        if B >= 2:
                            eng.wait_ge(s_mm, MM5(hf) + 16 + wi * 64 + (B - 2) * 8 + 8)
                        elif hf == 1 or wi == 1:
                            prev = MM5(hf) + 16 + wi * 64 if wi == 1 else MM5(hf - 1) + 336
                            eng.wait_ge(s_mm, prev)
                        else:
                            eng.wait_ge(s_mm, MM_WO)
                        for k in range(16):
                            ds_wb.throttle(eng)
                            ds_wb.issue(eng, eng.dma_start(
                                out=A_wb[B % 2][k],
                                in_=wbf[wnm][B * 1024:(B + 1) * 1024,
                                             k * 128:(k + 1) * 128], transpose=True))
                eng.wait_ge(s_flag, 8)
                cast_wait(ds_cw, eng, "wdown")
                for dh2 in range(2):
                    for k2 in range(64):
                        kk = dh2 * 64 + k2
                        if kk >= 2:
                            eng.wait_ge(s_mm, MM5(hf) + 208 + kk - 1)
                        else:
                            eng.wait_ge(s_mm, MM5(hf) + 144)
                        ds_wb.throttle(eng)
                        ds_wb.issue(eng, eng.dma_start(
                            out=A_wd[k2 % 2],
                            in_=wbf["wdown"][dh2 * 1024:(dh2 + 1) * 1024,
                                             k2 * 128:(k2 + 1) * 128], transpose=True))
                for tt in range(4):
                    eng.wait_ge(s_ln, LN5(hf) + tt + 1)
                    ds_out.throttle(eng)
                    ds_out.issue(eng, eng.dma_start(
                        out=out[hf * 512 + tt * 128:hf * 512 + (tt + 1) * 128, :],
                        in_=P5_out[tt % 2]))
            ds_out.wait_all(eng, 8)

        @block.gpsimd
        def _(eng):
            my = nc.partition_id([ET.Pool])
            side = my % 2
            bud = 1 - side
            pmark, pmarkpub, pwait = chain(eng, s_chp)

            def dynoff_p(ap, extra):
                return dataclasses.replace(ap, offset=ap.offset + extra)

            if STAGE >= 2:
                for nm in CAST_ORDER:
                    rows, cols = w_in[nm].shape
                    l2 = rows * cols // 2
                    half_off = side * l2
                    wf_o = wbf[nm].rearrange("(o a) b -> o (a b)", o=1)
                    wf_i = w_in[nm].rearrange("(o a) b -> o (a b)", o=1)
                    ds_cw.throttle(eng)
                    ds_cw.issue(eng, eng.dma_start(
                        out=dynoff_p(wf_o[0:1, 0:l2], half_off),
                        in_=dynoff_p(wf_i[0:1, 0:l2], half_off)))

            for op in [lambda: eng.memset(identf[:], 0.0),
                       lambda: eng.affine_select(out=identf[:], in_=identf[:],
                                                 compare_op=ALU.not_equal, fill=1.0,
                                                 base=0, pattern=[[-1, 128]],
                                                 channel_multiplier=1),
                       lambda: eng.iota(iota_i[:], pattern=[[0, 128]], base=0,
                                        channel_multiplier=1),
                       lambda: eng.iota(iota_j[:], pattern=[[1, 128]], base=0,
                                        channel_multiplier=0),
                       lambda: eng.memset(ones128[:], 1.0),
                       lambda: eng.memset(ones128f[:], 1.0),
                       lambda: eng.memset(onesrowf[:], 1.0),
                       lambda: eng.memset(cst[:, 0:1], EPS),
                       lambda: eng.memset(kmaxsq[:], 0.0)]:
                pmark(op())
                pwait()
            eng.memset(cst[:, 1:2], 0.0).then_inc(s_gen, 1)

            if STAGE < 2:
                return

            ds_vl.wait(eng, 0)
            FLAG_OF = {"wk": FL_WK, "wq": FL_WV, "wdown": FL_WO}
            for nm in CAST_ORDER:
                if nm not in FLAG_OF:
                    continue
                cast_wait(ds_cw, eng, nm)
                if nm == "wq":
                    cast_wait(ds_cw, eng, "wv")
                if nm == "wdown":
                    for other in ("wo", "wup", "wgate"):
                        cast_wait(ds_cw, eng, other)
                ds_bf.throttle(eng)
                ds_bf.issue(eng, eng.dma_start(
                    out=shsl(flags_d, side, flags_d[0][FLAG_OF[nm]:FLAG_OF[nm] + 1, :]),
                    in_=nonce_sb[:]))

            with eng.register("want") as want, eng.register("go") as go, \
                 eng.register("cur") as cur, eng.register("itr") as itr:
                eng.reg_load(want, nonce_sb[0:1, 0:1])
                eng.reg_mov(itr, 0)

                def poll(fl):
                    eng.reg_mov(go, 1)
                    with eng.While(go):
                        eng.reg_alu(itr, itr, 16, ALU.add)
                        eng.dma_start(out=flag_sb[:],
                                      in_=shsl(flags_d, bud, flags_d[0][fl:fl + 1, :])
                                      ).then_inc(s_poll, 16)
                        eng.wait_ge(s_poll, itr)
                        eng.reg_load(cur, flag_sb[0:1, 0:1])
                        eng.reg_alu(go, cur, want, ALU.not_equal)
                    eng.reg_alu(go, go, 0, ALU.add).then_inc(s_flag, 1)

                poll(FL_WK)
                poll(FL_WV)
                eng.reg_alu(go, go, 0, ALU.add).then_inc(s_flag, 1)
                ds_cp.wait_all(eng, 6)
                ds_kss.wait(eng, 0)
                ds_bf.throttle(eng)
                ds_bf.issue(eng, eng.dma_start(
                    out=shsl(flags_d, side, flags_d[0][FL_KV:FL_KV + 1, :]),
                    in_=nonce_sb[:]))
                poll(FL_KV)
                ds_ksf.issue(eng, eng.dma_start(
                    out=kst_sb[:], in_=shsl(kstat_d, bud, kstat_d[0][:, :])))
                if STAGE >= 3:
                    for p3 in range(8):
                        for C3 in range(64):
                            eng.wait_ge(s_exp, S_EXP(p3, C3))
                            if C3 >= 1:
                                eng.wait_ge(s_acc2, S_ACC(p3, C3 - 1))
                            pwait()
                            if C3 == 0:
                                pmarkpub(eng.tensor_copy(
                                    E_acc[:, 1536:2048], E_pp[0][:, 1536:2048]
                                    ).then_inc(s_acc2, 1), s_acc2, S_ACC(p3, C3))
                            else:
                                pmarkpub(eng.tensor_tensor(
                                    E_acc[:, 1536:2048], E_acc[:, 1536:2048],
                                    E_pp[C3 % 2][:, 1536:2048],
                                    ALU.add).then_inc(s_acc2, 1), s_acc2, S_ACC(p3, C3))
                        eng.wait_ge(s_pe2, PE2_DB(p3))
                if STAGE >= 4:
                    poll(FL_WO)
                if STAGE >= 5:
                    for _x5 in range(3):
                        eng.reg_alu(go, go, 0, ALU.add).then_inc(s_flag, 1)

        @block.tensor
        def _(eng):
            eng.wait_ge(s_gen, 1)
            for i, (nm, rows) in enumerate(VECS):
                ds_vl.wait(eng, i + 1)
                if i > 0:
                    eng.wait_ge(s_vecd, i)
                eng.transpose(psf[0:128, 6 * 512:6 * 512 + rows], vrow[0:rows, :],
                              identf[0:rows, 0:rows]).then_inc(s_vecp, 1)
            ds_vl.wait(eng, NV + 1)
            eng.wait_ge(s_vecd, NV)
            for c in range(16):
                inst = eng.matmul(psf[:, c * 128:(c + 1) * 128],
                                  onesrowf[:, :], brow[0:1, c * 128:(c + 1) * 128],
                                  start=(c % 4 == 0), stop=(c % 4 == 3),
                                  skip_group_check=True)
            inst.then_inc(s_vecp, 1)

            if STAGE < 2:
                return

            ds_wt.wait_all(eng, 16)
            eng.wait_ge(s_vecd, NV + 2)
            for blk in range(NBK):
                bb = blk % 2
                ds_nt.wait_all(eng, 16 * (blk + 1))
                if blk >= 1:
                    eng.wait_ge(s_kst, blk)
                for dout in range(16):
                    g = blk * 16 + dout
                    if g >= 4:
                        eng.wait_ge(s_ev, g - 3)
                    for k in range(16):
                        inst = eng.matmul(bkf(dout % 4),
                                          A_w1[k][:, dout * 128:(dout + 1) * 128],
                                          B_nt[bb][k], start=(k == 0), stop=(k == 15))
                    inst.then_inc(s_mm, 1)
                    eng.wait_ge(s_sq, g + 1)
                    inst = eng.matmul(psf[0:1, 4 * 512:5 * 512], ones128[:, 0:1],
                                      E_ksq[dout % 2],
                                      start=(dout == 0), stop=(dout == 15))
                inst.then_inc(s_pe2, 1)
            ds_wt.wait_all(eng, 32)
            for blk in range(NBK):
                bb = blk % 2
                ds_nt.wait_all(eng, 128 + 16 * (blk + 1))
                for ts in range(4):
                    for ns in range(4):
                        g = blk * 16 + ts * 4 + ns
                        if g >= 4:
                            eng.wait_ge(s_ev2, g - 3)
                        for k in range(16):
                            inst = eng.matmul(bkf(g % 4),
                                              B_nt[bb][k][:, ts * 128:(ts + 1) * 128],
                                              A_w1[k][:, ns * 512:(ns + 1) * 512],
                                              start=(k == 0), stop=(k == 15))
                        inst.then_inc(s_mm, 1)
            ds_wt.wait_all(eng, 48)
            ds_nt.wait_all(eng, 256 + 16)
            for dout in range(16):
                if dout >= 2:
                    eng.wait_ge(s_ev, EV_K + dout - 1)
                b0 = (dout % 2) * 2
                for k in range(16):
                    eng.matmul(bkf(b0), A_w1[k][:, dout * 128:(dout + 1) * 128],
                               B_ntq[k][:, 0:512], start=(k == 0), stop=(k == 15))
                for k in range(16):
                    inst = eng.matmul(bkf(b0 + 1), A_w1[k][:, dout * 128:(dout + 1) * 128],
                                      B_ntq[k][:, 512:1024], start=(k == 0), stop=(k == 15))
                inst.then_inc(s_mm, 1)
            for h in range(16):
                eng.wait_ge(s_sq, EV_K + h + 1)
                if h >= 1:
                    eng.wait_ge(s_kst, 8 + h)
                eng.matmul(psf[0:1, 4 * 512:5 * 512], ones128[:, 0:1], E_qsq[:, 0:512],
                           start=True, stop=True)
                eng.matmul(psf[0:1, 5 * 512:6 * 512], ones128[:, 0:1], E_qsq[:, 512:1024],
                           start=True, stop=True).then_inc(s_pe2, 1)
            eng.wait_ge(s_dve2, 1)
            eng.matmul(psf[:, 6 * 512:6 * 512 + 8], onesrowf[:, :], cpair[:, :],
                       start=True, stop=True).then_inc(s_pe2, 1)

            if STAGE < 3:
                return

            def pv_group(p, Cp):
                vg = 8 * p + Cp // 8
                if Cp % 8 == 0:
                    ds_vp.wait(eng, vg)
                st, sp = (Cp == 0), (Cp == 63)
                vpb = E_vp[vg % 2]
                sl = (Cp % 8) * 256
                ppb = E_pp[Cp % 2]
                eng.matmul(bkf(4), vpb[:, sl:sl + 128], ppb[:, 0:512], start=st, stop=sp)
                eng.matmul(bkf(5), vpb[:, sl:sl + 128], ppb[:, 512:1024],
                           start=st, stop=sp)
                eng.matmul(bkf(6), vpb[:, sl + 128:sl + 256], ppb[:, 1024:1536],
                           start=st, stop=sp)
                eng.matmul(bkf(7), vpb[:, sl + 128:sl + 256], ppb[:, 1536:2048],
                           start=st, stop=sp).then_inc(s_mm, 1)

            for p in range(8):
                kta = A_kt[(p % 2) * 2]
                ktb = A_kt[(p % 2) * 2 + 1]
                for C in range(64):
                    eng.wait_ge(s_exp, S_EXP(p, C) - 1)
                    if C == 0:
                        eng.wait_ge(s_dve2, S_DVE2_SW(p) - 1)
                        ds_kt.wait(eng, 2 * p)
                        ds_kt.wait(eng, 2 * p + 1)
                    eng.matmul(bkf(0), kta[:, C * 128:(C + 1) * 128], D_qt[2 * p][:, 0:512],
                               start=True, stop=True)
                    eng.matmul(bkf(1), kta[:, C * 128:(C + 1) * 128],
                               D_qt[2 * p][:, 512:1024], start=True, stop=True)
                    eng.matmul(bkf(2), ktb[:, C * 128:(C + 1) * 128],
                               D_qt[2 * p + 1][:, 0:512], start=True, stop=True)
                    eng.matmul(bkf(3), ktb[:, C * 128:(C + 1) * 128],
                               D_qt[2 * p + 1][:, 512:1024],
                               start=True, stop=True).then_inc(s_pe2, 1)
                    if C >= 1:
                        pv_group(p, C - 1)
                eng.wait_ge(s_exp, S_EXP(p, 63))
                pv_group(p, 63)
                eng.wait_ge(s_acc, S_ACC(p, 63))
                eng.wait_ge(s_acc2, S_ACC(p, 63))
                eng.matmul(bkf(0), ones128f[:, :], E_acc[:, 0:512], start=True, stop=True)
                eng.matmul(bkf(1), ones128f[:, :], E_acc[:, 512:1024], start=True, stop=True)
                eng.matmul(bkf(2), ones128f[:, :], E_acc[:, 1024:1536], start=True, stop=True)
                eng.matmul(bkf(3), ones128f[:, :], E_acc[:, 1536:2048],
                           start=True, stop=True).then_inc(s_pe2, 1)

            if STAGE < 4:
                return

            ds_wt.wait_all(eng, 64)
            eng.wait_ge(s_dve2, 10)
            ds_dbi.wait(eng, 0)
            for i in range(8):
                if i >= 2:
                    eng.wait_ge(s_dve2, 10 + i - 1)
                b0 = (i % 2) * 4
                for dseg in range(4):
                    for k in range(16):
                        inst = eng.matmul(bkf(b0 + dseg),
                                          C_ao[k][:, i * 128:(i + 1) * 128],
                                          A_w1[k][:, dseg * 512:(dseg + 1) * 512],
                                          start=(k == 0), stop=False,
                                          skip_group_check=True)
                    if dseg == 3:
                        inst.then_inc(s_mm, 1)
                    inst = eng.matmul(bkf(b0 + dseg), onesrowf[:, :],
                                      bo_row[0:1, dseg * 512:(dseg + 1) * 512],
                                      start=False, stop=True, skip_group_check=True)
                    if dseg == 3:
                        inst.then_inc(s_pe2, 1)

            if STAGE < 5:
                return

            eng.wait_ge(s_dve2, 18)
            ds_dbi.wait(eng, 1)
            for cc in range(16):
                inst = eng.matmul(psf[:, cc * 128:(cc + 1) * 128], onesrowf[:, :],
                                  sdpc_row[0:1, cc * 128:(cc + 1) * 128],
                                  start=(cc % 4 == 0), stop=(cc % 4 == 3),
                                  skip_group_check=True)
            inst.then_inc(s_pe2, 1)

            for hf in range(2):
                for tt in range(4):
                    eng.wait_ge(s_ds, DS5(hf) + tt + 1)
                    if tt >= 1:
                        eng.wait_ge(s_ev, EV5(hf) + tt)
                    elif hf == 1:
                        eng.wait_ge(s_dve2, DVE2_L4(0, 3))
                    dst = bkf(4, 4)
                    for cc in range(16):
                        inst = eng.matmul(dst[:, cc * 128:(cc + 1) * 128],
                                          C_n3f[:, cc * 128:(cc + 1) * 128],
                                          identf[:, :], is_transpose=True,
                                          start=(cc % 4 == 0), stop=(cc % 4 == 3),
                                          skip_group_check=True)
                    inst.then_inc(s_pe2, 1)
                eng.wait_ge(s_dve2, DVE2_H16(hf))
                _, sl_map = butterfly_plan(16)
                for cc in range(16):
                    if cc >= 4:
                        eng.wait_ge(s_ev, EV5(hf) + 4 + cc - 3)
                    eng.matmul(bkf(cc % 4), h128[:, :], C_xt[sl_map[cc]],
                               start=True, stop=True).then_inc(s_mm, 1)
                for wi in range(2):
                    for B in range(8):
                        ds_wb.wait_range(eng, hf * 384 + wi * 128 + B * 16, 16)
                        for c8 in range(8):
                            g5 = B * 8 + c8
                            if wi == 0 and B == 0 and c8 < 4:
                                eng.wait_ge(s_ev, EV5(hf) + 20)
                            else:
                                gg = wi * 64 + g5
                                if gg >= 4:
                                    eng.wait_ge(s_ev, EV5(hf) + 20 + gg - 3)
                            for k in range(16):
                                inst = eng.matmul(bkf(c8 % 4),
                                                  A_wb[B % 2][k][:, c8 * 128:(c8 + 1) * 128],
                                                  C_xt[sl_map[k]],
                                                  start=(k == 0), stop=(k == 15))
                            inst.then_inc(s_mm, 1)
                eng.wait_ge(s_dve2, DVE2_H64(hf))
                _, slz = butterfly_plan(64)

                def zph(sl5):
                    return Z[sl5] if sl5 < 64 else C_xt[16]

                for cc in range(64):
                    if cc >= 4:
                        eng.wait_ge(s_ev, EV5(hf) + 148 + cc - 3)
                    eng.matmul(bkf(cc % 4), h128[:, :], zph(slz[cc]),
                               start=True, stop=True).then_inc(s_mm, 1)
                for dh2 in range(2):
                    for k2 in range(64):
                        kk = dh2 * 64 + k2
                        ds_wb.wait(eng, hf * 384 + 256 + kk)
                        if k2 == 0:
                            eng.wait_ge(s_ev, EV5(hf) + 148 + 64)
                            if dh2 == 1:
                                eng.wait_ge(s_ev, EV5(hf) + 212 + 4)
                        for tt in range(4):
                            for ds2 in range(2):
                                inst = eng.matmul(
                                    bkf(tt * 2 + ds2),
                                    zph(slz[k2])[:, tt * 128:(tt + 1) * 128],
                                    A_wd[k2 % 2][:, ds2 * 512:(ds2 + 1) * 512],
                                    start=(k2 == 0), stop=(k2 == 63),
                                    skip_group_check=True)
                        inst.then_inc(s_mm, 1)

        @block.scalar
        def _(eng):
            amark, amarkpub, await_ = chain(eng, s_cha)
            for i in range(NT_ALL):
                eng.wait_ge(s_st, i + 1)
                amark(eng.activation(ln_mv[:, 1:2], ln_mv[:, 1:2], AF.Ln,
                                     bias=cst[:, 0:1], scale=1.0))
                await_()
                eng.activation(ln_r[:], ln_mv[:, 1:2], AF.Exp, bias=cst[:, 1:2],
                               scale=-0.5).then_inc(s_lnr, 1)

            if STAGE < 2:
                return

            for blk in range(NBK):
                for dout in range(16):
                    g = blk * 16 + dout
                    eng.wait_ge(s_mm, g + 1)
                    if blk >= 1:
                        ds_ko.wait(eng, (blk - 1) * 16 + dout)
                        eng.wait_ge(s_sq, g - 16 + 1)
                    eng.activation(C_ko[dout], bkf(dout % 4), AF.Identity,
                                   bias=bkc[:, dout:dout + 1], scale=1.0).then_inc(s_ev, 1)
            for dout in range(16):
                eng.wait_ge(s_mm, MM_V + dout + 1)
                amark(eng.activation(D_qt[dout][:, 0:512], bkf((dout % 2) * 2), AF.Identity,
                                     bias=bqc[:, dout:dout + 1], scale=RSQ_DH))
                await_()
                eng.activation(D_qt[dout][:, 512:1024], bkf((dout % 2) * 2 + 1), AF.Identity,
                               bias=bqc[:, dout:dout + 1], scale=RSQ_DH).then_inc(s_ev, 1)
            eng.wait_ge(s_kst, 24)
            eng.wait_ge(s_ds, 1)
            amark(eng.activation(cwork[:], cwork[:], AF.Ln, bias=cst[0:1, 1:2], scale=1.0))
            await_()
            eng.activation(cwork[:], cwork[:], AF.Exp, bias=cst[0:1, 1:2],
                           scale=0.5).then_inc(s_lnr, 1)

            if STAGE < 3:
                return

            eng.wait_ge(s_dve2, 2)
            for p in range(8):
                for C in range(64):
                    eng.wait_ge(s_pe2, PE2_S(p, C))
                    if C >= 2:
                        eng.wait_ge(s_mm, MM_PV(p, C - 2))
                        eng.wait_ge(s_acc, S_ACC(p, C - 2))
                        eng.wait_ge(s_acc2, S_ACC(p, C - 2))
                    eng.activation(E_pp[C % 2], psf[:, 0:2048], AF.Exp,
                                   bias=negCb[:, p:p + 1],
                                   scale=1.0).then_inc(s_exp, 1)

            if STAGE < 4:
                return

            for i in range(8):
                eng.wait_ge(s_st, ST_P4 + i + 1)
                amark(eng.activation(ln_mv[:, 1:2], ln_mv[:, 1:2], AF.Ln,
                                     bias=cst[:, 0:1], scale=1.0))
                await_()
                eng.activation(ln_r[:], ln_mv[:, 1:2], AF.Exp, bias=cst[:, 1:2],
                               scale=-0.5).then_inc(s_lnr, 1)

            if STAGE < 5:
                return

            eng.wait_ge(s_pe2, 554)
            eng.activation(sdpc_bb[:], bkf(0, 4), AF.Identity, bias=cst[:, 1:2],
                           scale=64.0).then_inc(s_ev, 1)
            for hf in range(2):
                _, sl_map = butterfly_plan(16)
                _, slz = butterfly_plan(64)
                for tt in range(4):
                    eng.wait_ge(s_st, ST5(hf) + tt + 1)
                    amark(eng.activation(ln_mv[:, 1:2], ln_mv[:, 1:2], AF.Ln,
                                         bias=cst[:, 0:1], scale=1.0))
                    await_()
                    eng.activation(ln_r[:], ln_mv[:, 1:2], AF.Exp, bias=cst[:, 1:2],
                                   scale=-0.5).then_inc(s_lnr, 1)
                    eng.wait_ge(s_pe2, PE25(hf) + tt + 1)
                    for cc in range(16):
                        src2 = bkf(4, 4)[:, cc * 128:(cc + 1) * 128]
                        inst = eng.activation(C_xt[cc][:, tt * 128:(tt + 1) * 128], src2,
                                              AF.Identity, bias=cst[:, 1:2],
                                              scale=su1c[:, cc:cc + 1])
                        if cc == 15:
                            inst.then_inc(s_ev, 1)
                        else:
                            amark(inst)
                            await_()
                for cc in range(16):
                    eng.wait_ge(s_mm, MM5(hf) + cc + 1)
                    eng.activation(C_xt[sl_map[cc]], bkf(cc % 4), AF.Identity,
                                   bias=cst[:, 1:2],
                                   scale=1.0).then_inc(s_ev, 1)
                if hf == 1:
                    ds_out.wait_range(eng, 0, 4)
                for wi in range(2):
                    for B in range(8):
                        for c8 in range(8):
                            g5 = B * 8 + c8
                            eng.wait_ge(s_mm, MM5(hf) + 16 + wi * 64 + g5 + 1)
                            cgl = B * 8 + c8
                            if wi == 0:
                                eng.activation(Z[cgl], bkf(c8 % 4), AF.Identity,
                                               bias=cst[:, 1:2],
                                               scale=supc[:, cgl:cgl + 1]
                                               ).then_inc(s_ev, 1)
                            else:
                                if g5 >= 2:
                                    eng.wait_ge(s_ev2, EV25(hf) + g5 - 1)
                                eng.activation(C_sg[c8 % 2], bkf(c8 % 4), AF.Silu,
                                               bias=cst[:, 1:2],
                                               scale=sgtc[:, cgl:cgl + 1]
                                               ).then_inc(s_ev, 1)
                for cc in range(64):
                    eng.wait_ge(s_mm, MM5(hf) + 144 + cc + 1)
                    zt = Z[slz[cc]] if slz[cc] < 64 else C_xt[16]
                    eng.activation(zt, bkf(cc % 4), AF.Identity, bias=cst[:, 1:2],
                                   scale=1.0).then_inc(s_ev, 1)
                for tt in range(4):
                    eng.wait_ge(s_ev2, EV25(hf) + 64 + tt + 1)
                    eng.activation(C_ao[tt], bkf(tt * 2, 2), AF.Identity,
                                   bias=cst[:, 1:2],
                                   scale=1.0).then_inc(s_ev, 1)
                for tt in range(4):
                    eng.wait_ge(s_st, ST5(hf) + 4 + tt + 1)
                    amark(eng.activation(ln_mv[:, 1:2], ln_mv[:, 1:2], AF.Ln,
                                         bias=cst[:, 0:1], scale=1.0))
                    await_()
                    eng.activation(ln_r[:], ln_mv[:, 1:2], AF.Exp, bias=cst[:, 1:2],
                                   scale=-0.5).then_inc(s_lnr, 1)

        @block.vector
        def _(eng):
            vmark, vmarkpub, vwait = chain(eng, s_chv)
            eng.wait_ge(s_gen, 1)
            vmark(eng.tensor_copy(identb[:], identf[:]))
            vwait()
            vmark(eng.tensor_tensor(hwork[:], iota_i[:], iota_j[:], ALU.bitwise_and))
            for sh in (4, 2, 1):
                vwait()
                vmark(eng.tensor_scalar(iota_j[:], hwork[:], sh, None,
                                        ALU.logical_shift_right))
                vwait()
                vmark(eng.tensor_tensor(hwork[:], hwork[:], iota_j[:], ALU.bitwise_xor))
            vwait()
            vmark(eng.tensor_scalar(hwork[:], hwork[:], 1, None, ALU.bitwise_and))
            vwait()
            vmark(eng.tensor_copy(hworkf[:], hwork[:]))
            vwait()
            vmark(eng.tensor_scalar(h128[:], hworkf[:], -2.0, 1.0, ALU.mult, ALU.add))

            for i, (nm, rows) in enumerate(VECS):
                eng.wait_ge(s_vecp, i + 1)
                srcp = psf[0:128, 6 * 512:6 * 512 + rows]
                if nm == "bq":
                    inst = eng.tensor_scalar(bqc[:], srcp, RSQ_DH, None, ALU.mult)
                elif nm == "bk":
                    inst = eng.tensor_copy(bkc[:], srcp)
                elif nm == "bv":
                    inst = eng.tensor_copy(hworkf[:, 0:16], srcp)
                elif nm == "bo":
                    inst = eng.tensor_copy(boc[:], srcp)
                elif nm == "su_ug":
                    inst = eng.tensor_scalar(su1c[:], srcp,
                                             float(1.0 / (64.0 * np.sqrt(2048.0))),
                                             None, ALU.mult)
                elif nm == "wsc_dp":
                    inst = eng.tensor_scalar(sdpc[:], srcp, 64.0, None, ALU.mult)
                elif nm == "su_dp":
                    inst = eng.tensor_copy(sdp_t[:], srcp)
                elif nm == "wsc_ug":
                    inst = eng.tensor_copy(wsc_t[:], srcp)
                inst.then_inc(s_vecd, 1)
            eng.wait_ge(s_vecd, NV)
            vmark(eng.tensor_tensor(supc[:], wsc_t[:, 0:64], sdp_t[:], ALU.mult))
            vwait()
            vmark(eng.tensor_scalar(supc[:], supc[:], float(1.0 / np.sqrt(8192.0)),
                                    None, ALU.mult))
            vwait()
            eng.tensor_scalar(sgtc[:], wsc_t[:, 64:128], 64.0, None,
                              ALU.mult).then_inc(s_vecd, 1)

            if STAGE >= 2:
                eng.wait_ge(s_vecp, NV + 1)
                eng.tensor_copy(bvbc[:], psf[:, 0:2048]).then_inc(s_vecd, 1)
            for i in range(NT_ALL):
                ds_xh.wait(eng, i)
                for sl in range(4):
                    vmark(eng.bn_stats(bnst[:, sl * 6:(sl + 1) * 6],
                                       E_xh[i % 2][:, sl * 512:(sl + 1) * 512]))
                vwait()
                eng.bn_aggr(ln_mv[:], bnst[:]).then_inc(s_st, 1)
                eng.wait_ge(s_lnr, i + 1)
                vmark(eng.tensor_scalar(ln_mr[:], ln_mv[:, 0:1], ln_r[:], -1.0,
                                        ALU.mult, ALU.mult))
                if i >= 2:
                    ds_nxw.wait(eng, i - 2)
                vwait()
                vmarkpub(eng.tensor_scalar(E_nx[i % 2], E_xh[i % 2], ln_r[:], ln_mr[:],
                                           ALU.mult, ALU.add).then_inc(s_ln, 1),
                         s_ln, i + 1)
                if STAGE == 1 and i >= NTH:
                    if i > NTH:
                        ds_dbo.wait(eng, i - NTH - 1)
                    vwait()
                    vmarkpub(eng.tensor_copy(E_dbg[:], E_nx[i % 2]).then_inc(s_dbg, 1),
                             s_dbg, i - NTH + 1)

            if STAGE < 2:
                return


            for blk in range(NBK):
                for dout in range(16):
                    g = blk * 16 + dout
                    eng.wait_ge(s_ev, g + 1)
                    if g >= 2:
                        eng.wait_ge(s_mm, g)
                    vwait()
                    vmarkpub(eng.tensor_tensor(E_ksq[g % 2], C_ko[dout], C_ko[dout],
                                               ALU.mult).then_inc(s_sq, 1), s_sq, g + 1)
                eng.wait_ge(s_pe2, blk + 1)
                vwait()
                vmark(eng.reduce_max(cwork[0:1, 0:1], psf[0:1, 4 * 512:5 * 512], axis=AX.X))
                vwait()
                if blk == 0:
                    vmarkpub(eng.tensor_copy(kmaxsq[0:1, 0:1],
                                             cwork[0:1, 0:1]).then_inc(s_kst, 1),
                             s_kst, blk + 1)
                else:
                    vmarkpub(eng.tensor_tensor(kmaxsq[0:1, 0:1], kmaxsq[0:1, 0:1],
                                               cwork[0:1, 0:1], ALU.max).then_inc(s_kst, 1),
                             s_kst, blk + 1)
                vwait()

            for blk in range(NBK):
                for ts in range(4):
                    for ns in range(4):
                        g = blk * 16 + ts * 4 + ns
                        eng.wait_ge(s_mm, MM_K + g + 1)
                        if blk >= 1:
                            ds_vv.wait(eng, (blk - 1) * 4 + ts)
                        eng.tensor_tensor(C_vo[ts][:, ns * 512:(ns + 1) * 512],
                                          bkf(g % 4), bvbc[:, ns * 512:(ns + 1) * 512],
                                          ALU.add).then_inc(s_ev2, 1)

            for h in range(16):
                eng.wait_ge(s_ev, EV_K + h + 1)
                if h >= 1:
                    eng.wait_ge(s_pe2, 8 + h)
                eng.tensor_tensor(E_qsq, D_qt[h], D_qt[h], ALU.mult).then_inc(s_sq, 1)
                eng.wait_ge(s_pe2, 8 + h + 1)
                vwait()
                vmark(eng.reduce_max(cwork[0:1, 0:1], psf[0:1, 4 * 512:6 * 512], axis=AX.X))
                vwait()
                vmarkpub(eng.tensor_copy(qmaxsq[0:1, h:h + 1],
                                         cwork[0:1, 0:1]).then_inc(s_kst, 1),
                         s_kst, 8 + h + 1)
                vwait()

            eng.wait_ge(s_flag, 4)
            ds_ksf.wait(eng, 0)
            vmark(eng.tensor_tensor(kst_sb[0:1, 0:1], kst_sb[0:1, 0:1],
                                    kmaxsq[0:1, 0:1], ALU.max))
            vwait()
            eng.tensor_scalar(cwork[:], qmaxsq[:], kst_sb[0:1, 0:1], None,
                              ALU.mult).then_inc(s_ds, 1)
            eng.wait_ge(s_lnr, NT_ALL + 1)
            vmark(eng.tensor_tensor(cpair[:], cwork[0:1, 0:16:2], cwork[0:1, 1:16:2],
                                    ALU.max))
            vwait()
            eng.tensor_scalar(cpair[:], cpair[:], -1.0, None,
                              ALU.mult).then_inc(s_dve2, 1)
            eng.wait_ge(s_pe2, PE2_P2)
            eng.tensor_copy(negCb[:], psf[:, 6 * 512:6 * 512 + 8]).then_inc(s_dve2, 1)

            if STAGE >= 3:
                for p in range(8):
                    for C in range(64):
                        eng.wait_ge(s_exp, S_EXP(p, C))
                        if C >= 1:
                            eng.wait_ge(s_acc, S_ACC(p, C - 1))
                        vwait()
                        if C == 0:
                            vmarkpub(eng.tensor_copy(E_acc[:, 0:1536],
                                                     E_pp[0][:, 0:1536]
                                                     ).then_inc(s_acc, 1),
                                     s_acc, S_ACC(p, C))
                        else:
                            vmarkpub(eng.tensor_tensor(E_acc[:, 0:1536], E_acc[:, 0:1536],
                                                       E_pp[C % 2][:, 0:1536],
                                                       ALU.add).then_inc(s_acc, 1),
                                     s_acc, S_ACC(p, C))
                    eng.wait_ge(s_pe2, PE2_DB(p))
                    vwait()
                    vmark(eng.reciprocal(E_dinv[:, 0:1024], bkf(0, 2)))
                    vwait()
                    vmark(eng.reciprocal(E_dinv[:, 1024:2048], bkf(2, 2)))
                    vwait()
                    vmark(eng.tensor_tensor(C_ao[2 * p], bkf(4, 2),
                                            E_dinv[:, 0:1024], ALU.mult))
                    vwait()
                    vmarkpub(eng.tensor_tensor(C_ao[2 * p + 1], bkf(6, 2),
                                               E_dinv[:, 1024:2048],
                                               ALU.mult).then_inc(s_dve2, 1),
                             s_dve2, S_DVE2_SW(p))

            if STAGE == 3:
                for k in range(16):
                    if k > 0:
                        ds_dbo.wait(eng, k - 1)
                    vwait()
                    vmarkpub(eng.tensor_copy(B_res[0][:, 0:1024],
                                             C_ao[k]).then_inc(s_dbg, 1), s_dbg, k + 1)
                return

            if STAGE == 2:
                for k in range(16):
                    if k > 0:
                        ds_dbo.wait(eng, k - 1)
                    vwait()
                    vmarkpub(eng.tensor_copy(E_dbg[:, 0:1024],
                                             D_qt[k]).then_inc(s_dbg, 1), s_dbg, k + 1)
                ds_dbo.wait(eng, 15)
                ds_dbi.wait(eng, 0)
                vwait()
                vmarkpub(eng.tensor_copy(E_dbg[:], E_pp[0]).then_inc(s_dbg, 1),
                         s_dbg, 17)
                ds_dbo.wait(eng, 16)
                ds_dbi.wait(eng, 1)
                vwait()
                vmarkpub(eng.tensor_copy(E_dbg[:], E_pp[1]).then_inc(s_dbg, 1),
                         s_dbg, 18)
                ds_dbo.wait(eng, 17)
                vwait()
                vmark(eng.memset(E_dbg[:], 0.0))
                vwait()
                vmark(eng.tensor_copy(E_dbg[:, 0:8], negCb[:]))
                vwait()
                vmark(eng.tensor_copy(E_dbg[0:1, 8:24], kmaxsq[:]))
                vwait()
                vmark(eng.tensor_copy(E_dbg[0:1, 24:40], qmaxsq[:]))
                vwait()
                eng.tensor_copy(E_dbg[0:1, 40:56], kst_sb[:]).then_inc(s_dbg, 1)
                return

            if STAGE < 4:
                return

            for i in range(8):
                eng.wait_ge(s_pe2, PE2_P3 + i + 1)
                src = bkf((i % 2) * 4, 4)
                for sl in range(4):
                    vmark(eng.bn_stats(bnst[:, sl * 6:(sl + 1) * 6],
                                       src[:, sl * 512:(sl + 1) * 512]))
                vwait()
                eng.bn_aggr(ln_mv[:], bnst[:]).then_inc(s_st, 1)
                eng.wait_ge(s_lnr, LNR_P4 + i + 1)
                vmark(eng.tensor_scalar(ln_mr[:], ln_mv[:, 0:1], ln_r[:], -1.0,
                                        ALU.mult, ALU.mult))
                ds_x2.wait(eng, i)
                if i >= 2:
                    ds_nxw.wait(eng, NT_ALL + i - 2)
                vwait()
                vmarkpub(eng.tensor_scalar(D_x2[i % 2], src, ln_r[:], ln_mr[:],
                                           ALU.mult, ALU.add).then_inc(s_dve2, 1),
                         s_dve2, 10 + i + 1)
                vwait()
                vmarkpub(eng.tensor_tensor(D_x2[i % 2], D_x2[i % 2], D_xo[i % 2],
                                           ALU.add).then_inc(s_ln, 1),
                         s_ln, LN_P4 + i + 1)

            if STAGE < 5:
                return

            for hf in range(2):
                ops16, sl_map = butterfly_plan(16)
                ops64, slz = butterfly_plan(64)

                def zph(sl5):
                    return Z[sl5] if sl5 < 64 else C_xt[16]

                for tt in range(4):
                    ds_x2.wait(eng, 8 + hf * 4 + tt)
                    for sl in range(4):
                        vmark(eng.bn_stats(bnst[:, sl * 6:(sl + 1) * 6],
                                           B_res[tt][:, sl * 512:(sl + 1) * 512]))
                    vwait()
                    eng.bn_aggr(ln_mv[:], bnst[:]).then_inc(s_st, 1)
                    eng.wait_ge(s_lnr, LNR5(hf) + tt + 1)
                    vmark(eng.tensor_scalar(ln_mr[:], ln_mv[:, 0:1], ln_r[:], -1.0,
                                            ALU.mult, ALU.mult))
                    if tt >= 1:
                        eng.wait_ge(s_pe2, PE25(hf) + tt)
                    elif hf == 1:
                        eng.wait_ge(s_pe2, PE25(0) + 4)
                    vwait()
                    vmarkpub(eng.tensor_scalar(C_n3f[:], B_res[tt], ln_r[:],
                                               ln_mr[:], ALU.mult,
                                               ALU.add).then_inc(s_ds, 1),
                             s_ds, DS5(hf) + tt + 1)
                eng.wait_ge(s_ev, EV5(hf) + 4)
                for j, (ba, bb, sp) in enumerate(ops16):
                    vwait()
                    vmark(eng.tensor_tensor(C_xt[sp], C_xt[ba], C_xt[bb], ALU.add))
                    vwait()
                    inst = eng.tensor_tensor(C_xt[bb], C_xt[ba], C_xt[bb],
                                             ALU.subtract)
                    if j == len(ops16) - 1:
                        vmarkpub(inst.then_inc(s_dve2, 1), s_dve2, DVE2_H16(hf))
                    else:
                        vmark(inst)
                for g5 in range(64):
                    eng.wait_ge(s_ev, EV5(hf) + 84 + g5 + 1)
                    vwait()
                    vmarkpub(eng.tensor_tensor(Z[g5], Z[g5], C_sg[g5 % 2],
                                               ALU.mult).then_inc(s_ev2, 1),
                             s_ev2, EV25(hf) + g5 + 1)
                for j, (ba, bb, sp) in enumerate(ops64):
                    vwait()
                    vmark(eng.tensor_tensor(zph(sp), zph(ba), zph(bb), ALU.add))
                    vwait()
                    inst = eng.tensor_tensor(zph(bb), zph(ba), zph(bb), ALU.subtract)
                    if j == len(ops64) - 1:
                        vmarkpub(inst.then_inc(s_dve2, 1), s_dve2, DVE2_H64(hf))
                    else:
                        vmark(inst)
                eng.wait_ge(s_ev, EV5(0))
                for dh2 in range(2):
                    eng.wait_ge(s_mm, MM5(hf) + 208 + dh2 * 64 + 64)
                    for tt in range(4):
                        vwait()
                        vmark(eng.tensor_tensor(bkf(tt * 2, 2), bkf(tt * 2, 2),
                                                sdpc_bb[:, dh2 * 1024:(dh2 + 1) * 1024],
                                                ALU.mult))
                        vwait()
                        vmark(eng.bn_stats(bnst5[:, (tt * 4 + dh2 * 2) * 6:
                                                 (tt * 4 + dh2 * 2 + 1) * 6],
                                           bkf(tt * 2)))
                        vwait()
                        inst = eng.bn_stats(bnst5[:, (tt * 4 + dh2 * 2 + 1) * 6:
                                                  (tt * 4 + dh2 * 2 + 2) * 6],
                                            bkf(tt * 2 + 1))
                        if dh2 == 0:
                            vmarkpub(inst.then_inc(s_ev2, 1), s_ev2,
                                     EV25(hf) + 64 + tt + 1)
                            continue
                        vmark(inst)
                        vwait()
                        eng.bn_aggr(ln_mv[:], bnst5[:, tt * 24:(tt + 1) * 24]
                                    ).then_inc(s_st, 1)
                        eng.wait_ge(s_lnr, LNR5(hf) + 4 + tt + 1)
                        vmark(eng.tensor_scalar(ln_mr[:], ln_mv[:, 0:1], ln_r[:],
                                                -1.0, ALU.mult, ALU.mult))
                        if hf == 1 or tt >= 2:
                            ds_out.wait(eng, hf * 4 + tt - 2)
                        vwait()
                        vmark(eng.tensor_scalar(P5_out[tt % 2][:, 0:1024], C_ao[tt],
                                                ln_r[:], ln_mr[:], ALU.mult, ALU.add))
                        vwait()
                        vmarkpub(eng.tensor_scalar(P5_out[tt % 2][:, 1024:2048],
                                                   bkf(tt * 2, 2), ln_r[:], ln_mr[:],
                                                   ALU.mult,
                                                   ALU.add).then_inc(s_dve2, 1),
                                 s_dve2, DVE2_L4(hf, tt))
                        vwait()
                        vmarkpub(eng.tensor_tensor(P5_out[tt % 2], P5_out[tt % 2],
                                                   B_res[tt], ALU.add
                                                   ).then_inc(s_ln, 1),
                                 s_ln, LN5(hf) + tt + 1)

    return nc


_CACHE = {}


def kernel(L0_attn_X, L0_attn_Wq, L0_attn_bq, L0_attn_Wk, L0_attn_bk, L0_attn_Wv,
           L0_attn_bv, L0_attn_Wo, L0_attn_bo, L0_ffn_SU_ug, L0_ffn_SU_dp,
           L0_ffn_Wscale_ug, L0_ffn_Wscale_dp, L0_ffn_W_up, L0_ffn_W_gate,
           L0_ffn_W_down):
    X = np.ascontiguousarray(np.asarray(L0_attn_X).reshape(S, D))
    if "nc" not in _CACHE:
        _CACHE["nc"] = build_program()
    nc = _CACHE["nc"]
    nonce_val = int(np.random.SeedSequence().entropy % (2**31 - 2)) + 1
    common = {
        "wq": np.ascontiguousarray(L0_attn_Wq), "wk": np.ascontiguousarray(L0_attn_Wk),
        "wv": np.ascontiguousarray(L0_attn_Wv), "wo": np.ascontiguousarray(L0_attn_Wo),
        "wup": np.ascontiguousarray(L0_ffn_W_up),
        "wgate": np.ascontiguousarray(L0_ffn_W_gate),
        "wdown": np.ascontiguousarray(L0_ffn_W_down),
        "bq": np.ascontiguousarray(np.asarray(L0_attn_bq).reshape(1, D)),
        "bk": np.ascontiguousarray(np.asarray(L0_attn_bk).reshape(1, D)),
        "bv": np.ascontiguousarray(np.asarray(L0_attn_bv).reshape(1, D)),
        "bo": np.ascontiguousarray(np.asarray(L0_attn_bo).reshape(1, D)),
        "su_ug": np.ascontiguousarray(np.asarray(L0_ffn_SU_ug).reshape(1, D)),
        "su_dp": np.ascontiguousarray(np.asarray(L0_ffn_SU_dp).reshape(1, INTER)),
        "wsc_ug": np.ascontiguousarray(np.asarray(L0_ffn_Wscale_ug).reshape(1, 2 * INTER)),
        "wsc_dp": np.ascontiguousarray(np.asarray(L0_ffn_Wscale_dp).reshape(1, D)),
        "nonce": np.full((1, 16), nonce_val, np.int32),
    }
    in_maps = []
    for c in range(NC_):
        side = c % 2
        m = dict(common)
        m["x_own"] = np.ascontiguousarray(X[c * TOK:(c + 1) * TOK, :])
        m["x_half"] = np.ascontiguousarray(X[side * HALF:(side + 1) * HALF, :])
        in_maps.append(m)
    res = run_bass_kernel_spmd(nc, in_maps, core_ids=list(range(NC_)))
    outp = np.concatenate([res.results[c]["out"] for c in range(NC_)], axis=0)
    return outp.reshape(1, S, D)


# version 04
# speedup vs baseline: 1.2140x; 1.0774x over previous
import dataclasses
import numpy as np
import concourse.bass as bass
import concourse.mybir as mybir
from concourse.bass_utils import run_bass_kernel_spmd
from contextlib import ExitStack

F32 = mybir.dt.float32
BF16 = mybir.dt.bfloat16
I32 = mybir.dt.int32
AF = mybir.ActivationFunctionType
ALU = mybir.AluOpType
ET = mybir.EngineType
AX = mybir.AxisListType

S, D, NH, DH, INTER = 8192, 2048, 16, 128, 8192
NC_ = 8
TOK = S // NC_
HALF = S // 2
DC = D // 128
IC = INTER // 128
EPS = 1e-5
RSQ_DH = float(1.0 / np.sqrt(128.0))
NBK = HALF // 512
NTH = HALF // 128
NTO = TOK // 128
NT_ALL = NTH + NTO

STAGE = 5

OFF_A = 0
OFF_B = 64 * 1024
OFF_C = 96 * 1024
OFF_E = 128 * 1024
OFF_D = 160 * 1024
OFF_F = 192 * 1024
SBUF_LIMIT = 207 * 1024

FL_WK, FL_WV, FL_WQ, FL_KV, FL_WO = range(5)

VECS = [("bq", 16), ("bk", 16), ("bv", 16), ("bo", 16),
        ("su_ug", 16), ("wsc_dp", 16), ("su_dp", 64), ("wsc_ug", 128)]
NV = len(VECS)
CAST_ORDER = ["wk", "wv", "wq", "wo", "wup", "wgate", "wdown"]
CAST_T0 = {nm: i for i, nm in enumerate(CAST_ORDER)}
CAST_NT = {nm: 1 for nm in CAST_ORDER}
def cast_wait(ds, eng, nm):
    ds.wait_range(eng, CAST_T0[nm], CAST_NT[nm])

PE2_P2 = 25
MM_K, MM_V, MM_Q = 128, 256, 272
EV_K, EV_Q = 128, 144
def PE2_S(p, C):
    return PE2_P2 + 65 * p + C + 1
def PE2_DB(p):
    return PE2_P2 + 65 * p + 65
PE2_P3 = PE2_P2 + 8 * 65
def MM_PV(p, C):
    return MM_Q + 64 * p + C + 1
MM_P3 = MM_Q + 512
def S_EXP(p, C):
    return 64 * p + C + 1
ACC_ENG = ["d" if C % 3 < 2 else "p" for C in range(64)]
_dn, _pn, ACC_DN, ACC_PN = 0, 0, [], []
for _c in range(64):
    if ACC_ENG[_c] == "d":
        _dn += 1
    else:
        _pn += 1
    ACC_DN.append(_dn)
    ACC_PN.append(_pn)
def S_ACCD(p, C):
    return 44 * p + ACC_DN[C]
def S_ACCP(p, C):
    return 21 * p + ACC_PN[C]
def S_DVE2_SW(p):
    return 2 + p + 1
MM_WO = MM_P3 + 8
PE2_P4 = PE2_P3 + 8
def MM5(hf):
    return MM_WO + hf * 336
def PE25(hf):
    return 554 + hf * 4
def EV5(hf):
    return 145 + hf * 216
def EV25(hf):
    return EV2_V + hf * 68
EV2_V = 128
ST_P4, LNR_P4, LN_P4 = 40, 41, 40
def ST5(hf):
    return 48 + hf * 8
def LNR5(hf):
    return 49 + hf * 8
def LN5(hf):
    return 48 + hf * 4
def DS5(hf):
    return 1 + hf * 4
def DVE2_H16(hf):
    return 19 + hf * 6
def DVE2_H64(hf):
    return 20 + hf * 6
def DVE2_L4(hf, tt):
    return 20 + hf * 6 + tt + 1


def butterfly_plan(n):
    slots = list(range(n))
    spare = n
    ops = []
    h = 1
    while h < n:
        for j in range(0, n, 2 * h):
            for i in range(h):
                a, b = j + i, j + i + h
                ops.append((slots[a], slots[b], spare))
                slots[a], spare = spare, slots[a]
        h *= 2
    return ops, slots


def build_program():
    nc = bass.Bass("TRN2", target_bir_lowering=False, debug=False,
                   detect_race_conditions=False)

    x_own = nc.dram_tensor("x_own", [TOK, D], F32, kind="ExternalInput").ap()
    x_half = nc.dram_tensor("x_half", [HALF, D], F32, kind="ExternalInput").ap()
    w_in, wbf = {}, {}
    for nm, shp in [("wq", [D, D]), ("wk", [D, D]), ("wv", [D, D]), ("wo", [D, D]),
                    ("wup", [INTER, D]), ("wgate", [INTER, D]), ("wdown", [D, INTER])]:
        w_in[nm] = nc.dram_tensor(nm, shp, F32, kind="ExternalInput").ap()
        wbf[nm] = nc.dram_tensor("bf_" + nm, shp, BF16, kind="Internal",
                                 addr_space="Shared").ap()
    vec_in = {}
    for nm, rows in VECS:
        vec_in[nm] = nc.dram_tensor(nm, [1, rows * 128], F32, kind="ExternalInput").ap()
    nonce = nc.dram_tensor("nonce", [1, 16], I32, kind="ExternalInput").ap()
    out = nc.dram_tensor("out", [TOK, D], F32, kind="ExternalOutput").ap()
    dbg = nc.dram_tensor("dbg", [TOK, D], F32, kind="ExternalOutput").ap() if STAGE < 5 else None
    dbgk = (nc.dram_tensor("dbgk", [D, TOK], F32, kind="ExternalOutput").ap()
            if STAGE in (2, 3) else None)

    nxh_d = nc.dram_tensor("nxh_d", [HALF, D], BF16, kind="Internal").ap()
    nxo_d = nc.dram_tensor("nxo_d", [TOK, D], BF16, kind="Internal").ap()
    x2_d = nc.dram_tensor("x2_d", [TOK, D], F32, kind="Internal").ap()
    kloc_d = nc.dram_tensor("kloc_d", [DC, 128, HALF], BF16, kind="Internal").ap()
    vloc_d = nc.dram_tensor("vloc_d", [HALF, D], BF16, kind="Internal").ap()
    ksh_d = nc.dram_tensor("ksh_d", [DC, 128, S], BF16, kind="Internal",
                           addr_space="Shared").ap()
    vsh_d = nc.dram_tensor("vsh_d", [S, D], BF16, kind="Internal",
                           addr_space="Shared").ap()
    kstat_d = nc.dram_tensor("kstat_d", [2, 1, 16], F32, kind="Internal",
                             addr_space="Shared").ap()
    flags_d = nc.dram_tensor("flags_d", [2, 5, 16], I32, kind="Internal",
                             addr_space="Shared").ap()

    with ExitStack() as ctx:
        e = ctx.enter_context
        base = (nc.sbuf_base + 31) & ~31
        usable = nc.sbuf_top - base
        assert usable >= SBUF_LIMIT, f"usable sbuf {usable} < {SBUF_LIMIT}"

        def sb(name, shape, dtype, off):
            end = off + int(np.prod(shape[1:])) * mybir.dt.size(dtype)
            assert end <= usable, (name, end, usable)
            return nc.alloc_sbuf_tensor_at(name, shape, dtype, offset=base + off).ap()

        _fo = [OFF_F]
        def sbf(name, shape, dtype):
            t = sb(name, shape, dtype, _fo[0])
            _fo[0] += ((int(np.prod(shape[1:])) * mybir.dt.size(dtype) + 31) // 32) * 32
            return t

        identb = sbf("identb", [128, 128], BF16)
        identf = sbf("identf", [128, 128], F32)
        h128 = sbf("h128", [128, 128], BF16)
        ones128 = sbf("ones128", [128, 128], BF16)
        onesrowf = sbf("onesrowf", [1, 128], F32)
        ones128f = sbf("ones128f", [128, 128], F32)
        iota_i = sbf("iota_i", [128, 128], I32)
        iota_j = sbf("iota_j", [128, 128], I32)
        hwork = sbf("hwork", [128, 128], I32)
        hworkf = sbf("hworkf", [128, 128], F32)
        bqc = sbf("bqc", [128, 16], F32)
        bkc = sbf("bkc", [128, 16], F32)
        boc = sbf("boc", [128, 16], F32)
        su1c = sbf("su1c", [128, 16], F32)
        sdpc = sbf("sdpc", [128, 16], F32)
        sdp_t = sbf("sdp_t", [128, 64], F32)
        wsc_t = sbf("wsc_t", [128, 128], F32)
        supc = sbf("supc", [128, 64], F32)
        sgtc = sbf("sgtc", [128, 64], F32)
        bvbc = sbf("bvbc", [128, 2048], BF16)
        vrow = sbf("vrow", [128, 128], F32)
        kmaxsq = sbf("kmaxsq", [1, 16], F32)
        qmaxsq = sbf("qmaxsq", [1, 16], F32)
        cwork = sbf("cwork", [1, 16], F32)
        cpair = sbf("cpair", [1, 8], F32)
        negCb = sbf("negCb", [128, 8], F32)
        nonce_sb = sbf("nonce_sb", [1, 16], I32)
        flag_sb = sbf("flag_sb", [1, 16], I32)
        kst_sb = sbf("kst_sb", [1, 16], F32)
        bnst = sbf("bnst", [128, 24], F32)
        ln_mv = sbf("ln_mv", [128, 2], F32)
        ln_r = sbf("ln_r", [128, 1], F32)
        ln_mr = sbf("ln_mr", [128, 1], F32)
        cst = sbf("cst", [128, 2], F32)
        E_ksq = [sbf(f"f_ksq_{b}", [128, 512], BF16) for b in range(2)]
        E_qsq = sbf("f_qsq", [128, 1024], BF16)
        assert _fo[0] <= SBUF_LIMIT, _fo[0]

        A_w1 = [sb(f"a_w1_{k}", [128, 2048], BF16, OFF_A + k * 4096) for k in range(16)]
        A_kt = [sb(f"a_kt_{j}", [128, 8192], BF16, OFF_A + j * 16384) for j in range(4)]
        B_nt = [[sb(f"b_nt{b}_{k}", [128, 512], BF16, OFF_B + b * 16384 + k * 1024)
                 for k in range(16)] for b in range(2)]
        B_ntq = [sb(f"b_ntq_{k}", [128, 1024], BF16, OFF_B + k * 2048) for k in range(16)]
        B_res = [sb(f"b_res_{t}", [128, 2048], F32, OFF_B + t * 8192) for t in range(4)]
        P4_li = [sb(f"p4_li_{b}", [128, 2048], BF16, OFF_B + b * 4096) for b in range(2)]
        C_xt = [sb(f"c_xt_{k}", [128, 512], BF16, OFF_C + k * 1024) for k in range(17)]
        C_sg = [sb(f"c_sg_{b}", [128, 512], BF16, OFF_C + 17408 + b * 1024) for b in range(2)]
        C_n3f = sb("c_n3f", [128, 2048], F32, OFF_C + 19456)
        sdpc_bb = sb("sdpc_bb", [128, 2048], BF16, OFF_C + 27648)
        bnst5 = sb("bnst5", [128, 96], F32, OFF_C + 31744)
        bo_row = sb("bo_row", [1, 2048], F32, OFF_E)
        sdpc_row = sb("sdpc_row", [1, 2048], F32, OFF_E + 8192)
        A_wd = [sb(f"a_wd_{b}", [128, 1024], BF16, OFF_A + 32768 + b * 2048) for b in range(2)]
        Z = []
        for zc in range(64):
            zoff = (OFF_E + zc * 1024) if zc < 32 else (OFF_D + (zc - 32) * 1024)
            Z.append(sb(f"z_{zc}", [128, 512], BF16, zoff))
        P5_out = [sb(f"p5_out_{b}", [128, 2048], F32, OFF_E + b * 8192) for b in range(2)]
        A_wb = [[sb(f"a_wb{b}_{k}", [128, 1024], BF16, OFF_A + b * 32768 + k * 2048)
                 for k in range(16)] for b in range(2)]
        C_ko = [sb(f"c_ko_{k}", [128, 512], BF16, OFF_C + k * 1024) for k in range(16)]
        C_vo = [sb(f"c_vo_{t}", [128, 2048], BF16, OFF_C + 16384 + t * 4096) for t in range(4)]
        C_ao = [sb(f"c_ao_{k}", [128, 1024], BF16, OFF_C + k * 2048) for k in range(16)]
        E_xh = [sb(f"e_xh_{b}", [128, 2048], F32, OFF_E + b * 8192) for b in range(2)]
        brow = sb("brow", [1, 2048], F32, OFF_D)
        E_nx = [sb(f"e_nx_{b}", [128, 2048], BF16, OFF_E + 16384 + b * 4096) for b in range(2)]
        E_dbg = sb("e_dbg", [128, 2048], F32, OFF_E + 24576)
        E_vp = [sb(f"e_vp_{b}", [128, 2048], BF16, OFF_E + b * 4096) for b in range(2)]
        E_pp = [sb(f"e_pp_{b}", [128, 2048], BF16, OFF_E + 8192 + b * 4096) for b in range(2)]
        E_pp += [sb(f"e_pp_{b}", [128, 2048], BF16, OFF_B + 16384 + 8192 + (b - 2) * 4096)
                 for b in range(2, 4)]
        acc_b = sb("acc_b", [128, 2048], F32, OFF_B + 16384)
        E_acc = sb("e_acc", [128, 2048], F32, OFF_E + 16384)
        E_dinv = sb("e_dinv", [128, 2048], F32, OFF_E + 24576)
        E_li = [sb(f"e_li_{b}", [128, 2048], BF16, OFF_E + 24576 + b * 4096) for b in range(2)]
        D_qt = [sb(f"d_qt_{k}", [128, 1024], BF16, OFF_D + k * 2048) for k in range(16)]
        D_xo = [sb(f"d_xo_{b}", [128, 2048], F32, OFF_D + b * 8192) for b in range(2)]
        D_x2 = [sb(f"d_x2_{b}", [128, 2048], F32, OFF_D + 16384 + b * 8192) for b in range(2)]

        psf_h = e(nc.psum_tensor("psf", [128, 4096], F32))
        psf = psf_h.ap()
        psb_h = nc.virtual_psum_tensor("psb", [128, 8192], BF16)
        nc.back_virtual_tensor_with(psb_h, psf_h)
        psb = psb_h.ap()

        def bkf(k, n=1):
            return psf[:, k * 512:(k + n) * 512]

        def bkb(k, n=1):
            return psb[:, k * 1024:(k + n) * 1024]

        def sem(name):
            return e(nc.semaphore(name))
        s_gen = sem("s_gen"); s_vecp = sem("s_vecp"); s_vecd = sem("s_vecd")
        s_flag = sem("s_flag"); s_poll = sem("s_poll"); s_st = sem("s_st")
        s_lnr = sem("s_lnr"); s_ln = sem("s_ln"); s_mm = sem("s_mm")
        s_ev = sem("s_ev"); s_ev2 = sem("s_ev2"); s_sq = sem("s_sq")
        s_kst = sem("s_kst"); s_pe2 = sem("s_pe2"); s_exp = sem("s_exp")
        s_acc = sem("s_acc"); s_dve2 = sem("s_dve2"); s_ds = sem("s_ds")
        s_dbg = sem("s_dbg")
        s_chp = sem("s_chp"); s_chv = sem("s_chv"); s_cha = sem("s_cha")
        s_acc2 = sem("s_acc2"); s_ds2 = sem("s_ds2")

        def chain(eng, semh):
            last = [None]
            n = [0]
            def mark(inst):
                n[0] += 1
                inst.then_inc(semh, 1)
                last[0] = (semh, n[0])
                return inst
            def markpub(inst, pub, val):
                last[0] = (pub, val)
                return inst
            def wait():
                if last[0] is not None:
                    eng.wait_ge(last[0][0], last[0][1])
            return mark, markpub, wait

        class DS:
            def __init__(self, name, nsem):
                self.sems = [e(nc.semaphore(f"{name}{i}")) for i in range(nsem)]
                self.n = nsem
                self.j = 0
            def throttle(self, eng):
                j = self.j
                if j >= self.n:
                    eng.wait_ge(self.sems[j % self.n], 16 * (j // self.n))
            def issue(self, eng, inst):
                i = self.j % self.n
                inst.then_inc(self.sems[i], 16)
                self.j += 1
                return inst
            def wait(self, eng, j):
                eng.wait_ge(self.sems[j % self.n], 16 * (j // self.n + 1))
            def wait_all(self, eng, count):
                for i in range(min(self.n, count)):
                    cnt_i = (count - 1 - i) // self.n + 1
                    eng.wait_ge(self.sems[i], 16 * cnt_i)
            def wait_range(self, eng, j0, n):
                seen = {}
                for j in range(j0, j0 + n):
                    seen[j % self.n] = j // self.n + 1
                for i, cnt in seen.items():
                    eng.wait_ge(self.sems[i], 16 * cnt)

        ds_vl = DS("dvl", 2)
        ds_xh = DS("dxh", 2)
        ds_nxw = DS("dnw", 2)
        ds_cw = DS("dcw", 7)
        ds_bf = DS("dbf", 1)
        ds_wt = DS("dwt", 4)
        ds_nt = DS("dnt", 8)
        ds_ko = DS("dko", 4)
        ds_vv = DS("dvv", 4)
        ds_kss = DS("dks", 1)
        ds_ksf = DS("dkf", 1)
        ds_kt = DS("dkt", 2)
        ds_cp = DS("dcp", 2)
        ds_vp = DS("dvp", 2)
        ds_dbo = DS("ddb", 1)
        ds_dbi = DS("ddi", 2)
        ds_out = DS("dou", 2)
        ds_x2 = DS("dx2", 2)
        ds_wb = DS("dwb", 4)

        block = e(nc.Block())

        def shsl(t, sideval, sub):
            per = int(np.prod(t.shape[1:]))
            return dataclasses.replace(sub, offset=sideval * per + sub.offset)

        def dynoff(ap, extra):
            return dataclasses.replace(ap, offset=ap.offset + extra)

        @block.sync
        def _(eng):
            my = nc.partition_id([ET.SP])
            side = my % 2
            koff = side * HALF
            voff = side * (HALF * D)

            ds_vl.issue(eng, eng.dma_start(out=nonce_sb[:], in_=nonce[:]))
            for i, (nm, rows) in enumerate(VECS):
                if i > 0:
                    eng.wait_ge(s_vecp, i)
                ds_vl.throttle(eng)
                ds_vl.issue(eng, eng.dma_start(
                    out=vrow[0:rows, :],
                    in_=vec_in[nm].rearrange("o (r c) -> (o r) c", c=128)))

            def _nx_write(j):
                eng.wait_ge(s_ln, j + 1)
                dst = (nxh_d[j * 128:(j + 1) * 128, :] if j < NTH
                       else nxo_d[(j - NTH) * 128:(j - NTH + 1) * 128, :])
                ds_nxw.throttle(eng)
                ds_nxw.issue(eng, eng.dma_start(out=dst, in_=E_nx[j % 2]))

            def _dbg_write(j):
                if STAGE != 1 or j < NTH:
                    return
                jo = j - NTH
                eng.wait_ge(s_dbg, jo + 1)
                ds_dbo.throttle(eng)
                ds_dbo.issue(eng, eng.dma_start(
                    out=dbg[jo * 128:(jo + 1) * 128, :], in_=E_dbg[:]))

            ds_vl.throttle(eng)
            ds_vl.issue(eng, eng.dma_start(out=brow[:], in_=vec_in["bv"][:, :]))
            for i in range(NT_ALL):
                if i >= 2:
                    eng.wait_ge(s_ln, i - 1)
                srct = (x_half[i * 128:(i + 1) * 128, :] if i < NTH
                        else x_own[(i - NTH) * 128:(i - NTH + 1) * 128, :])
                ds_xh.throttle(eng)
                ds_xh.issue(eng, eng.dma_start(out=E_xh[i % 2], in_=srct))
                if i >= 2:
                    _nx_write(i - 2)
                    _dbg_write(i - 2)
            _nx_write(NT_ALL - 2)
            _dbg_write(NT_ALL - 2)
            _nx_write(NT_ALL - 1)
            _dbg_write(NT_ALL - 1)

            if STAGE == 1:
                ds_dbo.wait_all(eng, NTO)
                return

            eng.wait_ge(s_flag, 1)
            cast_wait(ds_cw, eng, "wk")
            for k in range(16):
                ds_wt.throttle(eng)
                ds_wt.issue(eng, eng.dma_start(
                    out=A_w1[k], in_=wbf["wk"][:, k * 128:(k + 1) * 128], transpose=True))
            for blk in range(NBK):
                bb = blk % 2
                if blk >= 2:
                    eng.wait_ge(s_mm, (blk - 2) * 16 + 16)
                ds_nxw.wait_all(eng, min(4 * (blk + 1), NTH))
                for k in range(16):
                    ds_nt.throttle(eng)
                    ds_nt.issue(eng, eng.dma_start(
                        out=B_nt[bb][k],
                        in_=nxh_d[blk * 512:(blk + 1) * 512, k * 128:(k + 1) * 128],
                        transpose=True))
                for dout in range(16):
                    g = blk * 16 + dout
                    eng.wait_ge(s_sq, g + 1)
                    ds_ko.throttle(eng)
                    ds_ko.issue(eng, eng.dma_start(
                        out=kloc_d[dout][:, blk * 512:(blk + 1) * 512],
                        in_=C_ko[dout]))
            eng.wait_ge(s_mm, MM_K)
            eng.wait_ge(s_flag, 2)
            cast_wait(ds_cw, eng, "wv")
            for k in range(16):
                ds_wt.throttle(eng)
                ds_wt.issue(eng, eng.dma_start(
                    out=A_w1[k], in_=wbf["wv"][:, k * 128:(k + 1) * 128], transpose=True))
            for blk in range(NBK):
                bb = blk % 2
                if blk >= 2:
                    eng.wait_ge(s_mm, MM_K + (blk - 2) * 16 + 16)
                for k in range(16):
                    ds_nt.throttle(eng)
                    ds_nt.issue(eng, eng.dma_start(
                        out=B_nt[bb][k],
                        in_=nxh_d[blk * 512:(blk + 1) * 512, k * 128:(k + 1) * 128],
                        transpose=True))
                for ts in range(4):
                    eng.wait_ge(s_ev2, blk * 16 + ts * 4 + 4)
                    ds_vv.throttle(eng)
                    ds_vv.issue(eng, eng.dma_start(
                        out=vsh_d[blk * 512 + ts * 128:blk * 512 + (ts + 1) * 128, :]
                        if False else
                        vloc_d[blk * 512 + ts * 128:blk * 512 + (ts + 1) * 128, :],
                        in_=C_vo[ts]))
            eng.wait_ge(s_kst, 8)
            ds_kss.issue(eng, eng.dma_start(
                out=shsl(kstat_d, side, kstat_d[0][:, :]), in_=kmaxsq[:]))
            ds_ko.wait_all(eng, NBK * 16)
            ds_vv.wait_all(eng, NBK * 4)
            for p in range(4):
                ds_cp.throttle(eng)
                ds_cp.issue(eng, eng.dma_start(
                    out=dynoff(ksh_d[4 * p:4 * p + 4][:, :, 0:HALF], koff),
                    in_=kloc_d[4 * p:4 * p + 4]))
            for p in range(2):
                ds_cp.throttle(eng)
                ds_cp.issue(eng, eng.dma_start(
                    out=dynoff(vsh_d[p * 2048:(p + 1) * 2048, :], voff),
                    in_=vloc_d[p * 2048:(p + 1) * 2048, :]))
            eng.wait_ge(s_flag, 3)
            cast_wait(ds_cw, eng, "wq")
            eng.wait_ge(s_mm, MM_V)
            for k in range(16):
                ds_wt.throttle(eng)
                ds_wt.issue(eng, eng.dma_start(
                    out=A_w1[k], in_=wbf["wq"][:, k * 128:(k + 1) * 128], transpose=True))
            ds_nxw.wait_all(eng, NT_ALL)
            for k in range(16):
                ds_nt.throttle(eng)
                ds_nt.issue(eng, eng.dma_start(
                    out=B_ntq[k], in_=nxo_d[:, k * 128:(k + 1) * 128], transpose=True))

            if STAGE == 2:
                for k in range(16):
                    eng.wait_ge(s_dbg, k + 1)
                    ds_dbo.throttle(eng)
                    ds_dbo.issue(eng, eng.dma_start(
                        out=dbgk[k * 128:(k + 1) * 128, :], in_=E_dbg[:, 0:1024]))
                eng.wait_ge(s_flag, 4)
                ds_dbi.issue(eng, eng.dma_start(out=E_pp[0], in_=ksh_d[0][:, 0:2048]))
                ds_dbi.issue(eng, eng.dma_start(out=E_pp[1], in_=vsh_d[0:128, :]))
                for j, r0 in enumerate([0, 128, 256]):
                    eng.wait_ge(s_dbg, 17 + j)
                    ds_dbo.throttle(eng)
                    ds_dbo.issue(eng, eng.dma_start(out=dbg[r0:r0 + 128, :], in_=E_dbg[:]))
                ds_dbo.wait_all(eng, 19)
                return

            eng.wait_ge(s_flag, 4)
            for p in range(8):
                if p < 2:
                    eng.wait_ge(s_mm, MM_Q)
                else:
                    eng.wait_ge(s_pe2, PE2_DB(p - 1))
                for e2 in range(2):
                    ds_kt.throttle(eng)
                    ds_kt.issue(eng, eng.dma_start(
                        out=A_kt[(p % 2) * 2 + e2], in_=ksh_d[2 * p + e2]))
                for g in range(8):
                    vg = 8 * p + g
                    if vg >= 2:
                        q2, g2 = divmod(vg - 2, 8)
                        eng.wait_ge(s_mm, MM_PV(q2, 8 * g2 + 7))
                    ds_vp.throttle(eng)
                    ds_vp.issue(eng, eng.dma_start(
                        out=E_vp[vg % 2].rearrange("p (c d) -> p c d", c=8),
                        in_=vsh_d[g * 1024:(g + 1) * 1024,
                                  2 * p * 128:(2 * p + 2) * 128].rearrange(
                                      "(c p) d -> p c d", p=128)))

            if STAGE == 3:
                for k in range(16):
                    eng.wait_ge(s_dbg, k + 1)
                    ds_dbo.throttle(eng)
                    ds_dbo.issue(eng, eng.dma_start(
                        out=dbgk[k * 128:(k + 1) * 128, :], in_=B_res[0][:, 0:1024]))
                ds_dbo.wait_all(eng, 16)
                return

            eng.wait_ge(s_flag, 5)
            cast_wait(ds_cw, eng, "wo")
            eng.wait_ge(s_pe2, PE2_P3)
            ds_dbi.throttle(eng)
            ds_dbi.issue(eng, eng.dma_start(out=bo_row[:], in_=vec_in["bo"]))
            ds_dbi.throttle(eng)
            ds_dbi.issue(eng, eng.dma_start(out=sdpc_row[:], in_=vec_in["wsc_dp"]))
            for k in range(16):
                ds_wt.throttle(eng)
                ds_wt.issue(eng, eng.dma_start(
                    out=A_w1[k], in_=wbf["wo"][:, k * 128:(k + 1) * 128], transpose=True))

            def _x2w(j):
                eng.wait_ge(s_ln, LN_P4 + j + 1)
                ds_nxw.throttle(eng)
                ds_nxw.issue(eng, eng.dma_start(
                    out=x2_d[j * 128:(j + 1) * 128, :], in_=D_x2[j % 2]))

            for i in range(8):
                if i >= 2:
                    eng.wait_ge(s_ln, LN_P4 + i - 1)
                ds_x2.throttle(eng)
                ds_x2.issue(eng, eng.dma_start(
                    out=D_xo[i % 2], in_=x_own[i * 128:(i + 1) * 128, :]))
                if i >= 2:
                    _x2w(i - 2)
            _x2w(6)
            _x2w(7)

            if STAGE == 4:
                ds_nxw.wait_all(eng, NT_ALL + 8)
                ds_dbo.issue(eng, eng.dma_start(out=dbg[:, :], in_=x2_d[:, :]))
                ds_dbo.wait_all(eng, 1)
                return

            for hf in range(2):
                ds_nxw.wait_all(eng, NT_ALL + 8)
                if hf == 1:
                    eng.wait_ge(s_ln, LN5(0) + 4)
                for tt in range(4):
                    ds_x2.throttle(eng)
                    ds_x2.issue(eng, eng.dma_start(
                        out=B_res[tt],
                        in_=x2_d[hf * 512 + tt * 128:hf * 512 + (tt + 1) * 128, :]))
                for wi, wnm in enumerate(["wup", "wgate"]):
                    eng.wait_ge(s_flag, 6 + wi)
                    cast_wait(ds_cw, eng, wnm)
                    for B in range(8):
                        jbase = hf * 384 + wi * 128 + B * 16
                        if B >= 2:
                            eng.wait_ge(s_mm, MM5(hf) + 16 + wi * 64 + (B - 2) * 8 + 8)
                        elif hf == 1 or wi == 1:
                            prev = MM5(hf) + 16 + wi * 64 if wi == 1 else MM5(hf - 1) + 336
                            eng.wait_ge(s_mm, prev)
                        else:
                            eng.wait_ge(s_mm, MM_WO)
                        for k in range(16):
                            ds_wb.throttle(eng)
                            ds_wb.issue(eng, eng.dma_start(
                                out=A_wb[B % 2][k],
                                in_=wbf[wnm][B * 1024:(B + 1) * 1024,
                                             k * 128:(k + 1) * 128], transpose=True))
                eng.wait_ge(s_flag, 8)
                cast_wait(ds_cw, eng, "wdown")
                for dh2 in range(2):
                    for k2 in range(64):
                        kk = dh2 * 64 + k2
                        if kk >= 2:
                            eng.wait_ge(s_mm, MM5(hf) + 208 + kk - 1)
                        else:
                            eng.wait_ge(s_mm, MM5(hf) + 144)
                        ds_wb.throttle(eng)
                        ds_wb.issue(eng, eng.dma_start(
                            out=A_wd[k2 % 2],
                            in_=wbf["wdown"][dh2 * 1024:(dh2 + 1) * 1024,
                                             k2 * 128:(k2 + 1) * 128], transpose=True))
                for tt in range(4):
                    eng.wait_ge(s_ln, LN5(hf) + tt + 1)
                    ds_out.throttle(eng)
                    ds_out.issue(eng, eng.dma_start(
                        out=out[hf * 512 + tt * 128:hf * 512 + (tt + 1) * 128, :],
                        in_=P5_out[tt % 2]))
            ds_out.wait_all(eng, 8)

        @block.gpsimd
        def _(eng):
            my = nc.partition_id([ET.Pool])
            side = my % 2
            bud = 1 - side
            pmark, pmarkpub, pwait = chain(eng, s_chp)

            def dynoff_p(ap, extra):
                return dataclasses.replace(ap, offset=ap.offset + extra)

            if STAGE >= 2:
                for nm in CAST_ORDER:
                    rows, cols = w_in[nm].shape
                    l2 = rows * cols // 2
                    half_off = side * l2
                    wf_o = wbf[nm].rearrange("(o a) b -> o (a b)", o=1)
                    wf_i = w_in[nm].rearrange("(o a) b -> o (a b)", o=1)
                    ds_cw.throttle(eng)
                    ds_cw.issue(eng, eng.dma_start(
                        out=dynoff_p(wf_o[0:1, 0:l2], half_off),
                        in_=dynoff_p(wf_i[0:1, 0:l2], half_off)))

            for op in [lambda: eng.memset(identf[:], 0.0),
                       lambda: eng.affine_select(out=identf[:], in_=identf[:],
                                                 compare_op=ALU.not_equal, fill=1.0,
                                                 base=0, pattern=[[-1, 128]],
                                                 channel_multiplier=1),
                       lambda: eng.iota(iota_i[:], pattern=[[0, 128]], base=0,
                                        channel_multiplier=1),
                       lambda: eng.iota(iota_j[:], pattern=[[1, 128]], base=0,
                                        channel_multiplier=0),
                       lambda: eng.memset(ones128[:], 1.0),
                       lambda: eng.memset(ones128f[:], 1.0),
                       lambda: eng.memset(onesrowf[:], 1.0),
                       lambda: eng.memset(cst[:, 0:1], EPS),
                       lambda: eng.memset(kmaxsq[:], 0.0)]:
                pmark(op())
                pwait()
            eng.memset(cst[:, 1:2], 0.0).then_inc(s_gen, 1)

            if STAGE < 2:
                return

            ds_vl.wait(eng, 0)
            FLAG_OF = {"wk": FL_WK, "wq": FL_WV, "wdown": FL_WO}
            for nm in CAST_ORDER:
                if nm not in FLAG_OF:
                    continue
                cast_wait(ds_cw, eng, nm)
                if nm == "wq":
                    cast_wait(ds_cw, eng, "wv")
                if nm == "wdown":
                    for other in ("wo", "wup", "wgate"):
                        cast_wait(ds_cw, eng, other)
                ds_bf.throttle(eng)
                ds_bf.issue(eng, eng.dma_start(
                    out=shsl(flags_d, side, flags_d[0][FLAG_OF[nm]:FLAG_OF[nm] + 1, :]),
                    in_=nonce_sb[:]))

            with eng.register("want") as want, eng.register("go") as go, \
                 eng.register("cur") as cur, eng.register("itr") as itr:
                eng.reg_load(want, nonce_sb[0:1, 0:1])
                eng.reg_mov(itr, 0)

                def poll(fl):
                    eng.reg_mov(go, 1)
                    with eng.While(go):
                        eng.reg_alu(itr, itr, 16, ALU.add)
                        eng.dma_start(out=flag_sb[:],
                                      in_=shsl(flags_d, bud, flags_d[0][fl:fl + 1, :])
                                      ).then_inc(s_poll, 16)
                        eng.wait_ge(s_poll, itr)
                        eng.reg_load(cur, flag_sb[0:1, 0:1])
                        eng.reg_alu(go, cur, want, ALU.not_equal)
                    eng.reg_alu(go, go, 0, ALU.add).then_inc(s_flag, 1)

                poll(FL_WK)
                poll(FL_WV)
                eng.reg_alu(go, go, 0, ALU.add).then_inc(s_flag, 1)
                ds_cp.wait_all(eng, 6)
                ds_kss.wait(eng, 0)
                ds_bf.throttle(eng)
                ds_bf.issue(eng, eng.dma_start(
                    out=shsl(flags_d, side, flags_d[0][FL_KV:FL_KV + 1, :]),
                    in_=nonce_sb[:]))
                poll(FL_KV)
                ds_ksf.issue(eng, eng.dma_start(
                    out=kst_sb[:], in_=shsl(kstat_d, bud, kstat_d[0][:, :])))
                if STAGE >= 3:
                    for p3 in range(8):
                        if p3 > 0:
                            eng.wait_ge(s_acc, 44 * p3)
                        first3 = True
                        for C3 in range(64):
                            if ACC_ENG[C3] == "d":
                                continue
                            eng.wait_ge(s_exp, S_EXP(p3, C3))
                            pwait()
                            if first3:
                                first3 = False
                                pmarkpub(eng.tensor_copy(
                                    acc_b[:], E_pp[C3 % 4][:]
                                    ).then_inc(s_acc2, 1), s_acc2, S_ACCP(p3, C3))
                            else:
                                pmarkpub(eng.tensor_tensor(
                                    acc_b[:], acc_b[:], E_pp[C3 % 4][:],
                                    ALU.add).then_inc(s_acc2, 1), s_acc2, S_ACCP(p3, C3))
                if STAGE >= 4:
                    poll(FL_WO)
                if STAGE >= 5:
                    for _x5 in range(3):
                        eng.reg_alu(go, go, 0, ALU.add).then_inc(s_flag, 1)

        @block.tensor
        def _(eng):
            eng.wait_ge(s_gen, 1)
            for i, (nm, rows) in enumerate(VECS):
                ds_vl.wait(eng, i + 1)
                if i > 0:
                    eng.wait_ge(s_vecd, i)
                eng.transpose(psf[0:128, 6 * 512:6 * 512 + rows], vrow[0:rows, :],
                              identf[0:rows, 0:rows]).then_inc(s_vecp, 1)
            ds_vl.wait(eng, NV + 1)
            eng.wait_ge(s_vecd, NV)
            for c in range(16):
                inst = eng.matmul(psf[:, c * 128:(c + 1) * 128],
                                  onesrowf[:, :], brow[0:1, c * 128:(c + 1) * 128],
                                  start=(c % 4 == 0), stop=(c % 4 == 3),
                                  skip_group_check=True)
            inst.then_inc(s_vecp, 1)

            if STAGE < 2:
                return

            ds_wt.wait_all(eng, 16)
            eng.wait_ge(s_vecd, NV + 2)
            for blk in range(NBK):
                bb = blk % 2
                ds_nt.wait_all(eng, 16 * (blk + 1))
                if blk >= 1:
                    eng.wait_ge(s_kst, blk)
                for dout in range(16):
                    g = blk * 16 + dout
                    if g >= 4:
                        eng.wait_ge(s_ev, g - 3)
                    for k in range(16):
                        inst = eng.matmul(bkf(dout % 4),
                                          A_w1[k][:, dout * 128:(dout + 1) * 128],
                                          B_nt[bb][k], start=(k == 0), stop=(k == 15))
                    inst.then_inc(s_mm, 1)
                    eng.wait_ge(s_sq, g + 1)
                    inst = eng.matmul(psf[0:1, 4 * 512:5 * 512], ones128[:, 0:1],
                                      E_ksq[dout % 2],
                                      start=(dout == 0), stop=(dout == 15))
                inst.then_inc(s_pe2, 1)
            ds_wt.wait_all(eng, 32)
            for blk in range(NBK):
                bb = blk % 2
                ds_nt.wait_all(eng, 128 + 16 * (blk + 1))
                for ts in range(4):
                    for ns in range(4):
                        g = blk * 16 + ts * 4 + ns
                        if g >= 4:
                            eng.wait_ge(s_ev2, g - 3)
                        for k in range(16):
                            inst = eng.matmul(bkf(g % 4),
                                              B_nt[bb][k][:, ts * 128:(ts + 1) * 128],
                                              A_w1[k][:, ns * 512:(ns + 1) * 512],
                                              start=(k == 0), stop=(k == 15))
                        inst.then_inc(s_mm, 1)
            ds_wt.wait_all(eng, 48)
            ds_nt.wait_all(eng, 256 + 16)
            for dout in range(16):
                if dout >= 2:
                    eng.wait_ge(s_ev, EV_K + dout - 1)
                b0 = (dout % 2) * 2
                for k in range(16):
                    eng.matmul(bkf(b0), A_w1[k][:, dout * 128:(dout + 1) * 128],
                               B_ntq[k][:, 0:512], start=(k == 0), stop=(k == 15))
                for k in range(16):
                    inst = eng.matmul(bkf(b0 + 1), A_w1[k][:, dout * 128:(dout + 1) * 128],
                                      B_ntq[k][:, 512:1024], start=(k == 0), stop=(k == 15))
                inst.then_inc(s_mm, 1)
            for h in range(16):
                eng.wait_ge(s_sq, EV_K + h + 1)
                if h >= 1:
                    eng.wait_ge(s_kst, 8 + h)
                eng.matmul(psf[0:1, 4 * 512:5 * 512], ones128[:, 0:1], E_qsq[:, 0:512],
                           start=True, stop=True)
                eng.matmul(psf[0:1, 5 * 512:6 * 512], ones128[:, 0:1], E_qsq[:, 512:1024],
                           start=True, stop=True).then_inc(s_pe2, 1)
            eng.wait_ge(s_dve2, 1)
            eng.matmul(psf[:, 6 * 512:6 * 512 + 8], onesrowf[:, :], cpair[:, :],
                       start=True, stop=True).then_inc(s_pe2, 1)

            if STAGE < 3:
                return

            def pv_group(p, Cp):
                vg = 8 * p + Cp // 8
                if Cp % 8 == 0:
                    ds_vp.wait(eng, vg)
                st, sp = (Cp == 0), (Cp == 63)
                vpb = E_vp[vg % 2]
                sl = (Cp % 8) * 256
                ppb = E_pp[Cp % 4]
                eng.matmul(bkf(4), vpb[:, sl:sl + 128], ppb[:, 0:512], start=st, stop=sp)
                eng.matmul(bkf(5), vpb[:, sl:sl + 128], ppb[:, 512:1024],
                           start=st, stop=sp)
                eng.matmul(bkf(6), vpb[:, sl + 128:sl + 256], ppb[:, 1024:1536],
                           start=st, stop=sp)
                eng.matmul(bkf(7), vpb[:, sl + 128:sl + 256], ppb[:, 1536:2048],
                           start=st, stop=sp).then_inc(s_mm, 1)

            for p in range(8):
                kta = A_kt[(p % 2) * 2]
                ktb = A_kt[(p % 2) * 2 + 1]
                for C in range(64):
                    eng.wait_ge(s_exp, S_EXP(p, C) - 1)
                    if C == 0:
                        eng.wait_ge(s_dve2, S_DVE2_SW(p) - 1)
                        ds_kt.wait(eng, 2 * p)
                        ds_kt.wait(eng, 2 * p + 1)
                    eng.matmul(bkf(0), kta[:, C * 128:(C + 1) * 128], D_qt[2 * p][:, 0:512],
                               start=True, stop=True)
                    eng.matmul(bkf(1), kta[:, C * 128:(C + 1) * 128],
                               D_qt[2 * p][:, 512:1024], start=True, stop=True)
                    eng.matmul(bkf(2), ktb[:, C * 128:(C + 1) * 128],
                               D_qt[2 * p + 1][:, 0:512], start=True, stop=True)
                    eng.matmul(bkf(3), ktb[:, C * 128:(C + 1) * 128],
                               D_qt[2 * p + 1][:, 512:1024],
                               start=True, stop=True).then_inc(s_pe2, 1)
                    if C >= 1:
                        pv_group(p, C - 1)
                eng.wait_ge(s_exp, S_EXP(p, 63))
                pv_group(p, 63)
                eng.wait_ge(s_acc, 44 * p + 44)
                eng.matmul(bkf(0), ones128f[:, :], E_acc[:, 0:512], start=True, stop=True)
                eng.matmul(bkf(1), ones128f[:, :], E_acc[:, 512:1024], start=True, stop=True)
                eng.matmul(bkf(2), ones128f[:, :], E_acc[:, 1024:1536], start=True, stop=True)
                eng.matmul(bkf(3), ones128f[:, :], E_acc[:, 1536:2048],
                           start=True, stop=True).then_inc(s_pe2, 1)

            if STAGE < 4:
                return

            ds_wt.wait_all(eng, 64)
            eng.wait_ge(s_dve2, 10)
            ds_dbi.wait(eng, 0)
            for i in range(8):
                if i >= 2:
                    eng.wait_ge(s_dve2, 10 + i - 1)
                b0 = (i % 2) * 4
                for dseg in range(4):
                    for k in range(16):
                        inst = eng.matmul(bkf(b0 + dseg),
                                          C_ao[k][:, i * 128:(i + 1) * 128],
                                          A_w1[k][:, dseg * 512:(dseg + 1) * 512],
                                          start=(k == 0), stop=False,
                                          skip_group_check=True)
                    if dseg == 3:
                        inst.then_inc(s_mm, 1)
                    inst = eng.matmul(bkf(b0 + dseg), onesrowf[:, :],
                                      bo_row[0:1, dseg * 512:(dseg + 1) * 512],
                                      start=False, stop=True, skip_group_check=True)
                    if dseg == 3:
                        inst.then_inc(s_pe2, 1)

            if STAGE < 5:
                return

            eng.wait_ge(s_dve2, 18)
            ds_dbi.wait(eng, 1)
            for cc in range(16):
                inst = eng.matmul(psf[:, cc * 128:(cc + 1) * 128], onesrowf[:, :],
                                  sdpc_row[0:1, cc * 128:(cc + 1) * 128],
                                  start=(cc % 4 == 0), stop=(cc % 4 == 3),
                                  skip_group_check=True)
            inst.then_inc(s_pe2, 1)

            for hf in range(2):
                for tt in range(4):
                    eng.wait_ge(s_ds, DS5(hf) + tt + 1)
                    if tt >= 1:
                        eng.wait_ge(s_ev, EV5(hf) + tt)
                    elif hf == 1:
                        eng.wait_ge(s_dve2, DVE2_L4(0, 3))
                    dst = bkf(4, 4)
                    for cc in range(16):
                        inst = eng.matmul(dst[:, cc * 128:(cc + 1) * 128],
                                          C_n3f[:, cc * 128:(cc + 1) * 128],
                                          identf[:, :], is_transpose=True,
                                          start=(cc % 4 == 0), stop=(cc % 4 == 3),
                                          skip_group_check=True)
                    inst.then_inc(s_pe2, 1)
                eng.wait_ge(s_dve2, DVE2_H16(hf))
                _, sl_map = butterfly_plan(16)
                for cc in range(16):
                    if cc >= 4:
                        eng.wait_ge(s_ev, EV5(hf) + 4 + cc - 3)
                    eng.matmul(bkf(cc % 4), h128[:, :], C_xt[sl_map[cc]],
                               start=True, stop=True).then_inc(s_mm, 1)
                for wi in range(2):
                    for B in range(8):
                        ds_wb.wait_range(eng, hf * 384 + wi * 128 + B * 16, 16)
                        for c8 in range(8):
                            g5 = B * 8 + c8
                            if wi == 0 and B == 0 and c8 < 4:
                                eng.wait_ge(s_ev, EV5(hf) + 20)
                            else:
                                gg = wi * 64 + g5
                                if gg >= 4:
                                    eng.wait_ge(s_ev, EV5(hf) + 20 + gg - 3)
                            for k in range(16):
                                inst = eng.matmul(bkf(c8 % 4),
                                                  A_wb[B % 2][k][:, c8 * 128:(c8 + 1) * 128],
                                                  C_xt[sl_map[k]],
                                                  start=(k == 0), stop=(k == 15))
                            inst.then_inc(s_mm, 1)
                eng.wait_ge(s_dve2, DVE2_H64(hf))
                _, slz = butterfly_plan(64)

                def zph(sl5):
                    return Z[sl5] if sl5 < 64 else C_xt[16]

                for cc in range(64):
                    if cc >= 4:
                        eng.wait_ge(s_ev, EV5(hf) + 148 + cc - 3)
                    eng.matmul(bkf(cc % 4), h128[:, :], zph(slz[cc]),
                               start=True, stop=True).then_inc(s_mm, 1)
                for dh2 in range(2):
                    for k2 in range(64):
                        kk = dh2 * 64 + k2
                        ds_wb.wait(eng, hf * 384 + 256 + kk)
                        if k2 == 0:
                            eng.wait_ge(s_ev, EV5(hf) + 148 + 64)
                            if dh2 == 1:
                                eng.wait_ge(s_ev, EV5(hf) + 212 + 4)
                        for tt in range(4):
                            for ds2 in range(2):
                                inst = eng.matmul(
                                    bkf(tt * 2 + ds2),
                                    zph(slz[k2])[:, tt * 128:(tt + 1) * 128],
                                    A_wd[k2 % 2][:, ds2 * 512:(ds2 + 1) * 512],
                                    start=(k2 == 0), stop=(k2 == 63),
                                    skip_group_check=True)
                        inst.then_inc(s_mm, 1)

        @block.scalar
        def _(eng):
            amark, amarkpub, await_ = chain(eng, s_cha)
            for i in range(NT_ALL):
                eng.wait_ge(s_st, i + 1)
                amark(eng.activation(ln_mv[:, 1:2], ln_mv[:, 1:2], AF.Ln,
                                     bias=cst[:, 0:1], scale=1.0))
                await_()
                eng.activation(ln_r[:], ln_mv[:, 1:2], AF.Exp, bias=cst[:, 1:2],
                               scale=-0.5).then_inc(s_lnr, 1)

            if STAGE < 2:
                return

            for blk in range(NBK):
                for dout in range(16):
                    g = blk * 16 + dout
                    eng.wait_ge(s_mm, g + 1)
                    if blk >= 1:
                        ds_ko.wait(eng, (blk - 1) * 16 + dout)
                        eng.wait_ge(s_sq, g - 16 + 1)
                    eng.activation(C_ko[dout], bkf(dout % 4), AF.Identity,
                                   bias=bkc[:, dout:dout + 1], scale=1.0).then_inc(s_ev, 1)
            for dout in range(16):
                eng.wait_ge(s_mm, MM_V + dout + 1)
                amark(eng.activation(D_qt[dout][:, 0:512], bkf((dout % 2) * 2), AF.Identity,
                                     bias=bqc[:, dout:dout + 1], scale=RSQ_DH))
                await_()
                eng.activation(D_qt[dout][:, 512:1024], bkf((dout % 2) * 2 + 1), AF.Identity,
                               bias=bqc[:, dout:dout + 1], scale=RSQ_DH).then_inc(s_ev, 1)
            eng.wait_ge(s_kst, 24)
            eng.wait_ge(s_ds, 1)
            amark(eng.activation(cwork[:], cwork[:], AF.Ln, bias=cst[0:1, 1:2], scale=1.0))
            await_()
            eng.activation(cwork[:], cwork[:], AF.Exp, bias=cst[0:1, 1:2],
                           scale=0.5).then_inc(s_lnr, 1)

            if STAGE < 3:
                return

            eng.wait_ge(s_dve2, 2)
            for p in range(8):
                for C in range(64):
                    eng.wait_ge(s_pe2, PE2_S(p, C))
                    if C >= 4:
                        eng.wait_ge(s_mm, MM_PV(p, C - 4))
                        if ACC_ENG[C - 4] == "d":
                            eng.wait_ge(s_acc, S_ACCD(p, C - 4))
                        else:
                            eng.wait_ge(s_acc2, S_ACCP(p, C - 4))
                    eng.activation(E_pp[C % 4], psf[:, 0:2048], AF.Exp,
                                   bias=negCb[:, p:p + 1],
                                   scale=1.0).then_inc(s_exp, 1)

            if STAGE < 4:
                return

            for i in range(8):
                eng.wait_ge(s_st, ST_P4 + i + 1)
                amark(eng.activation(ln_mv[:, 1:2], ln_mv[:, 1:2], AF.Ln,
                                     bias=cst[:, 0:1], scale=1.0))
                await_()
                eng.activation(ln_r[:], ln_mv[:, 1:2], AF.Exp, bias=cst[:, 1:2],
                               scale=-0.5).then_inc(s_lnr, 1)

            if STAGE < 5:
                return

            eng.wait_ge(s_pe2, 554)
            eng.activation(sdpc_bb[:], bkf(0, 4), AF.Identity, bias=cst[:, 1:2],
                           scale=64.0).then_inc(s_ev, 1)
            for hf in range(2):
                _, sl_map = butterfly_plan(16)
                _, slz = butterfly_plan(64)
                for tt in range(4):
                    eng.wait_ge(s_st, ST5(hf) + tt + 1)
                    amark(eng.activation(ln_mv[:, 1:2], ln_mv[:, 1:2], AF.Ln,
                                         bias=cst[:, 0:1], scale=1.0))
                    await_()
                    eng.activation(ln_r[:], ln_mv[:, 1:2], AF.Exp, bias=cst[:, 1:2],
                                   scale=-0.5).then_inc(s_lnr, 1)
                    eng.wait_ge(s_pe2, PE25(hf) + tt + 1)
                    for cc in range(16):
                        src2 = bkf(4, 4)[:, cc * 128:(cc + 1) * 128]
                        inst = eng.activation(C_xt[cc][:, tt * 128:(tt + 1) * 128], src2,
                                              AF.Identity, bias=cst[:, 1:2],
                                              scale=su1c[:, cc:cc + 1])
                        if cc == 15:
                            inst.then_inc(s_ev, 1)
                        else:
                            amark(inst)
                            await_()
                for cc in range(16):
                    eng.wait_ge(s_mm, MM5(hf) + cc + 1)
                    eng.activation(C_xt[sl_map[cc]], bkf(cc % 4), AF.Identity,
                                   bias=cst[:, 1:2],
                                   scale=1.0).then_inc(s_ev, 1)
                if hf == 1:
                    ds_out.wait_range(eng, 0, 4)
                for wi in range(2):
                    for B in range(8):
                        for c8 in range(8):
                            g5 = B * 8 + c8
                            eng.wait_ge(s_mm, MM5(hf) + 16 + wi * 64 + g5 + 1)
                            cgl = B * 8 + c8
                            if wi == 0:
                                eng.activation(Z[cgl], bkf(c8 % 4), AF.Identity,
                                               bias=cst[:, 1:2],
                                               scale=supc[:, cgl:cgl + 1]
                                               ).then_inc(s_ev, 1)
                            else:
                                if g5 >= 2:
                                    eng.wait_ge(s_ev2, EV25(hf) + g5 - 1)
                                eng.activation(C_sg[c8 % 2], bkf(c8 % 4), AF.Silu,
                                               bias=cst[:, 1:2],
                                               scale=sgtc[:, cgl:cgl + 1]
                                               ).then_inc(s_ev, 1)
                for cc in range(64):
                    eng.wait_ge(s_mm, MM5(hf) + 144 + cc + 1)
                    zt = Z[slz[cc]] if slz[cc] < 64 else C_xt[16]
                    eng.activation(zt, bkf(cc % 4), AF.Identity, bias=cst[:, 1:2],
                                   scale=1.0).then_inc(s_ev, 1)
                for tt in range(4):
                    eng.wait_ge(s_ev2, EV25(hf) + 64 + tt + 1)
                    eng.activation(C_ao[tt], bkf(tt * 2, 2), AF.Identity,
                                   bias=cst[:, 1:2],
                                   scale=1.0).then_inc(s_ev, 1)
                for tt in range(4):
                    eng.wait_ge(s_st, ST5(hf) + 4 + tt + 1)
                    amark(eng.activation(ln_mv[:, 1:2], ln_mv[:, 1:2], AF.Ln,
                                         bias=cst[:, 0:1], scale=1.0))
                    await_()
                    eng.activation(ln_r[:], ln_mv[:, 1:2], AF.Exp, bias=cst[:, 1:2],
                                   scale=-0.5).then_inc(s_lnr, 1)

        @block.vector
        def _(eng):
            vmark, vmarkpub, vwait = chain(eng, s_chv)
            eng.wait_ge(s_gen, 1)
            vmark(eng.tensor_copy(identb[:], identf[:]))
            vwait()
            vmark(eng.tensor_tensor(hwork[:], iota_i[:], iota_j[:], ALU.bitwise_and))
            for sh in (4, 2, 1):
                vwait()
                vmark(eng.tensor_scalar(iota_j[:], hwork[:], sh, None,
                                        ALU.logical_shift_right))
                vwait()
                vmark(eng.tensor_tensor(hwork[:], hwork[:], iota_j[:], ALU.bitwise_xor))
            vwait()
            vmark(eng.tensor_scalar(hwork[:], hwork[:], 1, None, ALU.bitwise_and))
            vwait()
            vmark(eng.tensor_copy(hworkf[:], hwork[:]))
            vwait()
            vmark(eng.tensor_scalar(h128[:], hworkf[:], -2.0, 1.0, ALU.mult, ALU.add))

            for i, (nm, rows) in enumerate(VECS):
                eng.wait_ge(s_vecp, i + 1)
                srcp = psf[0:128, 6 * 512:6 * 512 + rows]
                if nm == "bq":
                    inst = eng.tensor_scalar(bqc[:], srcp, RSQ_DH, None, ALU.mult)
                elif nm == "bk":
                    inst = eng.tensor_copy(bkc[:], srcp)
                elif nm == "bv":
                    inst = eng.tensor_copy(hworkf[:, 0:16], srcp)
                elif nm == "bo":
                    inst = eng.tensor_copy(boc[:], srcp)
                elif nm == "su_ug":
                    inst = eng.tensor_scalar(su1c[:], srcp,
                                             float(1.0 / (64.0 * np.sqrt(2048.0))),
                                             None, ALU.mult)
                elif nm == "wsc_dp":
                    inst = eng.tensor_scalar(sdpc[:], srcp, 64.0, None, ALU.mult)
                elif nm == "su_dp":
                    inst = eng.tensor_copy(sdp_t[:], srcp)
                elif nm == "wsc_ug":
                    inst = eng.tensor_copy(wsc_t[:], srcp)
                inst.then_inc(s_vecd, 1)
            eng.wait_ge(s_vecd, NV)
            vmark(eng.tensor_tensor(supc[:], wsc_t[:, 0:64], sdp_t[:], ALU.mult))
            vwait()
            vmark(eng.tensor_scalar(supc[:], supc[:], float(1.0 / np.sqrt(8192.0)),
                                    None, ALU.mult))
            vwait()
            eng.tensor_scalar(sgtc[:], wsc_t[:, 64:128], 64.0, None,
                              ALU.mult).then_inc(s_vecd, 1)

            if STAGE >= 2:
                eng.wait_ge(s_vecp, NV + 1)
                eng.tensor_copy(bvbc[:], psf[:, 0:2048]).then_inc(s_vecd, 1)
            for i in range(NT_ALL):
                ds_xh.wait(eng, i)
                for sl in range(4):
                    vmark(eng.bn_stats(bnst[:, sl * 6:(sl + 1) * 6],
                                       E_xh[i % 2][:, sl * 512:(sl + 1) * 512]))
                vwait()
                eng.bn_aggr(ln_mv[:], bnst[:]).then_inc(s_st, 1)
                eng.wait_ge(s_lnr, i + 1)
                vmark(eng.tensor_scalar(ln_mr[:], ln_mv[:, 0:1], ln_r[:], -1.0,
                                        ALU.mult, ALU.mult))
                if i >= 2:
                    ds_nxw.wait(eng, i - 2)
                vwait()
                vmarkpub(eng.tensor_scalar(E_nx[i % 2], E_xh[i % 2], ln_r[:], ln_mr[:],
                                           ALU.mult, ALU.add).then_inc(s_ln, 1),
                         s_ln, i + 1)
                if STAGE == 1 and i >= NTH:
                    if i > NTH:
                        ds_dbo.wait(eng, i - NTH - 1)
                    vwait()
                    vmarkpub(eng.tensor_copy(E_dbg[:], E_nx[i % 2]).then_inc(s_dbg, 1),
                             s_dbg, i - NTH + 1)

            if STAGE < 2:
                return


            for blk in range(NBK):
                for dout in range(16):
                    g = blk * 16 + dout
                    eng.wait_ge(s_ev, g + 1)
                    if g >= 2:
                        eng.wait_ge(s_mm, g)
                    vwait()
                    vmarkpub(eng.tensor_tensor(E_ksq[g % 2], C_ko[dout], C_ko[dout],
                                               ALU.mult).then_inc(s_sq, 1), s_sq, g + 1)
                eng.wait_ge(s_pe2, blk + 1)
                vwait()
                vmark(eng.reduce_max(cwork[0:1, 0:1], psf[0:1, 4 * 512:5 * 512], axis=AX.X))
                vwait()
                if blk == 0:
                    vmarkpub(eng.tensor_copy(kmaxsq[0:1, 0:1],
                                             cwork[0:1, 0:1]).then_inc(s_kst, 1),
                             s_kst, blk + 1)
                else:
                    vmarkpub(eng.tensor_tensor(kmaxsq[0:1, 0:1], kmaxsq[0:1, 0:1],
                                               cwork[0:1, 0:1], ALU.max).then_inc(s_kst, 1),
                             s_kst, blk + 1)
                vwait()

            for blk in range(NBK):
                for ts in range(4):
                    for ns in range(4):
                        g = blk * 16 + ts * 4 + ns
                        eng.wait_ge(s_mm, MM_K + g + 1)
                        if blk >= 1:
                            ds_vv.wait(eng, (blk - 1) * 4 + ts)
                        eng.tensor_tensor(C_vo[ts][:, ns * 512:(ns + 1) * 512],
                                          bkf(g % 4), bvbc[:, ns * 512:(ns + 1) * 512],
                                          ALU.add).then_inc(s_ev2, 1)

            for h in range(16):
                eng.wait_ge(s_ev, EV_K + h + 1)
                if h >= 1:
                    eng.wait_ge(s_pe2, 8 + h)
                eng.tensor_tensor(E_qsq, D_qt[h], D_qt[h], ALU.mult).then_inc(s_sq, 1)
                eng.wait_ge(s_pe2, 8 + h + 1)
                vwait()
                vmark(eng.reduce_max(cwork[0:1, 0:1], psf[0:1, 4 * 512:6 * 512], axis=AX.X))
                vwait()
                vmarkpub(eng.tensor_copy(qmaxsq[0:1, h:h + 1],
                                         cwork[0:1, 0:1]).then_inc(s_kst, 1),
                         s_kst, 8 + h + 1)
                vwait()

            eng.wait_ge(s_flag, 4)
            ds_ksf.wait(eng, 0)
            vmark(eng.tensor_tensor(kst_sb[0:1, 0:1], kst_sb[0:1, 0:1],
                                    kmaxsq[0:1, 0:1], ALU.max))
            vwait()
            eng.tensor_scalar(cwork[:], qmaxsq[:], kst_sb[0:1, 0:1], None,
                              ALU.mult).then_inc(s_ds, 1)
            eng.wait_ge(s_lnr, NT_ALL + 1)
            vmark(eng.tensor_tensor(cpair[:], cwork[0:1, 0:16:2], cwork[0:1, 1:16:2],
                                    ALU.max))
            vwait()
            eng.tensor_scalar(cpair[:], cpair[:], -1.0, None,
                              ALU.mult).then_inc(s_dve2, 1)
            eng.wait_ge(s_pe2, PE2_P2)
            eng.tensor_copy(negCb[:], psf[:, 6 * 512:6 * 512 + 8]).then_inc(s_dve2, 1)

            if STAGE >= 3:
                for p in range(8):
                    for C in range(64):
                        if ACC_ENG[C] == "p":
                            continue
                        eng.wait_ge(s_exp, S_EXP(p, C))
                        vwait()
                        if C == 0:
                            vmarkpub(eng.tensor_copy(E_acc[:],
                                                     E_pp[0][:]
                                                     ).then_inc(s_acc, 1),
                                     s_acc, S_ACCD(p, C))
                        else:
                            vmarkpub(eng.tensor_tensor(E_acc[:], E_acc[:],
                                                       E_pp[C % 4][:],
                                                       ALU.add).then_inc(s_acc, 1),
                                     s_acc, S_ACCD(p, C))
                    eng.wait_ge(s_acc2, 21 * (p + 1))
                    vwait()
                    vmarkpub(eng.tensor_tensor(E_acc[:], E_acc[:], acc_b[:],
                                               ALU.add).then_inc(s_acc, 1),
                             s_acc, 44 * p + 44)
                    eng.wait_ge(s_pe2, PE2_DB(p))
                    vwait()
                    vmark(eng.reciprocal(E_dinv[:, 0:1024], bkf(0, 2)))
                    vwait()
                    vmark(eng.reciprocal(E_dinv[:, 1024:2048], bkf(2, 2)))
                    vwait()
                    vmark(eng.tensor_tensor(C_ao[2 * p], bkf(4, 2),
                                            E_dinv[:, 0:1024], ALU.mult))
                    vwait()
                    vmarkpub(eng.tensor_tensor(C_ao[2 * p + 1], bkf(6, 2),
                                               E_dinv[:, 1024:2048],
                                               ALU.mult).then_inc(s_dve2, 1),
                             s_dve2, S_DVE2_SW(p))

            if STAGE == 3:
                for k in range(16):
                    if k > 0:
                        ds_dbo.wait(eng, k - 1)
                    vwait()
                    vmarkpub(eng.tensor_copy(B_res[0][:, 0:1024],
                                             C_ao[k]).then_inc(s_dbg, 1), s_dbg, k + 1)
                return

            if STAGE == 2:
                for k in range(16):
                    if k > 0:
                        ds_dbo.wait(eng, k - 1)
                    vwait()
                    vmarkpub(eng.tensor_copy(E_dbg[:, 0:1024],
                                             D_qt[k]).then_inc(s_dbg, 1), s_dbg, k + 1)
                ds_dbo.wait(eng, 15)
                ds_dbi.wait(eng, 0)
                vwait()
                vmarkpub(eng.tensor_copy(E_dbg[:], E_pp[0]).then_inc(s_dbg, 1),
                         s_dbg, 17)
                ds_dbo.wait(eng, 16)
                ds_dbi.wait(eng, 1)
                vwait()
                vmarkpub(eng.tensor_copy(E_dbg[:], E_pp[1]).then_inc(s_dbg, 1),
                         s_dbg, 18)
                ds_dbo.wait(eng, 17)
                vwait()
                vmark(eng.memset(E_dbg[:], 0.0))
                vwait()
                vmark(eng.tensor_copy(E_dbg[:, 0:8], negCb[:]))
                vwait()
                vmark(eng.tensor_copy(E_dbg[0:1, 8:24], kmaxsq[:]))
                vwait()
                vmark(eng.tensor_copy(E_dbg[0:1, 24:40], qmaxsq[:]))
                vwait()
                eng.tensor_copy(E_dbg[0:1, 40:56], kst_sb[:]).then_inc(s_dbg, 1)
                return

            if STAGE < 4:
                return

            for i in range(8):
                eng.wait_ge(s_pe2, PE2_P3 + i + 1)
                src = bkf((i % 2) * 4, 4)
                for sl in range(4):
                    vmark(eng.bn_stats(bnst[:, sl * 6:(sl + 1) * 6],
                                       src[:, sl * 512:(sl + 1) * 512]))
                vwait()
                eng.bn_aggr(ln_mv[:], bnst[:]).then_inc(s_st, 1)
                eng.wait_ge(s_lnr, LNR_P4 + i + 1)
                vmark(eng.tensor_scalar(ln_mr[:], ln_mv[:, 0:1], ln_r[:], -1.0,
                                        ALU.mult, ALU.mult))
                ds_x2.wait(eng, i)
                if i >= 2:
                    ds_nxw.wait(eng, NT_ALL + i - 2)
                vwait()
                vmarkpub(eng.tensor_scalar(D_x2[i % 2], src, ln_r[:], ln_mr[:],
                                           ALU.mult, ALU.add).then_inc(s_dve2, 1),
                         s_dve2, 10 + i + 1)
                vwait()
                vmarkpub(eng.tensor_tensor(D_x2[i % 2], D_x2[i % 2], D_xo[i % 2],
                                           ALU.add).then_inc(s_ln, 1),
                         s_ln, LN_P4 + i + 1)

            if STAGE < 5:
                return

            for hf in range(2):
                ops16, sl_map = butterfly_plan(16)
                ops64, slz = butterfly_plan(64)

                def zph(sl5):
                    return Z[sl5] if sl5 < 64 else C_xt[16]

                for tt in range(4):
                    ds_x2.wait(eng, 8 + hf * 4 + tt)
                    for sl in range(4):
                        vmark(eng.bn_stats(bnst[:, sl * 6:(sl + 1) * 6],
                                           B_res[tt][:, sl * 512:(sl + 1) * 512]))
                    vwait()
                    eng.bn_aggr(ln_mv[:], bnst[:]).then_inc(s_st, 1)
                    eng.wait_ge(s_lnr, LNR5(hf) + tt + 1)
                    vmark(eng.tensor_scalar(ln_mr[:], ln_mv[:, 0:1], ln_r[:], -1.0,
                                            ALU.mult, ALU.mult))
                    if tt >= 1:
                        eng.wait_ge(s_pe2, PE25(hf) + tt)
                    elif hf == 1:
                        eng.wait_ge(s_pe2, PE25(0) + 4)
                    vwait()
                    vmarkpub(eng.tensor_scalar(C_n3f[:], B_res[tt], ln_r[:],
                                               ln_mr[:], ALU.mult,
                                               ALU.add).then_inc(s_ds, 1),
                             s_ds, DS5(hf) + tt + 1)
                eng.wait_ge(s_ev, EV5(hf) + 4)
                for j, (ba, bb, sp) in enumerate(ops16):
                    vwait()
                    vmark(eng.tensor_tensor(C_xt[sp], C_xt[ba], C_xt[bb], ALU.add))
                    vwait()
                    inst = eng.tensor_tensor(C_xt[bb], C_xt[ba], C_xt[bb],
                                             ALU.subtract)
                    if j == len(ops16) - 1:
                        vmarkpub(inst.then_inc(s_dve2, 1), s_dve2, DVE2_H16(hf))
                    else:
                        vmark(inst)
                for g5 in range(64):
                    eng.wait_ge(s_ev, EV5(hf) + 84 + g5 + 1)
                    vwait()
                    vmarkpub(eng.tensor_tensor(Z[g5], Z[g5], C_sg[g5 % 2],
                                               ALU.mult).then_inc(s_ev2, 1),
                             s_ev2, EV25(hf) + g5 + 1)
                for j, (ba, bb, sp) in enumerate(ops64):
                    vwait()
                    vmark(eng.tensor_tensor(zph(sp), zph(ba), zph(bb), ALU.add))
                    vwait()
                    inst = eng.tensor_tensor(zph(bb), zph(ba), zph(bb), ALU.subtract)
                    if j == len(ops64) - 1:
                        vmarkpub(inst.then_inc(s_dve2, 1), s_dve2, DVE2_H64(hf))
                    else:
                        vmark(inst)
                eng.wait_ge(s_ev, EV5(0))
                for dh2 in range(2):
                    eng.wait_ge(s_mm, MM5(hf) + 208 + dh2 * 64 + 64)
                    for tt in range(4):
                        vwait()
                        vmark(eng.tensor_tensor(bkf(tt * 2, 2), bkf(tt * 2, 2),
                                                sdpc_bb[:, dh2 * 1024:(dh2 + 1) * 1024],
                                                ALU.mult))
                        vwait()
                        vmark(eng.bn_stats(bnst5[:, (tt * 4 + dh2 * 2) * 6:
                                                 (tt * 4 + dh2 * 2 + 1) * 6],
                                           bkf(tt * 2)))
                        vwait()
                        inst = eng.bn_stats(bnst5[:, (tt * 4 + dh2 * 2 + 1) * 6:
                                                  (tt * 4 + dh2 * 2 + 2) * 6],
                                            bkf(tt * 2 + 1))
                        if dh2 == 0:
                            vmarkpub(inst.then_inc(s_ev2, 1), s_ev2,
                                     EV25(hf) + 64 + tt + 1)
                            continue
                        vmark(inst)
                        vwait()
                        eng.bn_aggr(ln_mv[:], bnst5[:, tt * 24:(tt + 1) * 24]
                                    ).then_inc(s_st, 1)
                        eng.wait_ge(s_lnr, LNR5(hf) + 4 + tt + 1)
                        vmark(eng.tensor_scalar(ln_mr[:], ln_mv[:, 0:1], ln_r[:],
                                                -1.0, ALU.mult, ALU.mult))
                        if hf == 1 or tt >= 2:
                            ds_out.wait(eng, hf * 4 + tt - 2)
                        vwait()
                        vmark(eng.tensor_scalar(P5_out[tt % 2][:, 0:1024], C_ao[tt],
                                                ln_r[:], ln_mr[:], ALU.mult, ALU.add))
                        vwait()
                        vmarkpub(eng.tensor_scalar(P5_out[tt % 2][:, 1024:2048],
                                                   bkf(tt * 2, 2), ln_r[:], ln_mr[:],
                                                   ALU.mult,
                                                   ALU.add).then_inc(s_dve2, 1),
                                 s_dve2, DVE2_L4(hf, tt))
                        vwait()
                        vmarkpub(eng.tensor_tensor(P5_out[tt % 2], P5_out[tt % 2],
                                                   B_res[tt], ALU.add
                                                   ).then_inc(s_ln, 1),
                                 s_ln, LN5(hf) + tt + 1)

    return nc


_CACHE = {}


def kernel(L0_attn_X, L0_attn_Wq, L0_attn_bq, L0_attn_Wk, L0_attn_bk, L0_attn_Wv,
           L0_attn_bv, L0_attn_Wo, L0_attn_bo, L0_ffn_SU_ug, L0_ffn_SU_dp,
           L0_ffn_Wscale_ug, L0_ffn_Wscale_dp, L0_ffn_W_up, L0_ffn_W_gate,
           L0_ffn_W_down):
    X = np.ascontiguousarray(np.asarray(L0_attn_X).reshape(S, D))
    if "nc" not in _CACHE:
        _CACHE["nc"] = build_program()
    nc = _CACHE["nc"]
    nonce_val = int(np.random.SeedSequence().entropy % (2**31 - 2)) + 1
    common = {
        "wq": np.ascontiguousarray(L0_attn_Wq), "wk": np.ascontiguousarray(L0_attn_Wk),
        "wv": np.ascontiguousarray(L0_attn_Wv), "wo": np.ascontiguousarray(L0_attn_Wo),
        "wup": np.ascontiguousarray(L0_ffn_W_up),
        "wgate": np.ascontiguousarray(L0_ffn_W_gate),
        "wdown": np.ascontiguousarray(L0_ffn_W_down),
        "bq": np.ascontiguousarray(np.asarray(L0_attn_bq).reshape(1, D)),
        "bk": np.ascontiguousarray(np.asarray(L0_attn_bk).reshape(1, D)),
        "bv": np.ascontiguousarray(np.asarray(L0_attn_bv).reshape(1, D)),
        "bo": np.ascontiguousarray(np.asarray(L0_attn_bo).reshape(1, D)),
        "su_ug": np.ascontiguousarray(np.asarray(L0_ffn_SU_ug).reshape(1, D)),
        "su_dp": np.ascontiguousarray(np.asarray(L0_ffn_SU_dp).reshape(1, INTER)),
        "wsc_ug": np.ascontiguousarray(np.asarray(L0_ffn_Wscale_ug).reshape(1, 2 * INTER)),
        "wsc_dp": np.ascontiguousarray(np.asarray(L0_ffn_Wscale_dp).reshape(1, D)),
        "nonce": np.full((1, 16), nonce_val, np.int32),
    }
    in_maps = []
    for c in range(NC_):
        side = c % 2
        m = dict(common)
        m["x_own"] = np.ascontiguousarray(X[c * TOK:(c + 1) * TOK, :])
        m["x_half"] = np.ascontiguousarray(X[side * HALF:(side + 1) * HALF, :])
        in_maps.append(m)
    res = run_bass_kernel_spmd(nc, in_maps, core_ids=list(range(NC_)))
    outp = np.concatenate([res.results[c]["out"] for c in range(NC_)], axis=0)
    return outp.reshape(1, S, D)
